# version 03
# baseline (speedup 1.0000x reference)
import numpy as np
from contextlib import ExitStack
import concourse.bass as bass
import concourse.mybir as mybir
from concourse.bass_utils import run_bass_kernel_spmd

F32 = mybir.dt.float32
BF16 = mybir.dt.bfloat16
AF = mybir.ActivationFunctionType
ALU = mybir.AluOpType
AXX = mybir.AxisListType.X

ENGS = ("sync", "tensor", "vector", "scalar", "gpsimd")
G8 = [[0, 1, 2, 3, 4, 5, 6, 7]]
TOK = 2048
TOKA = 4096
D = 2048
KC = 16
FF = 5632
FFC = 44
EPS_LN = 1e-5
EPS_Q = 1e-8
ARENA_KB = 205


class Sem:
    __slots__ = ("h", "v", "name")

    def __init__(self, h, name):
        self.h = h
        self.v = 0
        self.name = name


class B:

    def __init__(self, nc, es):
        self.nc = nc
        self.es = es
        self.q = {e: [] for e in ENGS}
        self.nbar = 0
        self.barsem = None
        self.arena_b = es.enter_context(
            nc.sbuf_tensor("arena", [128, ARENA_KB * 512], BF16))
        self.arena_f = self.arena_b.bitcast(F32)
        self.aoff = 0
        self.amax = ARENA_KB * 1024
        self.pb = [es.enter_context(nc.psum_tensor(f"pb{i}", [128, 512], F32))
                   for i in range(8)]
        self.pbb = [p.bitcast(BF16) for p in self.pb]

    def sem(self, name):
        return Sem(self.es.enter_context(self.nc.semaphore(name)), name)

    def mark(self):
        return self.aoff

    def reset(self, m):
        self.aoff = m

    def tile(self, shape, dt):
        p, f = shape
        esz = 2 if dt == BF16 else 4
        off = (self.aoff + 63) & ~63
        self.aoff = off + f * esz
        assert self.aoff <= self.amax, f"arena overflow {self.aoff}"
        base = self.arena_b if dt == BF16 else self.arena_f
        return base[0:p, off // esz:off // esz + f]

    def sb(self, name, shape, dt):
        return self.es.enter_context(self.nc.sbuf_tensor(name, shape, dt))

    def wait(self, eng, sem, val):
        if val <= 0:
            return
        assert val <= sem.v, f"wait {sem.name}>={val} but planned {sem.v} (hang)"
        self.q[eng].append(lambda e, s=sem, v=val: e.wait_ge(s.h, v))

    def dma(self, eng, out, in_, sem):
        sem.v += 16
        self.q[eng].append(
            lambda e, o=out, i=in_, s=sem:
            e.dma_start(out=o, in_=i).then_inc(s.h, 16))
        return sem.v

    def op(self, eng, fn, sem=None):
        if sem is None:
            self.q[eng].append(lambda e, f=fn: f(e))
            self._lastfn = getattr(self, "_lastfn", {})
            self._lastfn[eng] = fn
            return None
        sem.v += 1
        self.q[eng].append(lambda e, f=fn, s=sem: f(e).then_inc(s.h, 1))
        return sem.v

    def inc_last(self, eng, sem):
        fn = self._lastfn.get(eng)
        assert fn is not None, "inc_last without preceding op"
        self.q[eng].pop()
        sem.v += 1
        self.q[eng].append(lambda e, f=fn, s=sem: f(e).then_inc(s.h, 1))
        self._lastfn[eng] = None
        return sem.v

    def coll(self, kind, op, groups, ins, outs, sem):
        sem.v += 1
        self.q["gpsimd"].append(
            lambda e, s=sem: e.collective_compute(
                kind, op, replica_groups=groups, ins=ins,
                outs=outs).then_inc(s.h))
        return sem.v

    def barrier(self):
        if self.barsem is None:
            self.barsem = self.sem("barsem")
        self.nbar += 1
        tgt = 5 * self.nbar
        self.barsem.v = tgt
        for e in ENGS:
            self.q[e].append(lambda x, s=self.barsem: x.nop().then_inc(s.h, 1))
        for e in ENGS:
            self.q[e].append(lambda x, s=self.barsem, v=tgt: x.wait_ge(s.h, v))

    def finalize(self):
        blk = self.es.enter_context(self.nc.Block())
        for name in ENGS:
            qq = self.q[name]

            def make(qq):
                def _(eng):
                    for f in qq:
                        f(eng)
                return _
            getattr(blk, name)(make(qq))


class MK:
    def __init__(self, nc, b, io):
        self.nc = nc
        self.b = b
        self.io = io
        self.c = {}
        self.sm = {n: b.sem(n) for n in
                   ("ld", "ldg", "st", "stg", "pe", "dve", "act", "cc")}

    def s(self, name):
        if name not in self.sm:
            self.sm[name] = self.b.sem(name)
        return self.sm[name]

    def wload(self, dst, c0, c1, dest, stream):
        gf, off, srow, cols = self.wmeta[dst]
        q = srow // 128
        w = c1 - c0
        v = gf[:, off:off + srow * cols].rearrange(
            "r (q p f) -> p q r f", q=q, p=128, f=cols)
        dk = dest.rearrange("p (r q f) -> p q r f", r=8, q=q, f=w)
        lv = None
        for qq in range(q):
            lv = self.b.dma("sync", dk[:, qq, :, :],
                            v[:, qq, :, c0:c1], stream)
        return lv

    def wview(self, dst, c0, c1):
        gf, off, srow, cols = self.wmeta[dst]
        q = srow // 128
        v = gf[:, off:off + srow * cols]
        return v.rearrange("r (q p f) -> p r q f", q=q, p=128,
                           f=cols)[:, :, :, c0:c1]

    def wview_rows(self, dst, rb, c0, c1):
        gf, off, srow, cols = self.wmeta[dst]
        q = srow // 128
        v = gf[rb:rb + 1, off:off + srow * cols]
        return v.rearrange("one (j p f) -> p (one j) f", j=q, p=128,
                           f=cols)[:, :, c0:c1]

    def wait_dma_all(self, eng):
        for name, sem in list(self.sm.items()):
            if name in ("pe", "dve", "act", "cc"):
                continue
            self.b.wait(eng, sem, sem.v)

    def emit_startup(self):
        b = self.b
        ld, dve = self.sm["ld"], self.sm["dve"]
        for nm in ("ident_f", "ident_b", "ones_f", "ones_b"):
            dt = F32 if nm.endswith("_f") else BF16
            shape = [128, 128] if nm.startswith("ident") else [1, 128]
            t = b.sb("sb_" + nm, shape, dt)
            b.dma("sync", t[:, :], self.io[nm][:, :], ld)
            self.c[nm] = t
        eps = b.sb("epst", [128, 1], F32)
        onescol = b.sb("onescol", [128, 1], BF16)
        onescolf = b.sb("onescolf", [128, 1], F32)
        b.op("vector", lambda v: v.memset(eps[:, :], EPS_LN))
        b.op("vector", lambda v: v.memset(onescol[:, :], 1.0))
        b.op("vector", lambda v: v.memset(onescolf[:, :], 1.0), sem=dve)
        self.c["eps"] = eps
        self.c["ones_col_b"] = onescol
        self.c["ones_col_f"] = onescolf
        b.wait("sync", ld, ld.v)
        b.barrier()

    def prep_params(self, chunks=(), bf16rows=(), bcasts=()):
        b = self.b
        ld, dve = self.sm["ld"], self.sm["dve"]
        act, pe = self.sm["act"], self.sm["pe"]
        pv = b.pb[0]
        outs = {}
        for name, n in chunks:
            outs[name] = b.tile([128, max(n // 128, 1)], F32)
        for name, n in bf16rows:
            outs[name] = b.tile([1, n], BF16)
        for name, n in bcasts:
            outs[name] = b.tile([128, n], F32)
        mstage = b.mark()
        stage = b.tile([128, 8192], F32)
        for name, n in chunks:
            nch = max(n // 128, 1)
            out = outs[name]
            if n >= 128:
                src = self.io[name].rearrange("one (c p) -> (one c) p", p=128)
            else:
                src = self.io[name]
            b.wait("vector", pe, pe.v)
            zv = b.op("vector", lambda v: v.memset(
                stage[0:128, 0:128], 0.0), sem=dve)
            b.wait("sync", dve, zv)
            lv = b.dma("sync", stage[0:nch, 0:128], src, ld)
            b.wait("tensor", ld, lv)
            b.wait("tensor", dve, dve.v)
            b.op("tensor", lambda t: t.transpose(
                out=pv[:, 0:128], in_=stage[0:128, 0:128],
                identity=self.c["ident_f"][:, :]), sem=pe)
            b.wait("vector", pe, pe.v)
            b.op("vector", lambda v, o=out, nch=nch: v.tensor_copy(
                out=o[:, 0:nch], in_=pv[:, 0:nch]), sem=dve)
            b.wait("sync", dve, dve.v)
            self.c[name] = out
        for name, n in bf16rows:
            out = outs[name]
            lv = b.dma("sync", stage[0:1, 0:n], self.io[name][:, :], ld)
            b.wait("scalar", ld, lv)
            av = b.op("scalar", lambda s, o=out, n=n: s.activation(
                out=o[0:1, 0:n], in_=stage[0:1, 0:n], func=AF.Copy, bias=0.0),
                sem=act)
            b.wait("sync", act, av)
            self.c[name] = out
        for name, n in bcasts:
            out = outs[name]
            self._prep_ctr = getattr(self, "_prep_ctr", 0) + 1
            scr = self.nc.dram_tensor(f"prep_bc_{self._prep_ctr}", [1, n],
                                      F32, kind="Internal").ap()
            lv = b.dma("sync", stage[0:1, 0:n], self.io[name][:, :], ld)
            b.wait("sync", ld, lv)
            sv = b.dma("sync", scr[:, :], stage[0:1, 0:n], self.sm["st"])
            b.wait("sync", self.sm["st"], sv)
            b.dma("sync", out[:, 0:n], scr.broadcast_to([128, n]), ld)
            b.wait("sync", ld, ld.v)
            self.c[name] = out
        for eng in ("sync", "gpsimd", "tensor", "vector", "scalar"):
            b.wait(eng, dve, dve.v)
            b.wait(eng, act, act.v)
            b.wait(eng, pe, pe.v)
        b.reset(mstage)

    def emit_W(self, weights, quant):
        b, nc = self.b, self.nc
        ld, st, cc = self.sm["ld"], self.sm["st"], self.sm["cc"]
        dve, act, pe = self.sm["dve"], self.sm["act"], self.sm["pe"]
        pv = b.pb[0]
        m0 = b.mark()
        CW = 8192
        stage = [b.tile([128, CW], F32) for _ in range(2)]
        outb = [b.tile([128, CW], BF16) for _ in range(2)]
        sga = b.tile([128, CW], BF16)
        red = b.tile([128, 1], F32)
        redtmp = b.tile([128, 1], F32)
        sums = b.tile([1, 8], F32)
        thr = b.tile([1, 8], F32)
        thr_bc = b.tile([128, 8], F32)
        thr_nbc = b.tile([128, 8], F32)
        dbg = b.tile([128, 4], F32)
        self.dbg_tiles = dbg
        self._w_ctr = getattr(self, "_w_ctr", 0) + 1
        wc = self._w_ctr
        if quant:
            ar_in = nc.dram_tensor(f"w_ar_in{wc}", [1, 8], F32,
                                   kind="Internal").ap()
            ar_out = nc.dram_tensor(f"w_ar_out{wc}", [1, 8], F32,
                                    kind="Internal").ap()
            red_dram = nc.dram_tensor(f"w_red_dram{wc}", [768], F32,
                                      kind="Internal").ap()
            redrow = b.tile([1, 768], F32)
            red2 = b.tile([128, 6], F32)
            stage_cons = [0, 0]
            b.op("vector", lambda v: v.memset(sums[0:1, 0:8], 0.0), sem=dve)
            for qi, (src, dst, rows, cols) in enumerate(quant):
                srow = rows // 8
                ntile = (srow + 127) // 128
                b.op("vector", lambda v, n=ntile: v.memset(
                    red2[:, 0:n], 0.0), sem=dve)
                for ti in range(ntile):
                    r0, r1 = ti * 128, min((ti + 1) * 128, srow)
                    bu = stage[ti % 2]
                    wls = self.s(f"wld{ti % 2}")
                    b.wait("sync", dve, stage_cons[ti % 2])
                    lv = b.dma("sync", bu[0:r1 - r0, 0:cols],
                               self.io[src][r0:r1, :], wls)
                    b.wait("vector", wls, lv)
                    b.op("vector", lambda v, bu=bu, r=r1 - r0, ti=ti,
                         cols=cols: v.reduce_sum(
                             out=red2[0:r, ti:ti + 1], in_=bu[0:r, 0:cols],
                             axis=AXX, apply_absolute_value=True), sem=dve)
                    stage_cons[ti % 2] = dve.v
                b.wait("gpsimd", dve, dve.v)
                b.dma("gpsimd", red_dram[0:128 * ntile],
                      red2[:, 0:ntile], st)
                b.wait("gpsimd", st, st.v)
                rl = b.dma("gpsimd", redrow[0:1, 0:128 * ntile],
                           red_dram[0:128 * ntile], st)
                b.wait("vector", st, rl)
                b.op("vector", lambda v, qi=qi, n=ntile: v.reduce_sum(
                    out=sums[0:1, qi:qi + 1], in_=redrow[0:1, 0:128 * n],
                    axis=AXX), sem=dve)
            b.op("vector", lambda v: v.tensor_copy(
                out=dbg[32:33, 0:3], in_=sums[0:1, 0:3]), sem=dve)
            b.wait("gpsimd", dve, dve.v)
            b.dma("gpsimd", ar_in[:, :], sums[0:1, 0:8], st)
            b.wait("gpsimd", st, st.v)
            b.coll("AllReduce", ALU.add, G8, [ar_in.opt()], [ar_out.opt()], cc)
            b.wait("gpsimd", cc, cc.v)
            b.dma("gpsimd", sums[0:1, 0:8], ar_out[:, :], st)
            b.wait("vector", st, st.v)
            b.op("vector", lambda v: v.tensor_copy(
                out=dbg[64:65, 0:3], in_=sums[0:1, 0:3]), sem=dve)
            for qi, (src, dst, rows, cols) in enumerate(quant):
                b.op("vector",
                     lambda v, qi=qi, n=QNUMEL[quant[qi][1]]: v.tensor_scalar(
                         out=thr[0:1, qi:qi + 1], in0=sums[0:1, qi:qi + 1],
                         scalar1=0.5 / n, scalar2=0.5 * EPS_Q,
                         op0=ALU.mult, op1=ALU.add), sem=dve)
            b.wait("gpsimd", dve, dve.v)
            thr_dram = nc.dram_tensor(f"w_thr_dram{wc}", [1, 8], F32,
                                      kind="Internal").ap()
            b.dma("gpsimd", thr_dram[:, :], thr[0:1, 0:8], st)
            b.wait("gpsimd", st, st.v)
            b.dma("gpsimd", thr_bc[:, 0:8],
                  thr_dram.broadcast_to([128, 8]), st)
            b.wait("vector", st, st.v)
            b.op("vector", lambda v: v.tensor_scalar_mul(
                out=thr_nbc[:, 0:8], in0=thr_bc[:, 0:8], scalar1=-1.0),
                sem=dve)
            b.wait("scalar", dve, dve.v)
            b.wait("sync", dve, dve.v)
            b.wait("gpsimd", dve, dve.v)
        if not hasattr(self, "gins"):
            self.gins = {}
        allw = [(s, d, r, c, False) for (s, d, r, c) in weights] + \
               [(s, d, r, c, True) for (s, d, r, c) in quant]
        rbsize = sum(r // 8 * c for (_, _, r, c, _) in allw)
        gf_in = nc.dram_tensor(f"gfin{wc}", [1, rbsize], BF16,
                               kind="Internal").ap()
        gf_out = nc.dram_tensor(f"gfout{wc}", [8, rbsize], BF16,
                                kind="Internal").ap()
        goff = 0
        if not hasattr(self, "wmeta"):
            self.wmeta = {}
        stv = [0, 0]
        acons = [0, 0]
        sga_cons = 0
        tig = 0
        for src, dst, rows, cols, isq in allw:
            srow = rows // 8
            gin = gf_in[0:1, goff:goff + srow * cols].rearrange(
                "one (s f) -> (one s) f", f=cols)
            self.wmeta[dst] = (gf_out, goff, srow, cols)
            goff += srow * cols
            self.gins[dst] = gin
            if isq:
                qi = [q[0] for q in quant].index(src)
            ntile = (srow + 127) // 128
            for ti in range(ntile):
                r0, r1 = ti * 128, min((ti + 1) * 128, srow)
                rr = r1 - r0
                par = tig % 2
                tig += 1
                bu, ob = stage[par], outb[par]
                wls = self.s(f"wld{par}")
                wss = self.s(f"wst{par}")
                b.wait("sync", wss, stv[par])
                b.wait("sync", act, acons[par])
                lv = b.dma("sync", bu[0:rr, 0:cols], self.io[src][r0:r1, :],
                           wls)
                b.wait("scalar", wls, lv)
                b.wait("scalar", wss, stv[par])
                if isq:
                    b.wait("scalar", dve, sga_cons)
                    b.op("scalar", lambda s, bu=bu, rr=rr, cols=cols,
                         qi=qi: s.activation(
                             out=sga[0:rr, 0:cols], in_=bu[0:rr, 0:cols],
                             func=AF.Sign,
                             bias=thr_nbc[0:rr, qi:qi + 1]), sem=act)
                    av = b.op("scalar", lambda s, bu=bu, ob=ob, rr=rr,
                              cols=cols, qi=qi: s.activation(
                                  out=ob[0:rr, 0:cols], in_=bu[0:rr, 0:cols],
                                  func=AF.Sign,
                                  bias=thr_bc[0:rr, qi:qi + 1]), sem=act)
                    b.wait("vector", act, av)
                    dv = b.op("vector", lambda v, ob=ob, rr=rr, cols=cols:
                              v.tensor_tensor(out=ob[0:rr, 0:cols],
                                              in0=ob[0:rr, 0:cols],
                                              in1=sga[0:rr, 0:cols],
                                              op=ALU.add), sem=dve)
                    b.wait("gpsimd", dve, dv)
                    acons[par] = av
                    sga_cons = dv
                else:
                    av = b.op("scalar", lambda s, bu=bu, ob=ob, rr=rr,
                              cols=cols: s.activation(
                                  out=ob[0:rr, 0:cols], in_=bu[0:rr, 0:cols],
                                  func=AF.Copy, bias=0.0), sem=act)
                    b.wait("gpsimd", act, av)
                    acons[par] = av
                stv[par] = b.dma("gpsimd", gin[r0:r1, :],
                                 ob[0:rr, 0:cols], wss)
        self.wait_dma_all("gpsimd")
        if allw:
            b.coll("AllGather", ALU.bypass, G8, [gf_in.opt()],
                   [gf_out.opt()], cc)
        b.wait("gpsimd", cc, cc.v)
        b.barrier()
        b.reset(m0)

    def emit_ln(self, xin, lnT, ntile, n_add=0, x_add_out=None,
                keep_f32=None, inner=False, gamma_beta=None):
        b = self.b
        ld, st = self.sm["ldg"], self.sm["stg"]
        dve, act, pe = self.sm["dve"], self.sm["act"], self.sm["pe"]
        ncol = ntile * 128
        light = (n_add == 0 and keep_f32 is None and not inner)
        m0 = b.mark()
        xa = [b.tile([128, D], F32) for _ in range(2)]
        lnf = None if light else [b.tile([128, D], F32) for _ in range(2)]
        lnb = [b.tile([128, D], BF16) for _ in range(2)]
        xm = [b.tile([128, D], F32) for _ in range(2)] if n_add else None
        stats = b.tile([128, 24], F32)
        mv = b.tile([128, 2], F32)
        rstd = b.tile([128, 1], F32)
        pT = [b.pbb[6], b.pbb[7]]
        xa_free = [0, 0]
        lnb_free = [0, 0]
        xm_free = [0, 0]
        lnf_free = [0, 0]
        pT_free = [0, 0]
        act_mv_read = 0

        def stats_rstd(src, p):
            for i in range(3):
                b.op("vector", lambda v, p=p, i=i, s=src: v.bn_stats(
                    out=stats[:, i * 6:(i + 1) * 6],
                    in_=s[p][:, i * 512:(i + 1) * 512]))
            sv = b.op("vector", lambda v, p=p, s=src: v.bn_stats(
                out=stats[:, 18:24], in_=s[p][:, 1536:2048]), sem=dve)
            b.wait("vector", dve, sv)
            b.op("vector", lambda v: v.bn_aggr(out=mv[:, :], in_=stats[:, :]),
                 sem=dve)
            v_ = b.op("vector", lambda v: v.engine_nop(), sem=dve)
            b.wait("scalar", dve, v_)
            b.op("scalar", lambda s: s.activation(
                out=rstd[:, :], in_=mv[:, 1:2], func=AF.Ln,
                bias=self.c["eps"][:, :]))
            return b.op("scalar", lambda s: s.activation(
                out=rstd[:, :], in_=rstd[:, :], func=AF.Exp, scale=-0.5),
                sem=act)

        for t in range(ntile):
            p = t % 2
            lxs = self.s(f"lnx{p}")
            lms = self.s(f"lnm{p}")
            lfs = self.s(f"lnf{p}")
            b.wait("gpsimd", dve, xa_free[p])
            if n_add:
                b.wait("gpsimd", lms, xm_free[p])
            lv = b.dma("gpsimd", xa[p][:, :], xin[t * 128:(t + 1) * 128, :],
                       lxs)
            b.wait("vector", lxs, lv)
            b.wait("vector", act, act_mv_read)
            av = stats_rstd(xa, p)
            act_mv_read = av
            b.wait("vector", act, av)
            if keep_f32 is not None:
                b.wait("vector", lfs, lnf_free[p])
            if light:
                b.wait("vector", pe, lnb_free[p])
                b.op("vector", lambda v, p=p: v.tensor_scalar(
                    out=lnb[p][:, :], in0=xa[p][:, :], scalar1=mv[:, 0:1],
                    scalar2=rstd[:, 0:1], op0=ALU.subtract, op1=ALU.mult),
                    sem=dve)
            else:
                b.op("vector", lambda v, p=p: v.tensor_scalar(
                    out=lnf[p][:, :], in0=xa[p][:, :], scalar1=mv[:, 0:1],
                    scalar2=rstd[:, 0:1], op0=ALU.subtract, op1=ALU.mult),
                    sem=dve)
            if t < n_add:
                dv = b.op("vector", lambda v, p=p: v.tensor_tensor(
                    out=xm[p][:, :], in0=xa[p][:, :], in1=lnf[p][:, :],
                    op=ALU.add), sem=dve)
                b.wait("gpsimd", dve, dv)
                xm_free[p] = b.dma("gpsimd",
                                   x_add_out[t * 128:(t + 1) * 128, :],
                                   xm[p][:, :], lms)
            xa_free[p] = dve.v
            if keep_f32 is not None:
                b.wait("gpsimd", dve, dve.v)
                lnf_free[p] = b.dma("gpsimd",
                                    keep_f32[t * 128:(t + 1) * 128, :],
                                    lnf[p][:, :], lfs)
            if inner:
                b.wait("vector", act, act_mv_read)
                av = stats_rstd(lnf, p)
                act_mv_read = av
                b.wait("vector", act, av)
            if not light:
                b.wait("vector", pe, lnb_free[p])
                if inner:
                    b.op("vector", lambda v, p=p: v.tensor_scalar(
                        out=lnb[p][:, :], in0=lnf[p][:, :],
                        scalar1=mv[:, 0:1], scalar2=rstd[:, 0:1],
                        op0=ALU.subtract, op1=ALU.mult), sem=dve)
                else:
                    b.op("vector", lambda v, p=p: v.tensor_copy(
                        out=lnb[p][:, :], in_=lnf[p][:, :]), sem=dve)
            lnb_ready = dve.v
            for g in range(4):
                gp = g % 2
                b.wait("tensor", dve, max(lnb_ready, pT_free[gp]))
                for j in range(4):
                    k = g * 4 + j
                    b.op("tensor", lambda tt, p=p, gp=gp, j=j, k=k:
                         tt.transpose(
                             out=pT[gp][:, j * 128:(j + 1) * 128],
                             in_=lnb[p][:, k * 128:(k + 1) * 128],
                             identity=self.c["ident_b"][:, :]))
                pvv = b.inc_last("tensor", pe)
                b.wait("vector", pe, pvv)
                if gamma_beta is None:
                    b.op("vector", lambda v, gp=gp, g=g, t=t: v.tensor_copy(
                        out=lnT.rearrange("p (k c) -> p k c", k=KC)
                        [:, 4 * g:4 * g + 4, t * 128:(t + 1) * 128],
                        in_=pT[gp][:, 0:512].rearrange(
                            "p (j c) -> p j c", j=4)), sem=dve)
                else:
                    gch, bch = gamma_beta
                    for j in range(4):
                        k = g * 4 + j
                        b.op("vector", lambda v, gp=gp, j=j, k=k, t=t:
                             v.tensor_scalar(
                                 out=lnT[:, k * ncol + t * 128:
                                         k * ncol + (t + 1) * 128],
                                 in0=pT[gp][:, j * 128:(j + 1) * 128],
                                 scalar1=gch[:, k:k + 1],
                                 scalar2=bch[:, k:k + 1],
                                 op0=ALU.mult, op1=ALU.add), sem=dve)
                pT_free[gp] = dve.v
            lnb_free[p] = pe.v
        self.wait_dma_all("gpsimd")
        b.wait("tensor", dve, dve.v)
        b.reset(m0)

    def emit_A(self, x_all, xout):
        b, nc = self.b, self.nc
        ld = self.sm["ld"]
        ldg, stg = self.sm["ldg"], self.sm["stg"]
        dve, act, pe = self.sm["dve"], self.sm["act"], self.sm["pe"]
        xA = nc.dram_tensor("a_xA", [TOK, D], F32, kind="Internal").ap()
        vhbm = nc.dram_tensor("a_v", [16, 32, 128, 256], BF16,
                              kind="Internal").ap()
        gthbm = nc.dram_tensor("a_gt", [4, 32, 128, 512], BF16,
                               kind="Internal").ap()
        mA = b.mark()
        qT = b.tile([128, TOK], BF16)
        kT = b.tile([128, TOKA], BF16)
        self.prep_params(
            chunks=[("b_qk", 128), ("gq", 128), ("bq", 128), ("gk", 128),
                    ("bk", 128), ("b_hidden", 8192)],
            bf16rows=[("b_hidden_bf", 8192), ("b_out0_bf", 2048)])
        m1 = b.mark()
        for th in range(2):
            toff = th * TOK
            ln0T = b.tile([128, KC * TOK], BF16)
            self.emit_ln(x_all[toff:toff + TOK, :], ln0T, TOK // 128,
                         n_add=(TOK // 128 if th == 0 else 0), x_add_out=xA)
            b.barrier()
            qkT = b.tile([128, TOK], F32)
            wqk = b.tile([128, KC * 128], BF16)
            pq = [b.pb[0], b.pb[1]]
            awq = self.s("awq")
            lw = self.wload("W_qk_bf", 0, 128, wqk, awq)
            b.wait("tensor", awq, lw)
            pq_free = [0, 0]
            for tb in range(4):
                bk = tb % 2
                b.wait("tensor", act, pq_free[bk])
                for k in range(KC):
                    b.op("tensor", lambda t, k=k, tb=tb, bk=bk: t.matmul(
                        out=pq[bk][:, :], lhsT=wqk[:, k * 128:(k + 1) * 128],
                        rhs=ln0T[:, k * TOK + tb * 512:
                                 k * TOK + tb * 512 + 512],
                        start=(k == 0), stop=(k == KC - 1)))
                pvv = b.inc_last("tensor", pe)
                b.wait("scalar", pe, pvv)
                pq_free[bk] = b.op("scalar", lambda s, bk=bk, tb=tb:
                                   s.activation(
                                       out=qkT[:, tb * 512:(tb + 1) * 512],
                                       in_=pq[bk][:, :], func=AF.Silu,
                                       bias=self.c["b_qk"][:, 0:1]), sem=act)
            b.wait("vector", act, act.v)
            if th == 0:
                b.op("vector", lambda v: v.tensor_scalar(
                    out=qT[:, :], in0=qkT[:, :], scalar1=self.c["gq"][:, 0:1],
                    scalar2=self.c["bq"][:, 0:1], op0=ALU.mult, op1=ALU.add))
            b.op("vector", lambda v, toff=toff: v.tensor_scalar(
                out=kT[:, toff:toff + TOK], in0=qkT[:, :],
                scalar1=self.c["gk"][:, 0:1], scalar2=self.c["bk"][:, 0:1],
                op0=ALU.mult, op1=ALU.add), sem=dve)
            b.wait("tensor", dve, dve.v)
            whb = b.tile([128, KC * 512], BF16)
            vt = [b.tile([128, 512], BF16) for _ in range(2)]
            w_cons = 0
            pa_free = [0, 0]
            vst = [0, 0]
            awh = self.s("awh")
            avs = [self.s("avs0"), self.s("avs1")]
            for wb in range(16 if th == 0 else 8):
                b.wait("sync", pe, w_cons)
                lw = self.wload("W_hidden_bf", wb * 512, (wb + 1) * 512,
                                whb, awh)
                b.wait("tensor", awh, lw)
                if wb < 8:
                    for t in range(16):
                        tg = th * 16 + t
                        bk = t % 2
                        b.wait("tensor", act, pa_free[bk])
                        for k in range(KC):
                            b.op("tensor", lambda tt, k=k, t=t, bk=bk:
                                 tt.matmul(
                                     out=pq[bk][:, :],
                                     lhsT=ln0T[:, k * TOK + t * 128:
                                               k * TOK + t * 128 + 128],
                                     rhs=whb[:, k * 512:(k + 1) * 512],
                                     start=(k == 0), stop=False))
                        b.op("tensor", lambda tt, wb=wb, bk=bk: tt.matmul(
                            out=pq[bk][:, :], lhsT=self.c["ones_b"][:, :],
                            rhs=self.c["b_hidden_bf"][0:1, wb * 512:
                                                      (wb + 1) * 512],
                            start=False, stop=True))
                        pvv = b.inc_last("tensor", pe)
                        b.wait("scalar", pe, pvv)
                        b.wait("scalar", avs[bk], vst[bk])
                        av = b.op("scalar", lambda s, bk=bk: s.activation(
                            out=vt[bk][:, :], in_=pq[bk][:, :],
                            func=AF.Silu), sem=act)
                        pa_free[bk] = av
                        b.wait("gpsimd", act, av)
                        b.dma("gpsimd", vhbm[2 * wb, tg, :, :],
                              vt[bk][:, 0:256], avs[bk])
                        vst[bk] = b.dma("gpsimd", vhbm[2 * wb + 1, tg, :, :],
                                        vt[bk][:, 256:512], avs[bk])
                else:
                    for gfc in range(4):
                        for tb in range(4):
                            bk = (gfc * 4 + tb) % 2
                            b.wait("tensor", act, pa_free[bk])
                            for k in range(KC):
                                b.op("tensor", lambda tt, k=k, gfc=gfc,
                                     tb=tb, bk=bk: tt.matmul(
                                         out=pq[bk][:, :],
                                         lhsT=whb[:, k * 512 + gfc * 128:
                                                  k * 512 + gfc * 128 + 128],
                                         rhs=ln0T[:, k * TOK + tb * 512:
                                                  k * TOK + tb * 512 + 512],
                                         start=(k == 0),
                                         stop=(k == KC - 1)))
                            pvv = b.inc_last("tensor", pe)
                            b.wait("scalar", pe, pvv)
                            b.wait("scalar", avs[bk], vst[bk])
                            gch = 32 + (wb - 8) * 4 + gfc
                            av = b.op("scalar", lambda s, bk=bk, gch=gch:
                                      s.activation(
                                          out=vt[bk][:, :], in_=pq[bk][:, :],
                                          func=AF.Silu,
                                          bias=self.c["b_hidden"]
                                          [:, gch:gch + 1]), sem=act)
                            pa_free[bk] = av
                            b.wait("gpsimd", act, av)
                            vst[bk] = b.dma(
                                "gpsimd", gthbm[tb, (wb - 8) * 4 + gfc, :, :],
                                vt[bk][:, :], avs[bk])
                w_cons = pe.v
            self.wait_dma_all("gpsimd")
            b.wait("tensor", act, act.v)
            b.barrier()
            b.reset(m1)
        attnT = b.tile([128, 32 * 512], BF16)
        ogT = b.tile([128, 32 * 512], BF16)
        vsec = [b.tile([128, 32 * 256], BF16) for _ in range(2)]
        wob_t = b.tile([128, 32 * 512], BF16)
        wob = [wob_t, wob_t]
        gt2 = [b.tile([128, 2 * 512], BF16) for _ in range(2)]
        simf = [b.tile([128, 512], F32) for _ in range(2)]
        xr2 = [b.tile([128, 512], F32) for _ in range(2)]
        yo2 = [b.tile([128, 512], F32) for _ in range(2)]
        psim = [b.pb[0], b.pb[1]]
        pout = [b.pb[2], b.pb[3], b.pb[4], b.pb[5]]
        pyy = [b.pb[6], b.pb[7]]
        attn_pe_reads = 0
        og_pe_reads = 0
        for g in range(4):
            psim_free = [0, 0]
            simf_free = [0, 0]
            b.wait("scalar", pe, attn_pe_reads)
            for tkc in range(32):
                sp = tkc % 2
                b.wait("tensor", dve, psim_free[sp])
                b.op("tensor", lambda t, tkc=tkc, g=g, sp=sp: t.matmul(
                    out=psim[sp][:, :],
                    lhsT=kT[:, tkc * 128:(tkc + 1) * 128],
                    rhs=qT[:, g * 512:(g + 1) * 512], start=True, stop=True))
                pvv = b.inc_last("tensor", pe)
                b.wait("vector", pe, pvv)
                b.wait("vector", act, simf_free[sp])
                dv = b.op("vector", lambda v, sp=sp: v.tensor_scalar_max(
                    out=simf[sp][:, :], in0=psim[sp][:, :], scalar1=0.0),
                    sem=dve)
                psim_free[sp] = dv
                b.wait("scalar", dve, dv)
                simf_free[sp] = b.op(
                    "scalar", lambda s, sp=sp, tkc=tkc: s.activation(
                        out=attnT[:, tkc * 512:(tkc + 1) * 512],
                        in_=simf[sp][:, :], func=AF.Square,
                        scale=1.0 / TOKA), sem=act)
            attn_ready = act.v
            vcons = [0, 0]
            gcons = [0, 0]
            vld = [0, 0]
            gld = [0, 0]
            po_free = [0, 0, 0, 0]
            b.wait("tensor", act, attn_ready)
            b.wait("vector", pe, og_pe_reads)
            avv = [self.s("av0"), self.s("av1")]
            agg = [self.s("ag0"), self.s("ag1")]
            for s in range(16):
                vp = s % 2
                b.wait("sync", pe, vcons[vp])
                vld[vp] = b.dma("sync",
                                vsec[vp].rearrange("p (t f) -> p t f", t=32),
                                vhbm[s, :, :, :].rearrange("t p f -> p t f"),
                                avv[vp])
                b.wait("gpsimd", dve, gcons[vp])
                b.dma("gpsimd", gt2[vp][:, 0:512], gthbm[g, 2 * s, :, :],
                      agg[vp])
                gld[vp] = b.dma("gpsimd", gt2[vp][:, 512:1024],
                                gthbm[g, 2 * s + 1, :, :], agg[vp])
                b.wait("tensor", avv[vp], vld[vp])
                for vfc in range(2):
                    ob = (2 * s + vfc) % 4
                    b.wait("tensor", dve, po_free[ob])
                    for tkc in range(32):
                        b.op("tensor", lambda t, vfc=vfc, tkc=tkc, ob=ob,
                             vp=vp: t.matmul(
                                 out=pout[ob][:, :],
                                 lhsT=vsec[vp][:, tkc * 256 + vfc * 128:
                                               tkc * 256 + vfc * 128 + 128],
                                 rhs=attnT[:, tkc * 512:(tkc + 1) * 512],
                                 start=(tkc == 0), stop=(tkc == 31)))
                    pvv = b.inc_last("tensor", pe)
                    b.wait("vector", pe, pvv)
                    b.wait("vector", agg[vp], gld[vp])
                    dv = b.op("vector", lambda v, s=s, vfc=vfc, ob=ob, vp=vp:
                              v.tensor_tensor(
                                  out=ogT[:, (2 * s + vfc) * 512:
                                          (2 * s + vfc) * 512 + 512],
                                  in0=pout[ob][:, :],
                                  in1=gt2[vp][:, vfc * 512:(vfc + 1) * 512],
                                  op=ALU.mult), sem=dve)
                    po_free[ob] = dv
                    gcons[vp] = dv
                vcons[vp] = pe.v
            og_ready = dve.v
            attn_pe_reads = pe.v
            wob_cons = [0, 0]
            pyy_free = [0, 0]
            xr_cons = [0, 0]
            yo_st = [0, 0]
            b.wait("tensor", dve, og_ready)
            awo = self.s("awo")
            axs = [self.s("ax0"), self.s("ax1")]
            ays = [self.s("ay0"), self.s("ay1")]
            for dc in range(4):
                wp = 0
                b.wait("sync", pe, wob_cons[wp])
                lwo = self.wload("W_out0_bf", dc * 512, (dc + 1) * 512,
                                 wob[wp], awo)
                b.wait("tensor", awo, lwo)
                for tc in range(4):
                    yk = tc % 2
                    trow = g * 512 + tc * 128
                    b.wait("tensor", dve, pyy_free[yk])
                    for vfc in range(32):
                        b.op("tensor", lambda t, vfc=vfc, tc=tc, yk=yk,
                             wp=wp: t.matmul(
                                 out=pyy[yk][:, :],
                                 lhsT=ogT[:, vfc * 512 + tc * 128:
                                          vfc * 512 + tc * 128 + 128],
                                 rhs=wob[wp][:, vfc * 512:(vfc + 1) * 512],
                                 start=(vfc == 0), stop=False))
                    b.op("tensor", lambda t, dc=dc, yk=yk: t.matmul(
                        out=pyy[yk][:, :], lhsT=self.c["ones_b"][:, :],
                        rhs=self.c["b_out0_bf"][0:1, dc * 512:(dc + 1) * 512],
                        start=False, stop=True))
                    pvv = b.inc_last("tensor", pe)
                    b.wait("gpsimd", dve, xr_cons[yk])
                    xld = b.dma("gpsimd", xr2[yk][:, :],
                                xA[trow:trow + 128, dc * 512:(dc + 1) * 512],
                                axs[yk])
                    b.wait("vector", pe, pvv)
                    b.wait("vector", axs[yk], xld)
                    b.wait("vector", ays[yk], yo_st[yk])
                    dv = b.op("vector", lambda v, yk=yk: v.tensor_tensor(
                        out=yo2[yk][:, :], in0=pyy[yk][:, :],
                        in1=xr2[yk][:, :], op=ALU.add), sem=dve)
                    pyy_free[yk] = dv
                    xr_cons[yk] = dv
                    b.wait("gpsimd", dve, dv)
                    yo_st[yk] = b.dma("gpsimd",
                                      xout[trow:trow + 128,
                                           dc * 512:(dc + 1) * 512],
                                      yo2[yk][:, :], ays[yk])
                wob_cons[wp] = pe.v
            og_pe_reads = pe.v
        self.wait_dma_all("gpsimd")
        b.barrier()
        b.reset(mA)

    def emit_ffn(self, xin, xout, w_up1, w_up2, w_dn, mode):
        b = self.b
        ld = self.sm["ld"]
        ldg, stg = self.sm["ldg"], self.sm["stg"]
        dve, act, pe = self.sm["dve"], self.sm["act"], self.sm["pe"]
        m0 = b.mark()
        if mode == "L1":
            self.prep_params(chunks=[("gate_s", FF), ("up_s", FF)],
                             bcasts=[("down_s", D)])
        HT = 1024
        lnT = b.tile([128, KC * HT], BF16)
        hdnT = b.tile([128, FFC * HT], BF16)
        sgt = [b.tile([128, 512], F32) for _ in range(2)]
        xr = [b.tile([128, 512], F32) for _ in range(2)]
        yo = [b.tile([128, 512], F32) for _ in range(2)]
        pu = [b.pb[0], b.pb[1], b.pb[2], b.pb[3]]
        py = b.pb
        for half in range(2):
            toff = half * HT
            mh = b.mark()
            self.emit_ln(xin[toff:toff + HT, :], lnT, HT // 128)
            b.barrier()
            w1b = b.tile([128, KC * 512], BF16)
            w2b = b.tile([128, KC * 512], BF16)
            w3b = [b.tile([128, 6 * 512], BF16) for _ in range(2)]
            w_free = 0
            pu_free = [0, 0, 0, 0]
            sgt_free = [0, 0]
            fwu = self.s("fwu")
            for hb in range(11):
                b.wait("sync", pe, w_free)
                self.wload(w_up1, hb * 512, (hb + 1) * 512, w1b, fwu)
                l2 = self.wload(w_up2, hb * 512, (hb + 1) * 512, w2b, fwu)
                b.wait("tensor", fwu, l2)
                for hc in range(4):
                    hcg = hb * 4 + hc
                    for tb in range(2):
                        bk = (hc * 2 + tb) % 2
                        b.wait("tensor", act, pu_free[bk])
                        b.wait("tensor", dve, pu_free[2 + bk])
                        for wb_, po in ((w1b, 0), (w2b, 2)):
                            for k in range(KC):
                                b.op("tensor", lambda t, k=k, hc=hc, tb=tb,
                                     bk=bk, wb_=wb_, po=po: t.matmul(
                                         out=pu[po + bk][:, :],
                                         lhsT=wb_[:, k * 512 + hc * 128:
                                                  k * 512 + hc * 128 + 128],
                                         rhs=lnT[:, k * HT + tb * 512:
                                                 k * HT + tb * 512 + 512],
                                         start=(k == 0),
                                         stop=(k == KC - 1)))
                        pvv = b.inc_last("tensor", pe)
                        b.wait("scalar", pe, pvv)
                        b.wait("scalar", dve, sgt_free[bk])
                        if mode == "L0":
                            av = b.op("scalar", lambda s, bk=bk: s.activation(
                                out=sgt[bk][:, :], in_=pu[bk][:, :],
                                func=AF.Sigmoid, scale=0.5), sem=act)
                        else:
                            av = b.op("scalar", lambda s, bk=bk, hcg=hcg:
                                      s.activation(
                                          out=sgt[bk][:, :],
                                          in_=pu[bk][:, :], func=AF.Silu,
                                          scale=self.c["gate_s"]
                                          [:, hcg:hcg + 1]), sem=act)
                        pu_free[bk] = av
                        b.wait("vector", act, av)
                        if mode == "L0":
                            dv = b.op("vector", lambda v, bk=bk, hcg=hcg,
                                      tb=tb: v.tensor_tensor(
                                          out=hdnT[:, hcg * HT + tb * 512:
                                                   hcg * HT + tb * 512 + 512],
                                          in0=sgt[bk][:, :],
                                          in1=pu[2 + bk][:, :],
                                          op=ALU.mult), sem=dve)
                        else:
                            dv = b.op("vector", lambda v, bk=bk, hcg=hcg,
                                      tb=tb: v.scalar_tensor_tensor(
                                          out=hdnT[:, hcg * HT + tb * 512:
                                                   hcg * HT + tb * 512 + 512],
                                          in0=pu[2 + bk][:, :],
                                          scalar=self.c["up_s"]
                                          [:, hcg:hcg + 1],
                                          in1=sgt[bk][:, :],
                                          op0=ALU.mult, op1=ALU.mult),
                                      sem=dve)
                        pu_free[2 + bk] = dv
                        sgt_free[bk] = dv
                w_free = pe.v
            b.wait("tensor", act, act.v)
            b.wait("tensor", dve, dve.v)
            xr_cons = [0, 0]
            yo_st = [0, 0]
            py_all_free = 0
            fw3 = [self.s("fw30"), self.s("fw31")]
            fxs = [self.s("fx0"), self.s("fx1")]
            fys = [self.s("fy0"), self.s("fy1")]
            w3_cons = [0, 0]
            for dc in range(4):
                for rb in range(8):
                    wp = rb % 2
                    nj = 6 if rb < 7 else 2
                    b.wait("sync", pe, w3_cons[wp])
                    l3 = b.dma("sync",
                               w3b[wp].rearrange("p (b f) -> p b f", b=6),
                               self.wview_rows(w_dn, rb, dc * 512,
                                               (dc + 1) * 512), fw3[wp])
                    b.wait("tensor", fw3[wp], l3)
                    if rb == 0:
                        b.wait("tensor", dve, py_all_free)
                    for tc in range(8):
                        for hc in range(nj):
                            hcg = rb * 6 + hc
                            b.op("tensor", lambda t, hcg=hcg, hc=hc, tc=tc,
                                 wp=wp: t.matmul(
                                     out=py[tc][:, :],
                                     lhsT=hdnT[:, hcg * HT + tc * 128:
                                               hcg * HT + tc * 128 + 128],
                                     rhs=w3b[wp][:, hc * 512:(hc + 1) * 512],
                                     start=(hcg == 0),
                                     stop=(hcg == FFC - 1)))
                    w3_cons[wp] = b.inc_last("tensor", pe)
                pvv = pe.v
                for tc in range(8):
                    bk = tc % 2
                    trow = toff + tc * 128
                    b.wait("gpsimd", dve, xr_cons[bk])
                    xld = b.dma("gpsimd", xr[bk][:, :],
                                xin[trow:trow + 128,
                                    dc * 512:(dc + 1) * 512], fxs[bk])
                    b.wait("vector", pe, pvv)
                    b.wait("vector", fxs[bk], xld)
                    b.wait("vector", fys[bk], yo_st[bk])
                    if mode == "L0":
                        dv = b.op("vector", lambda v, bk=bk, tc=tc:
                                  v.scalar_tensor_tensor(
                                      out=yo[bk][:, :], in0=py[tc][:, :],
                                      scalar=0.25, in1=xr[bk][:, :],
                                      op0=ALU.mult, op1=ALU.add), sem=dve)
                    else:
                        b.op("vector", lambda v, bk=bk, tc=tc, dc=dc:
                             v.tensor_tensor(
                                 out=yo[bk][:, :], in0=py[tc][:, :],
                                 in1=self.c["down_s"][:, dc * 512:
                                                      (dc + 1) * 512],
                                 op=ALU.mult))
                        dv = b.op("vector", lambda v, bk=bk:
                                  v.tensor_tensor(
                                      out=yo[bk][:, :], in0=yo[bk][:, :],
                                      in1=xr[bk][:, :], op=ALU.add), sem=dve)
                    xr_cons[bk] = dv
                    b.wait("gpsimd", dve, dv)
                    yo_st[bk] = b.dma("gpsimd",
                                      xout[trow:trow + 128,
                                           dc * 512:(dc + 1) * 512],
                                      yo[bk][:, :], fys[bk])
                py_all_free = dve.v
            b.wait("tensor", dve, dve.v)
            self.wait_dma_all("gpsimd")
            b.barrier()
            b.reset(mh)
        b.barrier()
        b.reset(m0)

    def emit_C(self, xin, xout):
        b, nc = self.b, self.nc
        ld = self.sm["ld"]
        ldg, stg = self.sm["ldg"], self.sm["stg"]
        dve, act, pe = self.sm["dve"], self.sm["act"], self.sm["pe"]
        cc = self.sm["cc"]
        Xbuf = nc.dram_tensor("c_X", [TOK, D], F32, kind="Internal").ap()
        Kbuf = nc.dram_tensor("c_K", [TOK, D], F32, kind="Internal").ap()
        Vbuf = nc.dram_tensor("c_V", [TOK, D], F32, kind="Internal").ap()
        ag_in = nc.dram_tensor("c_agin", [1, D], F32, kind="Internal").ap()
        ag_out = nc.dram_tensor("c_agout", [8, D], F32, kind="Internal").ap()
        asc = nc.dram_tensor("c_asc", [1, D], F32, kind="Internal").ap()
        mC = b.mark()
        QT = b.tile([128, KC * TOK], BF16)
        Achunks = b.tile([128, 16], F32)
        self.prep_params(chunks=[("g1", D), ("be1", D), ("b_in", 6144)],
                         bf16rows=[("b_in_bf", 6144), ("b_out1_bf", 2048)])
        m1 = b.mark()
        XnT = b.tile([128, KC * TOK], BF16)
        m2 = b.mark()
        self.emit_ln(xin, XnT, TOK // 128, keep_f32=Xbuf, inner=True,
                     gamma_beta=(self.c["g1"], self.c["be1"]))
        b.barrier()
        winb = b.tile([128, KC * 512], BF16)
        kv = [b.tile([128, 512], F32) for _ in range(2)]
        cst = [b.tile([128, 1024], F32) for _ in range(2)]
        rh = b.tile([128, 512], F32)
        tmp1 = b.tile([128, 512], F32)
        pk = [b.pb[0], b.pb[1]]
        pA = [b.pb[2], b.pb[3], b.pb[4], b.pb[5]]
        for which in range(2):
            coloff = D * (1 + which)
            dst = Kbuf if which == 0 else Vbuf
            w_cons = 0
            cwi = self.s("cwi")
            ccs = [self.s("ccs0"), self.s("ccs1")]
            cks = [self.s("cks0"), self.s("cks1")]
            for kcb in range(4):
                b.wait("sync", pe, w_cons)
                lw = self.wload("W_in_bf", coloff + kcb * 512,
                                coloff + kcb * 512 + 512, winb, cwi)
                b.wait("tensor", cwi, lw)
                pk_free = [0, 0]
                cs_ld = [0, 0]
                cs_cons = [0, 0]
                kv_st = [0, 0]
                for tc in range(16):
                    bk = tc % 2
                    if kcb < 2:
                        b.wait("gpsimd", dve, cs_cons[bk])
                        b.dma("gpsimd", cst[bk][:, 0:512],
                              self.io["cosT"][tc * 128:(tc + 1) * 128,
                                              kcb * 512:(kcb + 1) * 512],
                              ccs[bk])
                        cs_ld[bk] = b.dma(
                            "gpsimd", cst[bk][:, 512:1024],
                            self.io["sinT"][tc * 128:(tc + 1) * 128,
                                            kcb * 512:(kcb + 1) * 512],
                            ccs[bk])
                    b.wait("tensor", dve, pk_free[bk])
                    for k in range(KC):
                        b.op("tensor", lambda t, k=k, tc=tc, kcb=kcb, bk=bk:
                             t.matmul(
                                 out=pk[bk][:, :],
                                 lhsT=XnT[:, k * TOK + tc * 128:
                                          k * TOK + tc * 128 + 128],
                                 rhs=winb[:, k * 512:(k + 1) * 512],
                                 start=(k == 0), stop=False))
                    b.op("tensor", lambda t, kcb=kcb, coloff=coloff, bk=bk:
                         t.matmul(
                             out=pk[bk][:, :], lhsT=self.c["ones_b"][:, :],
                             rhs=self.c["b_in_bf"][0:1, coloff + kcb * 512:
                                                   coloff + kcb * 512 + 512],
                             start=False, stop=True))
                    pvv = b.inc_last("tensor", pe)
                    b.wait("vector", pe, pvv)
                    b.wait("vector", cks[bk], kv_st[bk])
                    if kcb < 2:
                        b.wait("vector", ccs[bk], cs_ld[bk])
                        b.op("vector", lambda v, bk=bk: v.tensor_scalar_mul(
                            out=rh.rearrange("p (c two) -> p c two", two=2)
                            [:, :, 0:1],
                            in0=pk[bk][:, 0:512].rearrange(
                                "p (c two) -> p c two", two=2)[:, :, 1:2],
                            scalar1=-1.0))
                        b.op("vector", lambda v, bk=bk: v.tensor_copy(
                            out=rh.rearrange("p (c two) -> p c two", two=2)
                            [:, :, 1:2],
                            in_=pk[bk][:, 0:512].rearrange(
                                "p (c two) -> p c two", two=2)[:, :, 0:1]))
                        b.op("vector", lambda v, bk=bk: v.tensor_tensor(
                            out=tmp1[:, :], in0=pk[bk][:, :],
                            in1=cst[bk][:, 0:512], op=ALU.mult))
                        b.op("vector", lambda v, bk=bk: v.tensor_tensor(
                            out=rh[:, :], in0=rh[:, :],
                            in1=cst[bk][:, 512:1024], op=ALU.mult))
                        dv = b.op("vector", lambda v, bk=bk: v.tensor_tensor(
                            out=kv[bk][:, :], in0=tmp1[:, :], in1=rh[:, :],
                            op=ALU.add), sem=dve)
                        cs_cons[bk] = dv
                    else:
                        dv = b.op("vector", lambda v, bk=bk: v.tensor_copy(
                            out=kv[bk][:, :], in_=pk[bk][:, :]), sem=dve)
                    pk_free[bk] = dv
                    b.wait("gpsimd", dve, dv)
                    kv_st[bk] = b.dma("gpsimd",
                                      dst[tc * 128:(tc + 1) * 128,
                                          kcb * 512:(kcb + 1) * 512],
                                      kv[bk][:, :], cks[bk])
                w_cons = pe.v
        self.wait_dma_all("gpsimd")
        b.barrier()
        b.reset(m2)
        Kf = b.tile([128, D], F32)
        Vf = b.tile([128, D], F32)
        ssq = b.tile([128, 2], F32)
        rn = b.tile([128, 2], F32)
        Kc2 = b.tile([128, D], BF16)
        Pc = b.tile([128, D], BF16)
        dump = Pc
        kv_cons = [0, 0]
        pe_P = 0
        ckv = self.s("ckv")
        for tc in range(16):
            b.wait("gpsimd", act, kv_cons[0])
            b.wait("gpsimd", dve, kv_cons[1])
            b.dma("gpsimd", Kf[:, :], Kbuf[tc * 128:(tc + 1) * 128, :], ckv)
            kld = b.dma("gpsimd", Vf[:, :],
                        Vbuf[tc * 128:(tc + 1) * 128, :], ckv)
            b.wait("scalar", ckv, kld)
            b.wait("scalar", pe, pe_P)
            b.op("scalar", lambda s: s.activation(
                out=dump[:, :], in_=Kf[:, :], func=AF.Square,
                accum_out=ssq[:, 0:1]))
            sv = b.op("scalar", lambda s: s.activation(
                out=dump[:, :], in_=Vf[:, :], func=AF.Square,
                accum_out=ssq[:, 1:2]), sem=act)
            b.wait("scalar", act, sv)
            b.op("scalar", lambda s: s.activation(
                out=rn[:, :], in_=ssq[:, :], func=AF.Ln))
            av = b.op("scalar", lambda s: s.activation(
                out=rn[:, :], in_=rn[:, :], func=AF.Exp, scale=-0.5), sem=act)
            kv_cons[0] = av
            b.wait("vector", act, av)
            b.wait("vector", pe, pe_P)
            b.op("vector", lambda v: v.tensor_scalar_mul(
                out=Kc2[:, :], in0=Kf[:, :], scalar1=rn[:, 0:1]))
            b.op("vector", lambda v: v.tensor_scalar_mul(
                out=Pc[:, :], in0=Vf[:, :], scalar1=rn[:, 1:2]))
            dv = b.op("vector", lambda v: v.tensor_tensor(
                out=Pc[:, :], in0=Pc[:, :], in1=Kc2[:, :], op=ALU.mult),
                sem=dve)
            kv_cons[1] = dv
            b.wait("tensor", dve, dv)
            for j in range(4):
                b.op("tensor", lambda t, j=j, tc=tc: t.matmul(
                    out=pA[j][0:1, :], lhsT=self.c["ones_col_b"][:, 0:1],
                    rhs=Pc[:, j * 512:(j + 1) * 512],
                    start=(tc == 0), stop=(tc == 15)))
            pe_P = b.inc_last("tensor", pe)
        Asb = b.tile([1, D], F32)
        agsb = b.tile([8, D], F32)
        shtmp = b.tile([4, D], F32)
        Astg = b.tile([128, 128], F32)
        masks = b.tile([8, 1], F32)
        lm = b.dma("sync", masks[0:8, 0:1], self.io["pairmask"][:, :], ld)
        b.wait("vector", pe, pe.v)
        for j in range(4):
            b.op("vector", lambda v, j=j: v.tensor_copy(
                out=Asb[0:1, j * 512:(j + 1) * 512], in_=pA[j][0:1, :]),
                sem=dve)
        b.wait("gpsimd", dve, dve.v)
        b.dma("gpsimd", ag_in[:, :], Asb[0:1, :], stg)
        b.wait("gpsimd", stg, stg.v)
        b.coll("AllGather", ALU.bypass, G8, [ag_in.opt()], [ag_out.opt()], cc)
        b.wait("gpsimd", cc, cc.v)
        gl = b.dma("gpsimd", agsb[0:8, :], ag_out[:, :], ldg)
        b.wait("vector", ldg, gl)
        b.wait("vector", ld, lm)
        b.op("vector", lambda v: v.tensor_scalar_mul(
            out=agsb[0:8, :], in0=agsb[0:8, :], scalar1=masks[0:8, 0:1]),
            sem=dve)
        for half in (4, 2, 1):
            b.wait("gpsimd", dve, dve.v)
            sh = b.dma("gpsimd", shtmp[0:half, :],
                       agsb[half:2 * half, :], ldg)
            b.wait("vector", ldg, sh)
            b.op("vector", lambda v, half=half: v.tensor_tensor(
                out=agsb[0:half, :], in0=agsb[0:half, :],
                in1=shtmp[0:half, :], op=ALU.add), sem=dve)
        b.op("vector", lambda v: v.tensor_scalar(
            out=Asb[0:1, :], in0=agsb[0:1, :],
            scalar1=0.5, scalar2=0.5, op0=ALU.mult, op1=ALU.add), sem=dve)
        b.wait("gpsimd", dve, dve.v)
        b.dma("gpsimd", asc[:, :], Asb[0:1, :], stg)
        b.wait("gpsimd", stg, stg.v)
        b.op("vector", lambda v: v.memset(Astg[0:128, 0:128], 0.0), sem=dve)
        b.wait("gpsimd", dve, dve.v)
        al = b.dma("gpsimd", Astg[0:16, 0:128],
                   asc.rearrange("one (c j) -> (one c) j", j=128), ldg)
        b.wait("tensor", ldg, al)
        b.wait("tensor", dve, dve.v)
        b.op("tensor", lambda t: t.transpose(
            out=pk[0][:, 0:128], in_=Astg[0:128, 0:128],
            identity=self.c["ident_f"][:, :]), sem=pe)
        b.wait("vector", pe, pe.v)
        b.op("vector", lambda v: v.tensor_copy(
            out=Achunks[:, :], in_=pk[0][:, 0:16]), sem=dve)
        b.wait("tensor", dve, dve.v)
        b.barrier()
        b.reset(m2)
        winb = b.tile([128, KC * 512], BF16)
        cwi = self.s("cwi")
        wq_cons = 0
        pq_free = [0, 0]
        for dcq in range(4):
            b.wait("sync", pe, wq_cons)
            lw = self.wload("W_in_bf", dcq * 512, (dcq + 1) * 512,
                            winb, cwi)
            b.wait("tensor", cwi, lw)
            for qcc in range(4):
                qc = dcq * 4 + qcc
                for tb in range(4):
                    bk = (qcc * 4 + tb) % 2
                    b.wait("tensor", dve, pq_free[bk])
                    for k in range(KC):
                        b.op("tensor", lambda t, k=k, qcc=qcc, tb=tb, bk=bk:
                             t.matmul(
                                 out=pk[bk][:, :],
                                 lhsT=winb[:, k * 512 + qcc * 128:
                                           k * 512 + qcc * 128 + 128],
                                 rhs=XnT[:, k * TOK + tb * 512:
                                         k * TOK + tb * 512 + 512],
                                 start=(k == 0), stop=(k == KC - 1)))
                    pvv = b.inc_last("tensor", pe)
                    b.wait("vector", pe, pvv)
                    dv = b.op("vector", lambda v, qc=qc, tb=tb, bk=bk:
                              v.tensor_scalar(
                                  out=QT[:, qc * TOK + tb * 512:
                                         qc * TOK + tb * 512 + 512],
                                  in0=pk[bk][:, :],
                                  scalar1=self.c["b_in"][:, qc:qc + 1],
                                  scalar2=Achunks[:, qc:qc + 1],
                                  op0=ALU.add, op1=ALU.mult), sem=dve)
                    pq_free[bk] = dv
            wq_cons = pe.v
        b.wait("tensor", dve, dve.v)
        self.wait_dma_all("gpsimd")
        b.barrier()
        b.reset(m1)
        wout = b.tile([128, KC * D], BF16)
        xr = [b.tile([128, 512], F32) for _ in range(2)]
        Xr = [b.tile([128, 512], F32) for _ in range(2)]
        yo = [b.tile([128, 512], F32) for _ in range(2)]
        pyy = [b.pb[0], b.pb[1]]
        cwo = self.s("cwo")
        cxs = [self.s("cx0"), self.s("cx1")]
        cys = [self.s("cy0"), self.s("cy1")]
        lw = self.wload("W_out1_bf", 0, D, wout, cwo)
        b.wait("tensor", cwo, lw)
        pyy_free = [0, 0]
        xr_cons = [0, 0]
        yo_st = [0, 0]
        for tc in range(16):
            for dc in range(4):
                yk = (tc * 4 + dc) % 2
                trow = tc * 128
                b.wait("tensor", dve, pyy_free[yk])
                for qc in range(KC):
                    b.op("tensor", lambda t, qc=qc, tc=tc, dc=dc, yk=yk:
                         t.matmul(
                             out=pyy[yk][:, :],
                             lhsT=QT[:, qc * TOK + tc * 128:
                                     qc * TOK + tc * 128 + 128],
                             rhs=wout[:, qc * D + dc * 512:
                                      qc * D + dc * 512 + 512],
                             start=(qc == 0), stop=False))
                b.op("tensor", lambda t, dc=dc, yk=yk: t.matmul(
                    out=pyy[yk][:, :], lhsT=self.c["ones_b"][:, :],
                    rhs=self.c["b_out1_bf"][0:1, dc * 512:(dc + 1) * 512],
                    start=False, stop=True))
                pvv = b.inc_last("tensor", pe)
                b.wait("gpsimd", dve, xr_cons[yk])
                b.dma("gpsimd", xr[yk][:, :],
                      xin[trow:trow + 128, dc * 512:(dc + 1) * 512],
                      cxs[yk])
                xld = b.dma("gpsimd", Xr[yk][:, :],
                            Xbuf[trow:trow + 128, dc * 512:(dc + 1) * 512],
                            cxs[yk])
                b.wait("vector", pe, pvv)
                b.wait("vector", cxs[yk], xld)
                b.wait("vector", cys[yk], yo_st[yk])
                b.op("vector", lambda v, yk=yk: v.tensor_tensor(
                    out=yo[yk][:, :], in0=pyy[yk][:, :], in1=Xr[yk][:, :],
                    op=ALU.add))
                dv = b.op("vector", lambda v, yk=yk: v.tensor_tensor(
                    out=yo[yk][:, :], in0=yo[yk][:, :], in1=xr[yk][:, :],
                    op=ALU.add), sem=dve)
                pyy_free[yk] = dv
                xr_cons[yk] = dv
                b.wait("gpsimd", dve, dv)
                yo_st[yk] = b.dma("gpsimd",
                                  xout[trow:trow + 128,
                                       dc * 512:(dc + 1) * 512],
                                  yo[yk][:, :], cys[yk])
        self.wait_dma_all("gpsimd")
        b.barrier()
        b.reset(mC)



QNUMEL = {"w1qT": FF * D, "w2qT": FF * D, "w3qT": FF * D}

WSPEC = {
    "A": dict(weights=[("W_hidden_sl", "W_hidden_bf", D, 8192),
                       ("W_qk_sl", "W_qk_bf", D, 128),
                       ("W_out0_sl", "W_out0_bf", 4096, D)], quant=[]),
    "B": dict(weights=[], quant=[("w1T_sl", "w1qT", D, FF),
                                 ("w2T_sl", "w2qT", D, FF),
                                 ("w3T_sl", "w3qT", 6144, D)]),
    "C": dict(weights=[("W_in_sl", "W_in_bf", D, 3 * D),
                       ("W_out1_sl", "W_out1_bf", D, D)], quant=[]),
    "D": dict(weights=[("gwT_sl", "gwT_bf", D, FF),
                       ("uwT_sl", "uwT_bf", D, FF),
                       ("dwT_sl", "dwT_bf", 6144, D)], quant=[]),
}

CONST_NAMES = ("ident_f", "ident_b", "ones_f", "ones_b")

PHASE_INPUTS = {
    "A": ["x_all", "W_hidden_sl", "W_qk_sl", "W_out0_sl",
          "b_qk", "gq", "bq", "gk", "bk", "b_hidden", "b_hidden_bf",
          "b_out0_bf"],
    "B": ["xs", "w1T_sl", "w2T_sl", "w3T_sl"],
    "C": ["xs", "W_in_sl", "W_out1_sl", "b_in", "b_in_bf", "b_out1_bf",
          "g1", "be1", "cosT", "sinT", "pairmask"],
    "D": ["xs", "gwT_sl", "uwT_sl", "dwT_sl", "gate_s", "up_s", "down_s"],
}

IN_SHAPES = {
    "x_all": ([TOKA, D], F32), "xs": ([TOK, D], F32),
    "W_hidden_sl": ([D // 8, 8192], F32), "W_qk_sl": ([D // 8, 128], F32),
    "W_out0_sl": ([4096 // 8, D], F32),
    "w1T_sl": ([D // 8, FF], F32), "w2T_sl": ([D // 8, FF], F32),
    "w3T_sl": ([768, D], F32),
    "W_in_sl": ([D // 8, 3 * D], F32), "W_out1_sl": ([D // 8, D], F32),
    "gwT_sl": ([D // 8, FF], F32), "uwT_sl": ([D // 8, FF], F32),
    "dwT_sl": ([768, D], F32),
    "b_qk": ([1, 128], F32), "gq": ([1, 128], F32), "bq": ([1, 128], F32),
    "gk": ([1, 128], F32), "bk": ([1, 128], F32),
    "b_hidden": ([1, 8192], F32), "b_hidden_bf": ([1, 8192], F32),
    "b_out0_bf": ([1, D], F32),
    "b_in": ([1, 3 * D], F32), "b_in_bf": ([1, 3 * D], F32),
    "b_out1_bf": ([1, D], F32),
    "g1": ([1, D], F32), "be1": ([1, D], F32),
    "gate_s": ([1, FF], F32), "up_s": ([1, FF], F32),
    "down_s": ([1, D], F32),
    "cosT": ([TOK, 1024], F32), "sinT": ([TOK, 1024], F32),
    "pairmask": ([8, 1], F32),
}

WDST_SHAPES = {
    "W_hidden_bf": [D, 8192], "W_qk_bf": [D, 128], "W_out0_bf": [4096, D],
    "w1qT": [D, FF], "w2qT": [D, FF], "w3qT": [6144, D],
    "W_in_bf": [D, 3 * D], "W_out1_bf": [D, D],
    "gwT_bf": [D, FF], "uwT_bf": [D, FF], "dwT_bf": [6144, D],
}


def build_program(phases, fused):
    nc = bass.Bass("TRN2", target_bir_lowering=False, debug=False)
    es = ExitStack()
    io = {}
    for nm in CONST_NAMES:
        dt = F32 if nm.endswith("_f") else BF16
        shape = [128, 128] if nm.startswith("ident") else [1, 128]
        io[nm] = nc.dram_tensor(nm, shape, dt, kind="ExternalInput").ap()
    names = set()
    for ph in phases:
        names.update(PHASE_INPUTS[ph])
    if fused and phases[0] == "A":
        names.discard("xs")
    for nm in sorted(names):
        shape, dt = IN_SHAPES[nm]
        io[nm] = nc.dram_tensor(nm, shape, dt, kind="ExternalInput").ap()

    if fused:
        out = nc.dram_tensor("xo", [TOK, D], F32, kind="ExternalOutput").ap()
        chain = {}
        prev = io.get("xs")
        for i, ph in enumerate(phases):
            if i == len(phases) - 1:
                nxt = out
            else:
                nxt = nc.dram_tensor(f"xbuf{i}", [TOK, D], F32,
                                     kind="Internal").ap()
            chain[ph] = (prev, nxt)
            prev = nxt
    else:
        out = nc.dram_tensor("xo", [TOK, D], F32, kind="ExternalOutput").ap()
        chain = {ph: (io.get("xs"), out) for ph in phases}
    b = B(nc, es)
    mk = MK(nc, b, io)
    mk.emit_startup()
    for ph in phases:
        spec = WSPEC[ph]
        mk.emit_W(spec["weights"], spec["quant"])
        xin, xout = chain[ph]
        if ph == "A":
            mk.emit_A(io["x_all"], xout)
        elif ph == "B":
            mk.emit_ffn(xin, xout, "w1qT", "w2qT", "w3qT", "L0")
        elif ph == "C":
            mk.emit_C(xin, xout)
        elif ph == "D":
            mk.emit_ffn(xin, xout, "gwT_bf", "uwT_bf", "dwT_bf", "L1")
    mk.wait_dma_all("gpsimd")
    b.finalize()
    return nc, es


_PROG_CACHE = {}


def _consts():
    import ml_dtypes
    return {
        "ident_f": np.eye(128, dtype=np.float32),
        "ident_b": np.eye(128, dtype=ml_dtypes.bfloat16),
        "ones_f": np.ones((1, 128), np.float32),
        "ones_b": np.ones((1, 128), ml_dtypes.bfloat16),
    }


def _rotary_tables(pos0, n, rot_dim=1024, theta=10000.0):
    inv = 1.0 / theta ** (np.arange(0, rot_dim, 2, dtype=np.float64) / rot_dim)
    f = (np.arange(pos0, pos0 + n, dtype=np.float64)[:, None] * inv[None, :])
    f = np.repeat(f, 2, axis=1)
    return (np.cos(f).astype(np.float32), np.sin(f).astype(np.float32))


def _prep_inputs(inputs):
    f = lambda a: np.ascontiguousarray(np.asarray(a), dtype=np.float32)
    x = f(inputs["L0_attn_x"])
    full = {
        "W_hidden": f(inputs["L0_attn_W_hidden"]),
        "W_qk": f(inputs["L0_attn_W_qk"]),
        "W_out0": f(inputs["L0_attn_W_out"]),
        "w1T": f(np.asarray(inputs["L0_ffn_w1"]).T),
        "w2T": f(np.asarray(inputs["L0_ffn_w2"]).T),
        "w3T": np.concatenate([f(np.asarray(inputs["L0_ffn_w3"]).T),
                               np.zeros((512, D), np.float32)], 0),
        "W_in": f(inputs["L1_attn_W_in"]),
        "W_out1": f(inputs["L1_attn_W_out"]),
        "gwT": f(np.asarray(inputs["L1_ffn_gate_w"]).T),
        "uwT": f(np.asarray(inputs["L1_ffn_up_w"]).T),
        "dwT": np.concatenate([f(np.asarray(inputs["L1_ffn_down_w"]).T),
                               np.zeros((512, D), np.float32)], 0),
    }
    row = lambda a: f(a).reshape(1, -1)
    vecs = {
        "b_qk": row(inputs["L0_attn_b_qk"]),
        "gq": row(inputs["L0_attn_gamma_q"]),
        "bq": row(inputs["L0_attn_beta_q"]),
        "gk": row(inputs["L0_attn_gamma_k"]),
        "bk": row(inputs["L0_attn_beta_k"]),
        "b_hidden": row(inputs["L0_attn_b_hidden"]),
        "b_hidden_bf": row(inputs["L0_attn_b_hidden"]),
        "b_out0_bf": row(inputs["L0_attn_b_out"]),
        "b_in": row(inputs["L1_attn_b_in"]),
        "b_in_bf": row(inputs["L1_attn_b_in"]),
        "b_out1_bf": row(inputs["L1_attn_b_out"]),
        "g1": row(inputs["L1_attn_gamma"]),
        "be1": row(inputs["L1_attn_beta"]),
        "gate_s": row(inputs["L1_ffn_gate_s"]),
        "up_s": row(inputs["L1_ffn_up_s"]),
        "down_s": row(inputs["L1_ffn_down_s"]),
    }
    consts = _consts()
    cores = []
    for c in range(8):
        bidx, h = c // 2, c % 2
        own = x[bidx, h * TOK:(h + 1) * TOK, :]
        other = x[bidx, (1 - h) * TOK:(2 - h) * TOK, :]
        cosT, sinT = _rotary_tables(h * TOK, TOK)
        m = dict(consts)
        m.update(vecs)
        m["x_all"] = np.ascontiguousarray(np.concatenate([own, other], 0))
        m["cosT"] = cosT
        m["sinT"] = sinT
        pm = np.zeros((8, 1), np.float32)
        pm[2 * bidx, 0] = 1.0
        pm[2 * bidx + 1, 0] = 1.0
        m["pairmask"] = pm
        for nm, w in full.items():
            r = w.shape[0] // 8
            m[nm + "_sl"] = np.ascontiguousarray(w[c * r:(c + 1) * r, :])
        cores.append(m)
    return cores


FUSED = True


def kernel(**inputs):
    cores = _prep_inputs(inputs)
    if FUSED:
        if "F" not in _PROG_CACHE:
            _PROG_CACHE["F"] = build_program("ABCD", fused=True)
        nc, _ = _PROG_CACHE["F"]
        res = run_bass_kernel_spmd(nc, cores, core_ids=list(range(8)))
        outs = [res.results[c]["xo"] for c in range(8)]
    else:
        cur = [m["x_all"][0:TOK, :] for m in cores]
        for ph in "ABCD":
            if ph not in _PROG_CACHE:
                _PROG_CACHE[ph] = build_program(ph, fused=False)
            nc, _ = _PROG_CACHE[ph]
            in_maps = []
            names = set(PHASE_INPUTS[ph]) | set(CONST_NAMES)
            for ci, m in enumerate(cores):
                mm = {k: m[k] for k in names if k in m}
                if "xs" in names:
                    mm["xs"] = np.ascontiguousarray(cur[ci])
                in_maps.append(mm)
            res = run_bass_kernel_spmd(nc, in_maps, core_ids=list(range(8)))
            cur = [res.results[c]["xo"] for c in range(8)]
        outs = cur
    y = np.empty((4, TOKA, D), np.float32)
    for c in range(8):
        bidx, h = c // 2, c % 2
        y[bidx, h * TOK:(h + 1) * TOK, :] = outs[c]
    return y


# version 04
# speedup vs baseline: 1.0413x; 1.0413x over previous
import numpy as np
from contextlib import ExitStack
import concourse.bass as bass
import concourse.mybir as mybir
from concourse.bass_utils import run_bass_kernel_spmd

F32 = mybir.dt.float32
BF16 = mybir.dt.bfloat16
AF = mybir.ActivationFunctionType
ALU = mybir.AluOpType
AXX = mybir.AxisListType.X

ENGS = ("sync", "tensor", "vector", "scalar", "gpsimd")
G8 = [[0, 1, 2, 3, 4, 5, 6, 7]]
TOK = 2048
TOKA = 4096
D = 2048
KC = 16
FF = 5632
FFC = 44
EPS_LN = 1e-5
EPS_Q = 1e-8
ARENA_KB = 205


class Sem:
    __slots__ = ("h", "v", "name")

    def __init__(self, h, name):
        self.h = h
        self.v = 0
        self.name = name


class B:

    def __init__(self, nc, es):
        self.nc = nc
        self.es = es
        self.q = {e: [] for e in ENGS}
        self.nbar = 0
        self.barsem = None
        self.arena_b = es.enter_context(
            nc.sbuf_tensor("arena", [128, ARENA_KB * 512], BF16))
        self.arena_f = self.arena_b.bitcast(F32)
        self.aoff = 0
        self.amax = ARENA_KB * 1024
        self.pb = [es.enter_context(nc.psum_tensor(f"pb{i}", [128, 512], F32))
                   for i in range(8)]
        self.pbb = [p.bitcast(BF16) for p in self.pb]

    def sem(self, name):
        return Sem(self.es.enter_context(self.nc.semaphore(name)), name)

    def mark(self):
        return self.aoff

    def reset(self, m):
        self.aoff = m

    def tile(self, shape, dt):
        p, f = shape
        esz = 2 if dt == BF16 else 4
        off = (self.aoff + 63) & ~63
        self.aoff = off + f * esz
        assert self.aoff <= self.amax, f"arena overflow {self.aoff}"
        base = self.arena_b if dt == BF16 else self.arena_f
        return base[0:p, off // esz:off // esz + f]

    def sb(self, name, shape, dt):
        return self.es.enter_context(self.nc.sbuf_tensor(name, shape, dt))

    def wait(self, eng, sem, val):
        if val <= 0:
            return
        assert val <= sem.v, f"wait {sem.name}>={val} but planned {sem.v} (hang)"
        self.q[eng].append(lambda e, s=sem, v=val: e.wait_ge(s.h, v))

    def dma(self, eng, out, in_, sem):
        sem.v += 16
        self.q[eng].append(
            lambda e, o=out, i=in_, s=sem:
            e.dma_start(out=o, in_=i).then_inc(s.h, 16))
        return sem.v

    def op(self, eng, fn, sem=None):
        if sem is None:
            self.q[eng].append(lambda e, f=fn: f(e))
            self._lastfn = getattr(self, "_lastfn", {})
            self._lastfn[eng] = fn
            return None
        sem.v += 1
        self.q[eng].append(lambda e, f=fn, s=sem: f(e).then_inc(s.h, 1))
        return sem.v

    def inc_last(self, eng, sem):
        fn = self._lastfn.get(eng)
        assert fn is not None, "inc_last without preceding op"
        self.q[eng].pop()
        sem.v += 1
        self.q[eng].append(lambda e, f=fn, s=sem: f(e).then_inc(s.h, 1))
        self._lastfn[eng] = None
        return sem.v

    def coll(self, kind, op, groups, ins, outs, sem):
        sem.v += 1
        self.q["gpsimd"].append(
            lambda e, s=sem: e.collective_compute(
                kind, op, replica_groups=groups, ins=ins,
                outs=outs).then_inc(s.h))
        return sem.v

    def barrier(self):
        if self.barsem is None:
            self.barsem = self.sem("barsem")
        self.nbar += 1
        tgt = 5 * self.nbar
        self.barsem.v = tgt
        for e in ENGS:
            self.q[e].append(lambda x, s=self.barsem: x.nop().then_inc(s.h, 1))
        for e in ENGS:
            self.q[e].append(lambda x, s=self.barsem, v=tgt: x.wait_ge(s.h, v))

    def finalize(self):
        blk = self.es.enter_context(self.nc.Block())
        for name in ENGS:
            qq = self.q[name]

            def make(qq):
                def _(eng):
                    for f in qq:
                        f(eng)
                return _
            getattr(blk, name)(make(qq))


class MK:
    def __init__(self, nc, b, io):
        self.nc = nc
        self.b = b
        self.io = io
        self.c = {}
        self.sm = {n: b.sem(n) for n in
                   ("ld", "ldg", "st", "stg", "pe", "dve", "act", "cc")}

    def s(self, name):
        if name not in self.sm:
            self.sm[name] = self.b.sem(name)
        return self.sm[name]

    def wload(self, dst, c0, c1, dest, stream):
        gf, off, srow, cols = self.wmeta[dst]
        q = srow // 128
        w = c1 - c0
        v = gf[:, off:off + srow * cols].rearrange(
            "r (q p f) -> p q r f", q=q, p=128, f=cols)
        dk = dest.rearrange("p (r q f) -> p q r f", r=8, q=q, f=w)
        lv = None
        for qq in range(q):
            lv = self.b.dma("sync", dk[:, qq, :, :],
                            v[:, qq, :, c0:c1], stream)
        return lv

    def wview(self, dst, c0, c1):
        gf, off, srow, cols = self.wmeta[dst]
        q = srow // 128
        v = gf[:, off:off + srow * cols]
        return v.rearrange("r (q p f) -> p r q f", q=q, p=128,
                           f=cols)[:, :, :, c0:c1]

    def wview_rows(self, dst, rb, c0, c1):
        gf, off, srow, cols = self.wmeta[dst]
        q = srow // 128
        v = gf[rb:rb + 1, off:off + srow * cols]
        return v.rearrange("one (j p f) -> p (one j) f", j=q, p=128,
                           f=cols)[:, :, c0:c1]

    def wait_dma_all(self, eng):
        for name, sem in list(self.sm.items()):
            if name in ("pe", "dve", "act", "cc"):
                continue
            self.b.wait(eng, sem, sem.v)

    def emit_startup(self):
        b = self.b
        ld, dve = self.sm["ld"], self.sm["dve"]
        for nm in ("ident_f", "ident_b", "ones_f", "ones_b"):
            dt = F32 if nm.endswith("_f") else BF16
            shape = [128, 128] if nm.startswith("ident") else [1, 128]
            t = b.sb("sb_" + nm, shape, dt)
            b.dma("sync", t[:, :], self.io[nm][:, :], ld)
            self.c[nm] = t
        eps = b.sb("epst", [128, 1], F32)
        onescol = b.sb("onescol", [128, 1], BF16)
        onescolf = b.sb("onescolf", [128, 1], F32)
        b.op("vector", lambda v: v.memset(eps[:, :], EPS_LN))
        b.op("vector", lambda v: v.memset(onescol[:, :], 1.0))
        b.op("vector", lambda v: v.memset(onescolf[:, :], 1.0), sem=dve)
        self.c["eps"] = eps
        self.c["ones_col_b"] = onescol
        self.c["ones_col_f"] = onescolf
        b.wait("sync", ld, ld.v)
        b.barrier()

    def prep_params(self, chunks=(), bf16rows=(), bcasts=()):
        b = self.b
        ld, dve = self.sm["ld"], self.sm["dve"]
        act, pe = self.sm["act"], self.sm["pe"]
        pv = b.pb[0]
        outs = {}
        for name, n in chunks:
            outs[name] = b.tile([128, max(n // 128, 1)], F32)
        for name, n in bf16rows:
            outs[name] = b.tile([1, n], BF16)
        for name, n in bcasts:
            outs[name] = b.tile([128, n], F32)
        mstage = b.mark()
        stage = b.tile([128, 8192], F32)
        for name, n in chunks:
            nch = max(n // 128, 1)
            out = outs[name]
            if n >= 128:
                src = self.io[name].rearrange("one (c p) -> (one c) p", p=128)
            else:
                src = self.io[name]
            b.wait("vector", pe, pe.v)
            zv = b.op("vector", lambda v: v.memset(
                stage[0:128, 0:128], 0.0), sem=dve)
            b.wait("sync", dve, zv)
            lv = b.dma("sync", stage[0:nch, 0:128], src, ld)
            b.wait("tensor", ld, lv)
            b.wait("tensor", dve, dve.v)
            b.op("tensor", lambda t: t.transpose(
                out=pv[:, 0:128], in_=stage[0:128, 0:128],
                identity=self.c["ident_f"][:, :]), sem=pe)
            b.wait("vector", pe, pe.v)
            b.op("vector", lambda v, o=out, nch=nch: v.tensor_copy(
                out=o[:, 0:nch], in_=pv[:, 0:nch]), sem=dve)
            b.wait("sync", dve, dve.v)
            self.c[name] = out
        for name, n in bf16rows:
            out = outs[name]
            lv = b.dma("sync", stage[0:1, 0:n], self.io[name][:, :], ld)
            b.wait("scalar", ld, lv)
            av = b.op("scalar", lambda s, o=out, n=n: s.activation(
                out=o[0:1, 0:n], in_=stage[0:1, 0:n], func=AF.Copy, bias=0.0),
                sem=act)
            b.wait("sync", act, av)
            self.c[name] = out
        for name, n in bcasts:
            out = outs[name]
            self._prep_ctr = getattr(self, "_prep_ctr", 0) + 1
            scr = self.nc.dram_tensor(f"prep_bc_{self._prep_ctr}", [1, n],
                                      F32, kind="Internal").ap()
            lv = b.dma("sync", stage[0:1, 0:n], self.io[name][:, :], ld)
            b.wait("sync", ld, lv)
            sv = b.dma("sync", scr[:, :], stage[0:1, 0:n], self.sm["st"])
            b.wait("sync", self.sm["st"], sv)
            b.dma("sync", out[:, 0:n], scr.broadcast_to([128, n]), ld)
            b.wait("sync", ld, ld.v)
            self.c[name] = out
        for eng in ("sync", "gpsimd", "tensor", "vector", "scalar"):
            b.wait(eng, dve, dve.v)
            b.wait(eng, act, act.v)
            b.wait(eng, pe, pe.v)
        b.reset(mstage)

    def emit_W(self, weights, quant):
        b, nc = self.b, self.nc
        ld, st, cc = self.sm["ld"], self.sm["st"], self.sm["cc"]
        dve, act, pe = self.sm["dve"], self.sm["act"], self.sm["pe"]
        pv = b.pb[0]
        m0 = b.mark()
        CW = 8192
        stage = [b.tile([128, CW], F32) for _ in range(2)]
        outb = [b.tile([128, CW], BF16) for _ in range(2)]
        sga = b.tile([128, CW], BF16)
        red = b.tile([128, 1], F32)
        redtmp = b.tile([128, 1], F32)
        sums = b.tile([1, 8], F32)
        thr = b.tile([1, 8], F32)
        thr_bc = b.tile([128, 8], F32)
        thr_nbc = b.tile([128, 8], F32)
        dbg = b.tile([128, 4], F32)
        self.dbg_tiles = dbg
        self._w_ctr = getattr(self, "_w_ctr", 0) + 1
        wc = self._w_ctr
        if quant:
            ar_in = nc.dram_tensor(f"w_ar_in{wc}", [1, 8], F32,
                                   kind="Internal").ap()
            ar_out = nc.dram_tensor(f"w_ar_out{wc}", [1, 8], F32,
                                    kind="Internal").ap()
            red_dram = nc.dram_tensor(f"w_red_dram{wc}", [768], F32,
                                      kind="Internal").ap()
            redrow = b.tile([1, 768], F32)
            red2 = b.tile([128, 6], F32)
            stage_cons = [0, 0]
            b.op("vector", lambda v: v.memset(sums[0:1, 0:8], 0.0), sem=dve)
            for qi, (src, dst, rows, cols) in enumerate(quant):
                srow = rows // 8
                ntile = (srow + 127) // 128
                b.op("vector", lambda v, n=ntile: v.memset(
                    red2[:, 0:n], 0.0), sem=dve)
                for ti in range(ntile):
                    r0, r1 = ti * 128, min((ti + 1) * 128, srow)
                    bu = stage[ti % 2]
                    wls = self.s(f"wld{ti % 2}")
                    b.wait("sync", dve, stage_cons[ti % 2])
                    lv = b.dma("sync", bu[0:r1 - r0, 0:cols],
                               self.io[src][r0:r1, :], wls)
                    b.wait("vector", wls, lv)
                    b.op("vector", lambda v, bu=bu, r=r1 - r0, ti=ti,
                         cols=cols: v.reduce_sum(
                             out=red2[0:r, ti:ti + 1], in_=bu[0:r, 0:cols],
                             axis=AXX, apply_absolute_value=True), sem=dve)
                    stage_cons[ti % 2] = dve.v
                b.wait("gpsimd", dve, dve.v)
                b.dma("gpsimd", red_dram[0:128 * ntile],
                      red2[:, 0:ntile], st)
                b.wait("gpsimd", st, st.v)
                rl = b.dma("gpsimd", redrow[0:1, 0:128 * ntile],
                           red_dram[0:128 * ntile], st)
                b.wait("vector", st, rl)
                b.op("vector", lambda v, qi=qi, n=ntile: v.reduce_sum(
                    out=sums[0:1, qi:qi + 1], in_=redrow[0:1, 0:128 * n],
                    axis=AXX), sem=dve)
            b.op("vector", lambda v: v.tensor_copy(
                out=dbg[32:33, 0:3], in_=sums[0:1, 0:3]), sem=dve)
            b.wait("gpsimd", dve, dve.v)
            b.dma("gpsimd", ar_in[:, :], sums[0:1, 0:8], st)
            b.wait("gpsimd", st, st.v)
            b.coll("AllReduce", ALU.add, G8, [ar_in.opt()], [ar_out.opt()], cc)
            b.wait("gpsimd", cc, cc.v)
            b.dma("gpsimd", sums[0:1, 0:8], ar_out[:, :], st)
            b.wait("vector", st, st.v)
            b.op("vector", lambda v: v.tensor_copy(
                out=dbg[64:65, 0:3], in_=sums[0:1, 0:3]), sem=dve)
            for qi, (src, dst, rows, cols) in enumerate(quant):
                b.op("vector",
                     lambda v, qi=qi, n=QNUMEL[quant[qi][1]]: v.tensor_scalar(
                         out=thr[0:1, qi:qi + 1], in0=sums[0:1, qi:qi + 1],
                         scalar1=0.5 / n, scalar2=0.5 * EPS_Q,
                         op0=ALU.mult, op1=ALU.add), sem=dve)
            b.wait("gpsimd", dve, dve.v)
            thr_dram = nc.dram_tensor(f"w_thr_dram{wc}", [1, 8], F32,
                                      kind="Internal").ap()
            b.dma("gpsimd", thr_dram[:, :], thr[0:1, 0:8], st)
            b.wait("gpsimd", st, st.v)
            b.dma("gpsimd", thr_bc[:, 0:8],
                  thr_dram.broadcast_to([128, 8]), st)
            b.wait("vector", st, st.v)
            b.op("vector", lambda v: v.tensor_scalar_mul(
                out=thr_nbc[:, 0:8], in0=thr_bc[:, 0:8], scalar1=-1.0),
                sem=dve)
            b.wait("scalar", dve, dve.v)
            b.wait("sync", dve, dve.v)
            b.wait("gpsimd", dve, dve.v)
        if not hasattr(self, "gins"):
            self.gins = {}
        allw = [(s, d, r, c, False) for (s, d, r, c) in weights] + \
               [(s, d, r, c, True) for (s, d, r, c) in quant]
        rbsize = sum(r // 8 * c for (_, _, r, c, _) in allw)
        gf_in = nc.dram_tensor(f"gfin{wc}", [1, rbsize], BF16,
                               kind="Internal").ap()
        gf_out = nc.dram_tensor(f"gfout{wc}", [8, rbsize], BF16,
                                kind="Internal").ap()
        goff = 0
        if not hasattr(self, "wmeta"):
            self.wmeta = {}
        stv = [0, 0]
        acons = [0, 0]
        sga_cons = 0
        tig = 0
        for src, dst, rows, cols, isq in allw:
            srow = rows // 8
            gin = gf_in[0:1, goff:goff + srow * cols].rearrange(
                "one (s f) -> (one s) f", f=cols)
            self.wmeta[dst] = (gf_out, goff, srow, cols)
            goff += srow * cols
            self.gins[dst] = gin
            if isq:
                qi = [q[0] for q in quant].index(src)
            ntile = (srow + 127) // 128
            for ti in range(ntile):
                r0, r1 = ti * 128, min((ti + 1) * 128, srow)
                rr = r1 - r0
                par = tig % 2
                tig += 1
                bu, ob = stage[par], outb[par]
                wls = self.s(f"wld{par}")
                wss = self.s(f"wst{par}")
                b.wait("sync", wss, stv[par])
                b.wait("sync", act, acons[par])
                lv = b.dma("sync", bu[0:rr, 0:cols], self.io[src][r0:r1, :],
                           wls)
                b.wait("scalar", wls, lv)
                b.wait("scalar", wss, stv[par])
                if isq:
                    b.wait("scalar", dve, sga_cons)
                    b.op("scalar", lambda s, bu=bu, rr=rr, cols=cols,
                         qi=qi: s.activation(
                             out=sga[0:rr, 0:cols], in_=bu[0:rr, 0:cols],
                             func=AF.Sign,
                             bias=thr_nbc[0:rr, qi:qi + 1]), sem=act)
                    av = b.op("scalar", lambda s, bu=bu, ob=ob, rr=rr,
                              cols=cols, qi=qi: s.activation(
                                  out=ob[0:rr, 0:cols], in_=bu[0:rr, 0:cols],
                                  func=AF.Sign,
                                  bias=thr_bc[0:rr, qi:qi + 1]), sem=act)
                    b.wait("vector", act, av)
                    dv = b.op("vector", lambda v, ob=ob, rr=rr, cols=cols:
                              v.tensor_tensor(out=ob[0:rr, 0:cols],
                                              in0=ob[0:rr, 0:cols],
                                              in1=sga[0:rr, 0:cols],
                                              op=ALU.add), sem=dve)
                    b.wait("gpsimd", dve, dv)
                    acons[par] = av
                    sga_cons = dv
                else:
                    av = b.op("scalar", lambda s, bu=bu, ob=ob, rr=rr,
                              cols=cols: s.activation(
                                  out=ob[0:rr, 0:cols], in_=bu[0:rr, 0:cols],
                                  func=AF.Copy, bias=0.0), sem=act)
                    b.wait("gpsimd", act, av)
                    acons[par] = av
                stv[par] = b.dma("gpsimd", gin[r0:r1, :],
                                 ob[0:rr, 0:cols], wss)
        self.wait_dma_all("gpsimd")
        if allw:
            b.coll("AllGather", ALU.bypass, G8, [gf_in.opt()],
                   [gf_out.opt()], cc)
        b.wait("gpsimd", cc, cc.v)
        b.barrier()
        b.reset(m0)

    def emit_ln(self, xin, lnT, ntile, n_add=0, x_add_out=None,
                keep_f32=None, inner=False, gamma_beta=None):
        b = self.b
        ld, st = self.sm["ldg"], self.sm["stg"]
        dve, act, pe = self.sm["dve"], self.sm["act"], self.sm["pe"]
        ncol = ntile * 128
        light = (n_add == 0 and keep_f32 is None and not inner)
        m0 = b.mark()
        xa = [b.tile([128, D], F32) for _ in range(2)]
        lnf = None if light else [b.tile([128, D], F32) for _ in range(2)]
        lnb = [b.tile([128, D], BF16) for _ in range(2)]
        xm = [b.tile([128, D], F32) for _ in range(2)] if n_add else None
        stats = b.tile([128, 24], F32)
        mv = b.tile([128, 2], F32)
        rstd = b.tile([128, 1], F32)
        pT = [b.pbb[6], b.pbb[7]]
        xa_free = [0, 0]
        lnb_free = [0, 0]
        xm_free = [0, 0]
        lnf_free = [0, 0]
        pT_free = [0, 0]
        act_mv_read = 0

        def stats_rstd(src, p):
            for i in range(3):
                b.op("vector", lambda v, p=p, i=i, s=src: v.bn_stats(
                    out=stats[:, i * 6:(i + 1) * 6],
                    in_=s[p][:, i * 512:(i + 1) * 512]))
            sv = b.op("vector", lambda v, p=p, s=src: v.bn_stats(
                out=stats[:, 18:24], in_=s[p][:, 1536:2048]), sem=dve)
            b.wait("vector", dve, sv)
            b.op("vector", lambda v: v.bn_aggr(out=mv[:, :], in_=stats[:, :]),
                 sem=dve)
            v_ = b.op("vector", lambda v: v.engine_nop(), sem=dve)
            b.wait("scalar", dve, v_)
            b.op("scalar", lambda s: s.activation(
                out=rstd[:, :], in_=mv[:, 1:2], func=AF.Ln,
                bias=self.c["eps"][:, :]))
            return b.op("scalar", lambda s: s.activation(
                out=rstd[:, :], in_=rstd[:, :], func=AF.Exp, scale=-0.5),
                sem=act)

        for t in range(ntile):
            p = t % 2
            lxs = self.s(f"lnx{p}")
            lms = self.s(f"lnm{p}")
            lfs = self.s(f"lnf{p}")
            b.wait("gpsimd", dve, xa_free[p])
            if n_add:
                b.wait("gpsimd", lms, xm_free[p])
            lv = b.dma("gpsimd", xa[p][:, :], xin[t * 128:(t + 1) * 128, :],
                       lxs)
            b.wait("vector", lxs, lv)
            b.wait("vector", act, act_mv_read)
            av = stats_rstd(xa, p)
            act_mv_read = av
            b.wait("vector", act, av)
            if keep_f32 is not None:
                b.wait("vector", lfs, lnf_free[p])
            if light:
                b.wait("vector", pe, lnb_free[p])
                b.op("vector", lambda v, p=p: v.tensor_scalar(
                    out=lnb[p][:, :], in0=xa[p][:, :], scalar1=mv[:, 0:1],
                    scalar2=rstd[:, 0:1], op0=ALU.subtract, op1=ALU.mult),
                    sem=dve)
            else:
                b.op("vector", lambda v, p=p: v.tensor_scalar(
                    out=lnf[p][:, :], in0=xa[p][:, :], scalar1=mv[:, 0:1],
                    scalar2=rstd[:, 0:1], op0=ALU.subtract, op1=ALU.mult),
                    sem=dve)
            if t < n_add:
                dv = b.op("vector", lambda v, p=p: v.tensor_tensor(
                    out=xm[p][:, :], in0=xa[p][:, :], in1=lnf[p][:, :],
                    op=ALU.add), sem=dve)
                b.wait("gpsimd", dve, dv)
                xm_free[p] = b.dma("gpsimd",
                                   x_add_out[t * 128:(t + 1) * 128, :],
                                   xm[p][:, :], lms)
            xa_free[p] = dve.v
            if keep_f32 is not None:
                b.wait("gpsimd", dve, dve.v)
                lnf_free[p] = b.dma("gpsimd",
                                    keep_f32[t * 128:(t + 1) * 128, :],
                                    lnf[p][:, :], lfs)
            if inner:
                b.wait("vector", act, act_mv_read)
                av = stats_rstd(lnf, p)
                act_mv_read = av
                b.wait("vector", act, av)
            if not light:
                b.wait("vector", pe, lnb_free[p])
                if inner:
                    b.op("vector", lambda v, p=p: v.tensor_scalar(
                        out=lnb[p][:, :], in0=lnf[p][:, :],
                        scalar1=mv[:, 0:1], scalar2=rstd[:, 0:1],
                        op0=ALU.subtract, op1=ALU.mult), sem=dve)
                else:
                    b.op("vector", lambda v, p=p: v.tensor_copy(
                        out=lnb[p][:, :], in_=lnf[p][:, :]), sem=dve)
            lnb_ready = dve.v
            for g in range(4):
                gp = g % 2
                b.wait("tensor", dve, max(lnb_ready, pT_free[gp]))
                for j in range(4):
                    k = g * 4 + j
                    b.op("tensor", lambda tt, p=p, gp=gp, j=j, k=k:
                         tt.transpose(
                             out=pT[gp][:, j * 128:(j + 1) * 128],
                             in_=lnb[p][:, k * 128:(k + 1) * 128],
                             identity=self.c["ident_b"][:, :]))
                pvv = b.inc_last("tensor", pe)
                b.wait("vector", pe, pvv)
                if gamma_beta is None:
                    b.op("vector", lambda v, gp=gp, g=g, t=t: v.tensor_copy(
                        out=lnT.rearrange("p (k c) -> p k c", k=KC)
                        [:, 4 * g:4 * g + 4, t * 128:(t + 1) * 128],
                        in_=pT[gp][:, 0:512].rearrange(
                            "p (j c) -> p j c", j=4)), sem=dve)
                else:
                    gch, bch = gamma_beta
                    for j in range(4):
                        k = g * 4 + j
                        b.op("vector", lambda v, gp=gp, j=j, k=k, t=t:
                             v.tensor_scalar(
                                 out=lnT[:, k * ncol + t * 128:
                                         k * ncol + (t + 1) * 128],
                                 in0=pT[gp][:, j * 128:(j + 1) * 128],
                                 scalar1=gch[:, k:k + 1],
                                 scalar2=bch[:, k:k + 1],
                                 op0=ALU.mult, op1=ALU.add), sem=dve)
                pT_free[gp] = dve.v
            lnb_free[p] = pe.v
        self.wait_dma_all("gpsimd")
        b.wait("tensor", dve, dve.v)
        b.reset(m0)

    def emit_A(self, x_all, xout):
        b, nc = self.b, self.nc
        ld = self.sm["ld"]
        ldg, stg = self.sm["ldg"], self.sm["stg"]
        dve, act, pe = self.sm["dve"], self.sm["act"], self.sm["pe"]
        xA = nc.dram_tensor("a_xA", [TOK, D], F32, kind="Internal").ap()
        vhbm = nc.dram_tensor("a_v", [16, 32, 128, 256], BF16,
                              kind="Internal").ap()
        gthbm = nc.dram_tensor("a_gt", [4, 32, 128, 512], BF16,
                               kind="Internal").ap()
        mA = b.mark()
        qT = b.tile([128, TOK], BF16)
        kT = b.tile([128, TOKA], BF16)
        self.prep_params(
            chunks=[("b_qk", 128), ("gq", 128), ("bq", 128), ("gk", 128),
                    ("bk", 128), ("b_hidden", 8192)],
            bf16rows=[("b_hidden_bf", 8192), ("b_out0_bf", 2048)])
        m1 = b.mark()
        for th in range(2):
            toff = th * TOK
            ln0T = b.tile([128, KC * TOK], BF16)
            self.emit_ln(x_all[toff:toff + TOK, :], ln0T, TOK // 128,
                         n_add=(TOK // 128 if th == 0 else 0), x_add_out=xA)
            b.barrier()
            qkT = b.tile([128, TOK], F32)
            wqk = b.tile([128, KC * 128], BF16)
            pq = [b.pb[0], b.pb[1]]
            awq = self.s("awq")
            lw = self.wload("W_qk_bf", 0, 128, wqk, awq)
            b.wait("tensor", awq, lw)
            pq_free = [0, 0]
            for tb in range(4):
                bk = tb % 2
                b.wait("tensor", act, pq_free[bk])
                for k in range(KC):
                    b.op("tensor", lambda t, k=k, tb=tb, bk=bk: t.matmul(
                        out=pq[bk][:, :], lhsT=wqk[:, k * 128:(k + 1) * 128],
                        rhs=ln0T[:, k * TOK + tb * 512:
                                 k * TOK + tb * 512 + 512],
                        start=(k == 0), stop=(k == KC - 1)))
                pvv = b.inc_last("tensor", pe)
                b.wait("scalar", pe, pvv)
                pq_free[bk] = b.op("scalar", lambda s, bk=bk, tb=tb:
                                   s.activation(
                                       out=qkT[:, tb * 512:(tb + 1) * 512],
                                       in_=pq[bk][:, :], func=AF.Silu,
                                       bias=self.c["b_qk"][:, 0:1]), sem=act)
            b.wait("vector", act, act.v)
            if th == 0:
                b.op("vector", lambda v: v.tensor_scalar(
                    out=qT[:, :], in0=qkT[:, :], scalar1=self.c["gq"][:, 0:1],
                    scalar2=self.c["bq"][:, 0:1], op0=ALU.mult, op1=ALU.add))
            b.op("vector", lambda v, toff=toff: v.tensor_scalar(
                out=kT[:, toff:toff + TOK], in0=qkT[:, :],
                scalar1=self.c["gk"][:, 0:1], scalar2=self.c["bk"][:, 0:1],
                op0=ALU.mult, op1=ALU.add), sem=dve)
            b.wait("tensor", dve, dve.v)
            whb = b.tile([128, KC * 512], BF16)
            vt = [b.tile([128, 512], BF16) for _ in range(2)]
            w_cons = 0
            pa_free = [0, 0]
            vst = [0, 0]
            awh = self.s("awh")
            avs = [self.s("avs0"), self.s("avs1")]
            for wb in range(16 if th == 0 else 8):
                b.wait("sync", pe, w_cons)
                lw = self.wload("W_hidden_bf", wb * 512, (wb + 1) * 512,
                                whb, awh)
                b.wait("tensor", awh, lw)
                if wb < 8:
                    for t in range(16):
                        tg = th * 16 + t
                        bk = t % 2
                        b.wait("tensor", act, pa_free[bk])
                        for k in range(KC):
                            b.op("tensor", lambda tt, k=k, t=t, bk=bk:
                                 tt.matmul(
                                     out=pq[bk][:, :],
                                     lhsT=ln0T[:, k * TOK + t * 128:
                                               k * TOK + t * 128 + 128],
                                     rhs=whb[:, k * 512:(k + 1) * 512],
                                     start=(k == 0), stop=False))
                        b.op("tensor", lambda tt, wb=wb, bk=bk: tt.matmul(
                            out=pq[bk][:, :], lhsT=self.c["ones_b"][:, :],
                            rhs=self.c["b_hidden_bf"][0:1, wb * 512:
                                                      (wb + 1) * 512],
                            start=False, stop=True))
                        pvv = b.inc_last("tensor", pe)
                        b.wait("scalar", pe, pvv)
                        b.wait("scalar", avs[bk], vst[bk])
                        av = b.op("scalar", lambda s, bk=bk: s.activation(
                            out=vt[bk][:, :], in_=pq[bk][:, :],
                            func=AF.Silu), sem=act)
                        pa_free[bk] = av
                        b.wait("gpsimd", act, av)
                        b.dma("gpsimd", vhbm[2 * wb, tg, :, :],
                              vt[bk][:, 0:256], avs[bk])
                        vst[bk] = b.dma("gpsimd", vhbm[2 * wb + 1, tg, :, :],
                                        vt[bk][:, 256:512], avs[bk])
                else:
                    for gfc in range(4):
                        for tb in range(4):
                            bk = (gfc * 4 + tb) % 2
                            b.wait("tensor", act, pa_free[bk])
                            for k in range(KC):
                                b.op("tensor", lambda tt, k=k, gfc=gfc,
                                     tb=tb, bk=bk: tt.matmul(
                                         out=pq[bk][:, :],
                                         lhsT=whb[:, k * 512 + gfc * 128:
                                                  k * 512 + gfc * 128 + 128],
                                         rhs=ln0T[:, k * TOK + tb * 512:
                                                  k * TOK + tb * 512 + 512],
                                         start=(k == 0),
                                         stop=(k == KC - 1)))
                            pvv = b.inc_last("tensor", pe)
                            b.wait("scalar", pe, pvv)
                            b.wait("scalar", avs[bk], vst[bk])
                            gch = 32 + (wb - 8) * 4 + gfc
                            av = b.op("scalar", lambda s, bk=bk, gch=gch:
                                      s.activation(
                                          out=vt[bk][:, :], in_=pq[bk][:, :],
                                          func=AF.Silu,
                                          bias=self.c["b_hidden"]
                                          [:, gch:gch + 1]), sem=act)
                            pa_free[bk] = av
                            b.wait("gpsimd", act, av)
                            vst[bk] = b.dma(
                                "gpsimd", gthbm[tb, (wb - 8) * 4 + gfc, :, :],
                                vt[bk][:, :], avs[bk])
                w_cons = pe.v
            self.wait_dma_all("gpsimd")
            b.wait("tensor", act, act.v)
            b.barrier()
            b.reset(m1)
        attnT = b.tile([128, 32 * 512], BF16)
        ogT = b.tile([128, 32 * 512], BF16)
        vsec = [b.tile([128, 32 * 256], BF16) for _ in range(2)]
        wob_t = b.tile([128, 32 * 512], BF16)
        wob = [wob_t, wob_t]
        gt2 = [b.tile([128, 2 * 512], BF16) for _ in range(2)]
        simf = [b.tile([128, 512], F32) for _ in range(2)]
        xr2 = [b.tile([128, 512], F32) for _ in range(2)]
        yo2 = [b.tile([128, 512], F32) for _ in range(2)]
        psim = [b.pb[0], b.pb[1]]
        pout = [b.pb[2], b.pb[3], b.pb[4], b.pb[5]]
        pyy = [b.pb[6], b.pb[7]]
        attn_pe_reads = 0
        og_pe_reads = 0
        for g in range(4):
            psim_free = [0, 0]
            simf_free = [0, 0]
            b.wait("scalar", pe, attn_pe_reads)
            for tkc in range(32):
                sp = tkc % 2
                b.wait("tensor", dve, psim_free[sp])
                b.op("tensor", lambda t, tkc=tkc, g=g, sp=sp: t.matmul(
                    out=psim[sp][:, :],
                    lhsT=kT[:, tkc * 128:(tkc + 1) * 128],
                    rhs=qT[:, g * 512:(g + 1) * 512], start=True, stop=True))
                pvv = b.inc_last("tensor", pe)
                b.wait("vector", pe, pvv)
                b.wait("vector", act, simf_free[sp])
                dv = b.op("vector", lambda v, sp=sp: v.tensor_scalar_max(
                    out=simf[sp][:, :], in0=psim[sp][:, :], scalar1=0.0),
                    sem=dve)
                psim_free[sp] = dv
                b.wait("scalar", dve, dv)
                simf_free[sp] = b.op(
                    "scalar", lambda s, sp=sp, tkc=tkc: s.activation(
                        out=attnT[:, tkc * 512:(tkc + 1) * 512],
                        in_=simf[sp][:, :], func=AF.Square,
                        scale=1.0 / TOKA), sem=act)
            attn_ready = act.v
            vcons = [0, 0]
            gcons = [0, 0]
            vld = [0, 0]
            gld = [0, 0]
            po_free = [0, 0, 0, 0]
            b.wait("tensor", act, attn_ready)
            b.wait("vector", pe, og_pe_reads)
            avv = [self.s("av0"), self.s("av1")]
            agg = [self.s("ag0"), self.s("ag1")]
            for s in range(16):
                vp = s % 2
                b.wait("sync", pe, vcons[vp])
                vld[vp] = b.dma("sync",
                                vsec[vp].rearrange("p (t f) -> p t f", t=32),
                                vhbm[s, :, :, :].rearrange("t p f -> p t f"),
                                avv[vp])
                b.wait("gpsimd", dve, gcons[vp])
                b.dma("gpsimd", gt2[vp][:, 0:512], gthbm[g, 2 * s, :, :],
                      agg[vp])
                gld[vp] = b.dma("gpsimd", gt2[vp][:, 512:1024],
                                gthbm[g, 2 * s + 1, :, :], agg[vp])
                b.wait("tensor", avv[vp], vld[vp])
                for vfc in range(2):
                    ob = (2 * s + vfc) % 4
                    b.wait("tensor", dve, po_free[ob])
                    for tkc in range(32):
                        b.op("tensor", lambda t, vfc=vfc, tkc=tkc, ob=ob,
                             vp=vp: t.matmul(
                                 out=pout[ob][:, :],
                                 lhsT=vsec[vp][:, tkc * 256 + vfc * 128:
                                               tkc * 256 + vfc * 128 + 128],
                                 rhs=attnT[:, tkc * 512:(tkc + 1) * 512],
                                 start=(tkc == 0), stop=(tkc == 31)))
                    pvv = b.inc_last("tensor", pe)
                    b.wait("vector", pe, pvv)
                    b.wait("vector", agg[vp], gld[vp])
                    dv = b.op("vector", lambda v, s=s, vfc=vfc, ob=ob, vp=vp:
                              v.tensor_tensor(
                                  out=ogT[:, (2 * s + vfc) * 512:
                                          (2 * s + vfc) * 512 + 512],
                                  in0=pout[ob][:, :],
                                  in1=gt2[vp][:, vfc * 512:(vfc + 1) * 512],
                                  op=ALU.mult), sem=dve)
                    po_free[ob] = dv
                    gcons[vp] = dv
                vcons[vp] = pe.v
            og_ready = dve.v
            attn_pe_reads = pe.v
            wob_cons = [0, 0]
            pyy_free = [0, 0]
            xr_cons = [0, 0]
            yo_st = [0, 0]
            b.wait("tensor", dve, og_ready)
            awo = self.s("awo")
            axs = [self.s("ax0"), self.s("ax1")]
            ays = [self.s("ay0"), self.s("ay1")]
            for dc in range(4):
                wp = 0
                b.wait("sync", pe, wob_cons[wp])
                lwo = self.wload("W_out0_bf", dc * 512, (dc + 1) * 512,
                                 wob[wp], awo)
                b.wait("tensor", awo, lwo)
                for tc in range(4):
                    yk = tc % 2
                    trow = g * 512 + tc * 128
                    b.wait("tensor", dve, pyy_free[yk])
                    for vfc in range(32):
                        b.op("tensor", lambda t, vfc=vfc, tc=tc, yk=yk,
                             wp=wp: t.matmul(
                                 out=pyy[yk][:, :],
                                 lhsT=ogT[:, vfc * 512 + tc * 128:
                                          vfc * 512 + tc * 128 + 128],
                                 rhs=wob[wp][:, vfc * 512:(vfc + 1) * 512],
                                 start=(vfc == 0), stop=False))
                    b.op("tensor", lambda t, dc=dc, yk=yk: t.matmul(
                        out=pyy[yk][:, :], lhsT=self.c["ones_b"][:, :],
                        rhs=self.c["b_out0_bf"][0:1, dc * 512:(dc + 1) * 512],
                        start=False, stop=True))
                    pvv = b.inc_last("tensor", pe)
                    b.wait("gpsimd", dve, xr_cons[yk])
                    xld = b.dma("gpsimd", xr2[yk][:, :],
                                xA[trow:trow + 128, dc * 512:(dc + 1) * 512],
                                axs[yk])
                    b.wait("vector", pe, pvv)
                    b.wait("vector", axs[yk], xld)
                    b.wait("vector", ays[yk], yo_st[yk])
                    dv = b.op("vector", lambda v, yk=yk: v.tensor_tensor(
                        out=yo2[yk][:, :], in0=pyy[yk][:, :],
                        in1=xr2[yk][:, :], op=ALU.add), sem=dve)
                    pyy_free[yk] = dv
                    xr_cons[yk] = dv
                    b.wait("gpsimd", dve, dv)
                    yo_st[yk] = b.dma("gpsimd",
                                      xout[trow:trow + 128,
                                           dc * 512:(dc + 1) * 512],
                                      yo2[yk][:, :], ays[yk])
                wob_cons[wp] = pe.v
            og_pe_reads = pe.v
        self.wait_dma_all("gpsimd")
        b.barrier()
        b.reset(mA)

    def emit_ffn(self, xin, xout, w_up1, w_up2, w_dn, mode):
        b = self.b
        ld = self.sm["ld"]
        ldg, stg = self.sm["ldg"], self.sm["stg"]
        dve, act, pe = self.sm["dve"], self.sm["act"], self.sm["pe"]
        m0 = b.mark()
        if mode == "L1":
            self.prep_params(chunks=[("gate_s", FF), ("up_s", FF)],
                             bcasts=[("down_s", D)])
        HT = 1024
        lnT = b.tile([128, KC * HT], BF16)
        hdnT = b.tile([128, FFC * HT], BF16)
        sgt = [b.tile([128, 512], F32) for _ in range(2)]
        xr = [b.tile([128, 512], F32) for _ in range(2)]
        yo = [b.tile([128, 512], F32) for _ in range(2)]
        pu = [b.pb[0], b.pb[1], b.pb[2], b.pb[3]]
        py = b.pb
        for half in range(2):
            toff = half * HT
            mh = b.mark()
            self.emit_ln(xin[toff:toff + HT, :], lnT, HT // 128)
            b.barrier()
            w1b = [b.tile([128, KC * 256], BF16) for _ in range(2)]
            w2b = [b.tile([128, KC * 256], BF16) for _ in range(2)]
            w3b = [b.tile([128, 6 * 512], BF16) for _ in range(2)]
            w_free = 0
            pu_free = [0, 0, 0, 0]
            sgt_free = [0, 0]
            fwu = [self.s("fwu0"), self.s("fwu1")]
            w_free2 = [0, 0]
            for hb2 in range(22):
                wp2 = hb2 % 2
                b.wait("sync", pe, w_free2[wp2])
                self.wload(w_up1, hb2 * 256, (hb2 + 1) * 256, w1b[wp2],
                           fwu[wp2])
                l2 = self.wload(w_up2, hb2 * 256, (hb2 + 1) * 256, w2b[wp2],
                                fwu[wp2])
                b.wait("tensor", fwu[wp2], l2)
                for hc in range(2):
                    hcg = hb2 * 2 + hc
                    for tb in range(2):
                        bk = (hc * 2 + tb) % 2
                        b.wait("tensor", act, pu_free[bk])
                        b.wait("tensor", dve, pu_free[2 + bk])
                        for wb_, po in ((w1b[wp2], 0), (w2b[wp2], 2)):
                            for k in range(KC):
                                b.op("tensor", lambda t, k=k, hc=hc, tb=tb,
                                     bk=bk, wb_=wb_, po=po: t.matmul(
                                         out=pu[po + bk][:, :],
                                         lhsT=wb_[:, k * 256 + hc * 128:
                                                  k * 256 + hc * 128 + 128],
                                         rhs=lnT[:, k * HT + tb * 512:
                                                 k * HT + tb * 512 + 512],
                                         start=(k == 0),
                                         stop=(k == KC - 1)))
                        pvv = b.inc_last("tensor", pe)
                        b.wait("scalar", pe, pvv)
                        b.wait("scalar", dve, sgt_free[bk])
                        if mode == "L0":
                            av = b.op("scalar", lambda s, bk=bk: s.activation(
                                out=sgt[bk][:, :], in_=pu[bk][:, :],
                                func=AF.Sigmoid, scale=0.5), sem=act)
                        else:
                            av = b.op("scalar", lambda s, bk=bk, hcg=hcg:
                                      s.activation(
                                          out=sgt[bk][:, :],
                                          in_=pu[bk][:, :], func=AF.Silu,
                                          scale=self.c["gate_s"]
                                          [:, hcg:hcg + 1]), sem=act)
                        pu_free[bk] = av
                        b.wait("vector", act, av)
                        if mode == "L0":
                            dv = b.op("vector", lambda v, bk=bk, hcg=hcg,
                                      tb=tb: v.tensor_tensor(
                                          out=hdnT[:, hcg * HT + tb * 512:
                                                   hcg * HT + tb * 512 + 512],
                                          in0=sgt[bk][:, :],
                                          in1=pu[2 + bk][:, :],
                                          op=ALU.mult), sem=dve)
                        else:
                            dv = b.op("vector", lambda v, bk=bk, hcg=hcg,
                                      tb=tb: v.scalar_tensor_tensor(
                                          out=hdnT[:, hcg * HT + tb * 512:
                                                   hcg * HT + tb * 512 + 512],
                                          in0=pu[2 + bk][:, :],
                                          scalar=self.c["up_s"]
                                          [:, hcg:hcg + 1],
                                          in1=sgt[bk][:, :],
                                          op0=ALU.mult, op1=ALU.mult),
                                      sem=dve)
                        pu_free[2 + bk] = dv
                        sgt_free[bk] = dv
                w_free2[wp2] = pe.v
            b.wait("tensor", act, act.v)
            b.wait("tensor", dve, dve.v)
            xr_cons = [0, 0]
            yo_st = [0, 0]
            py_all_free = 0
            fw3 = [self.s("fw30"), self.s("fw31")]
            fxs = [self.s("fx0"), self.s("fx1")]
            fys = [self.s("fy0"), self.s("fy1")]
            w3_cons = [0, 0]
            for dc in range(4):
                for rb in range(8):
                    wp = rb % 2
                    nj = 6 if rb < 7 else 2
                    b.wait("sync", pe, w3_cons[wp])
                    l3 = b.dma("sync",
                               w3b[wp].rearrange("p (b f) -> p b f", b=6),
                               self.wview_rows(w_dn, rb, dc * 512,
                                               (dc + 1) * 512), fw3[wp])
                    b.wait("tensor", fw3[wp], l3)
                    if rb == 0:
                        b.wait("tensor", dve, py_all_free)
                    for tc in range(8):
                        for hc in range(nj):
                            hcg = rb * 6 + hc
                            b.op("tensor", lambda t, hcg=hcg, hc=hc, tc=tc,
                                 wp=wp: t.matmul(
                                     out=py[tc][:, :],
                                     lhsT=hdnT[:, hcg * HT + tc * 128:
                                               hcg * HT + tc * 128 + 128],
                                     rhs=w3b[wp][:, hc * 512:(hc + 1) * 512],
                                     start=(hcg == 0),
                                     stop=(hcg == FFC - 1)))
                    w3_cons[wp] = b.inc_last("tensor", pe)
                pvv = pe.v
                for tc in range(8):
                    bk = tc % 2
                    trow = toff + tc * 128
                    b.wait("gpsimd", dve, xr_cons[bk])
                    xld = b.dma("gpsimd", xr[bk][:, :],
                                xin[trow:trow + 128,
                                    dc * 512:(dc + 1) * 512], fxs[bk])
                    b.wait("vector", pe, pvv)
                    b.wait("vector", fxs[bk], xld)
                    b.wait("vector", fys[bk], yo_st[bk])
                    if mode == "L0":
                        dv = b.op("vector", lambda v, bk=bk, tc=tc:
                                  v.scalar_tensor_tensor(
                                      out=yo[bk][:, :], in0=py[tc][:, :],
                                      scalar=0.25, in1=xr[bk][:, :],
                                      op0=ALU.mult, op1=ALU.add), sem=dve)
                    else:
                        b.op("vector", lambda v, bk=bk, tc=tc, dc=dc:
                             v.tensor_tensor(
                                 out=yo[bk][:, :], in0=py[tc][:, :],
                                 in1=self.c["down_s"][:, dc * 512:
                                                      (dc + 1) * 512],
                                 op=ALU.mult))
                        dv = b.op("vector", lambda v, bk=bk:
                                  v.tensor_tensor(
                                      out=yo[bk][:, :], in0=yo[bk][:, :],
                                      in1=xr[bk][:, :], op=ALU.add), sem=dve)
                    xr_cons[bk] = dv
                    b.wait("gpsimd", dve, dv)
                    yo_st[bk] = b.dma("gpsimd",
                                      xout[trow:trow + 128,
                                           dc * 512:(dc + 1) * 512],
                                      yo[bk][:, :], fys[bk])
                py_all_free = dve.v
            b.wait("tensor", dve, dve.v)
            self.wait_dma_all("gpsimd")
            b.barrier()
            b.reset(mh)
        b.barrier()
        b.reset(m0)

    def emit_C(self, xin, xout):
        b, nc = self.b, self.nc
        ld = self.sm["ld"]
        ldg, stg = self.sm["ldg"], self.sm["stg"]
        dve, act, pe = self.sm["dve"], self.sm["act"], self.sm["pe"]
        cc = self.sm["cc"]
        Xbuf = nc.dram_tensor("c_X", [TOK, D], F32, kind="Internal").ap()
        Kbuf = nc.dram_tensor("c_K", [TOK, D], F32, kind="Internal").ap()
        Vbuf = nc.dram_tensor("c_V", [TOK, D], F32, kind="Internal").ap()
        ag_in = nc.dram_tensor("c_agin", [1, D], F32, kind="Internal").ap()
        ag_out = nc.dram_tensor("c_agout", [8, D], F32, kind="Internal").ap()
        asc = nc.dram_tensor("c_asc", [1, D], F32, kind="Internal").ap()
        mC = b.mark()
        QT = b.tile([128, KC * TOK], BF16)
        Achunks = b.tile([128, 16], F32)
        self.prep_params(chunks=[("g1", D), ("be1", D), ("b_in", 6144)],
                         bf16rows=[("b_in_bf", 6144), ("b_out1_bf", 2048)])
        m1 = b.mark()
        XnT = b.tile([128, KC * TOK], BF16)
        m2 = b.mark()
        self.emit_ln(xin, XnT, TOK // 128, keep_f32=Xbuf, inner=True,
                     gamma_beta=(self.c["g1"], self.c["be1"]))
        b.barrier()
        winb = b.tile([128, KC * 512], BF16)
        kv = [b.tile([128, 512], F32) for _ in range(2)]
        cst = [b.tile([128, 1024], F32) for _ in range(2)]
        rh = b.tile([128, 512], F32)
        tmp1 = b.tile([128, 512], F32)
        pk = [b.pb[0], b.pb[1]]
        pA = [b.pb[2], b.pb[3], b.pb[4], b.pb[5]]
        for which in range(2):
            coloff = D * (1 + which)
            dst = Kbuf if which == 0 else Vbuf
            w_cons = 0
            cwi = self.s("cwi")
            ccs = [self.s("ccs0"), self.s("ccs1")]
            cks = [self.s("cks0"), self.s("cks1")]
            for kcb in range(4):
                b.wait("sync", pe, w_cons)
                lw = self.wload("W_in_bf", coloff + kcb * 512,
                                coloff + kcb * 512 + 512, winb, cwi)
                b.wait("tensor", cwi, lw)
                pk_free = [0, 0]
                cs_ld = [0, 0]
                cs_cons = [0, 0]
                kv_st = [0, 0]
                for tc in range(16):
                    bk = tc % 2
                    if kcb < 2:
                        b.wait("gpsimd", dve, cs_cons[bk])
                        b.dma("gpsimd", cst[bk][:, 0:512],
                              self.io["cosT"][tc * 128:(tc + 1) * 128,
                                              kcb * 512:(kcb + 1) * 512],
                              ccs[bk])
                        cs_ld[bk] = b.dma(
                            "gpsimd", cst[bk][:, 512:1024],
                            self.io["sinT"][tc * 128:(tc + 1) * 128,
                                            kcb * 512:(kcb + 1) * 512],
                            ccs[bk])
                    b.wait("tensor", dve, pk_free[bk])
                    for k in range(KC):
                        b.op("tensor", lambda t, k=k, tc=tc, kcb=kcb, bk=bk:
                             t.matmul(
                                 out=pk[bk][:, :],
                                 lhsT=XnT[:, k * TOK + tc * 128:
                                          k * TOK + tc * 128 + 128],
                                 rhs=winb[:, k * 512:(k + 1) * 512],
                                 start=(k == 0), stop=False))
                    b.op("tensor", lambda t, kcb=kcb, coloff=coloff, bk=bk:
                         t.matmul(
                             out=pk[bk][:, :], lhsT=self.c["ones_b"][:, :],
                             rhs=self.c["b_in_bf"][0:1, coloff + kcb * 512:
                                                   coloff + kcb * 512 + 512],
                             start=False, stop=True))
                    pvv = b.inc_last("tensor", pe)
                    b.wait("vector", pe, pvv)
                    b.wait("vector", cks[bk], kv_st[bk])
                    if kcb < 2:
                        b.wait("vector", ccs[bk], cs_ld[bk])
                        b.op("vector", lambda v, bk=bk: v.tensor_scalar_mul(
                            out=rh.rearrange("p (c two) -> p c two", two=2)
                            [:, :, 0:1],
                            in0=pk[bk][:, 0:512].rearrange(
                                "p (c two) -> p c two", two=2)[:, :, 1:2],
                            scalar1=-1.0))
                        b.op("vector", lambda v, bk=bk: v.tensor_copy(
                            out=rh.rearrange("p (c two) -> p c two", two=2)
                            [:, :, 1:2],
                            in_=pk[bk][:, 0:512].rearrange(
                                "p (c two) -> p c two", two=2)[:, :, 0:1]))
                        b.op("vector", lambda v, bk=bk: v.tensor_tensor(
                            out=tmp1[:, :], in0=pk[bk][:, :],
                            in1=cst[bk][:, 0:512], op=ALU.mult))
                        b.op("vector", lambda v, bk=bk: v.tensor_tensor(
                            out=rh[:, :], in0=rh[:, :],
                            in1=cst[bk][:, 512:1024], op=ALU.mult))
                        dv = b.op("vector", lambda v, bk=bk: v.tensor_tensor(
                            out=kv[bk][:, :], in0=tmp1[:, :], in1=rh[:, :],
                            op=ALU.add), sem=dve)
                        cs_cons[bk] = dv
                    else:
                        dv = b.op("vector", lambda v, bk=bk: v.tensor_copy(
                            out=kv[bk][:, :], in_=pk[bk][:, :]), sem=dve)
                    pk_free[bk] = dv
                    b.wait("gpsimd", dve, dv)
                    kv_st[bk] = b.dma("gpsimd",
                                      dst[tc * 128:(tc + 1) * 128,
                                          kcb * 512:(kcb + 1) * 512],
                                      kv[bk][:, :], cks[bk])
                w_cons = pe.v
        self.wait_dma_all("gpsimd")
        b.barrier()
        b.reset(m2)
        Kf = b.tile([128, D], F32)
        Vf = b.tile([128, D], F32)
        ssq = b.tile([128, 2], F32)
        rn = b.tile([128, 2], F32)
        Kc2 = b.tile([128, D], BF16)
        Pc = b.tile([128, D], BF16)
        dump = Pc
        kv_cons = [0, 0]
        pe_P = 0
        ckv = self.s("ckv")
        for tc in range(16):
            b.wait("gpsimd", act, kv_cons[0])
            b.wait("gpsimd", dve, kv_cons[1])
            b.dma("gpsimd", Kf[:, :], Kbuf[tc * 128:(tc + 1) * 128, :], ckv)
            kld = b.dma("gpsimd", Vf[:, :],
                        Vbuf[tc * 128:(tc + 1) * 128, :], ckv)
            b.wait("scalar", ckv, kld)
            b.wait("scalar", pe, pe_P)
            b.op("scalar", lambda s: s.activation(
                out=dump[:, :], in_=Kf[:, :], func=AF.Square,
                accum_out=ssq[:, 0:1]))
            sv = b.op("scalar", lambda s: s.activation(
                out=dump[:, :], in_=Vf[:, :], func=AF.Square,
                accum_out=ssq[:, 1:2]), sem=act)
            b.wait("scalar", act, sv)
            b.op("scalar", lambda s: s.activation(
                out=rn[:, :], in_=ssq[:, :], func=AF.Ln))
            av = b.op("scalar", lambda s: s.activation(
                out=rn[:, :], in_=rn[:, :], func=AF.Exp, scale=-0.5), sem=act)
            kv_cons[0] = av
            b.wait("vector", act, av)
            b.wait("vector", pe, pe_P)
            b.op("vector", lambda v: v.tensor_scalar_mul(
                out=Kc2[:, :], in0=Kf[:, :], scalar1=rn[:, 0:1]))
            b.op("vector", lambda v: v.tensor_scalar_mul(
                out=Pc[:, :], in0=Vf[:, :], scalar1=rn[:, 1:2]))
            dv = b.op("vector", lambda v: v.tensor_tensor(
                out=Pc[:, :], in0=Pc[:, :], in1=Kc2[:, :], op=ALU.mult),
                sem=dve)
            kv_cons[1] = dv
            b.wait("tensor", dve, dv)
            for j in range(4):
                b.op("tensor", lambda t, j=j, tc=tc: t.matmul(
                    out=pA[j][0:1, :], lhsT=self.c["ones_col_b"][:, 0:1],
                    rhs=Pc[:, j * 512:(j + 1) * 512],
                    start=(tc == 0), stop=(tc == 15)))
            pe_P = b.inc_last("tensor", pe)
        Asb = b.tile([1, D], F32)
        agsb = b.tile([8, D], F32)
        shtmp = b.tile([4, D], F32)
        Astg = b.tile([128, 128], F32)
        masks = b.tile([8, 1], F32)
        lm = b.dma("sync", masks[0:8, 0:1], self.io["pairmask"][:, :], ld)
        b.wait("vector", pe, pe.v)
        for j in range(4):
            b.op("vector", lambda v, j=j: v.tensor_copy(
                out=Asb[0:1, j * 512:(j + 1) * 512], in_=pA[j][0:1, :]),
                sem=dve)
        b.wait("gpsimd", dve, dve.v)
        b.dma("gpsimd", ag_in[:, :], Asb[0:1, :], stg)
        b.wait("gpsimd", stg, stg.v)
        b.coll("AllGather", ALU.bypass, G8, [ag_in.opt()], [ag_out.opt()], cc)
        b.wait("gpsimd", cc, cc.v)
        gl = b.dma("gpsimd", agsb[0:8, :], ag_out[:, :], ldg)
        b.wait("vector", ldg, gl)
        b.wait("vector", ld, lm)
        b.op("vector", lambda v: v.tensor_scalar_mul(
            out=agsb[0:8, :], in0=agsb[0:8, :], scalar1=masks[0:8, 0:1]),
            sem=dve)
        for half in (4, 2, 1):
            b.wait("gpsimd", dve, dve.v)
            sh = b.dma("gpsimd", shtmp[0:half, :],
                       agsb[half:2 * half, :], ldg)
            b.wait("vector", ldg, sh)
            b.op("vector", lambda v, half=half: v.tensor_tensor(
                out=agsb[0:half, :], in0=agsb[0:half, :],
                in1=shtmp[0:half, :], op=ALU.add), sem=dve)
        b.op("vector", lambda v: v.tensor_scalar(
            out=Asb[0:1, :], in0=agsb[0:1, :],
            scalar1=0.5, scalar2=0.5, op0=ALU.mult, op1=ALU.add), sem=dve)
        b.wait("gpsimd", dve, dve.v)
        b.dma("gpsimd", asc[:, :], Asb[0:1, :], stg)
        b.wait("gpsimd", stg, stg.v)
        b.op("vector", lambda v: v.memset(Astg[0:128, 0:128], 0.0), sem=dve)
        b.wait("gpsimd", dve, dve.v)
        al = b.dma("gpsimd", Astg[0:16, 0:128],
                   asc.rearrange("one (c j) -> (one c) j", j=128), ldg)
        b.wait("tensor", ldg, al)
        b.wait("tensor", dve, dve.v)
        b.op("tensor", lambda t: t.transpose(
            out=pk[0][:, 0:128], in_=Astg[0:128, 0:128],
            identity=self.c["ident_f"][:, :]), sem=pe)
        b.wait("vector", pe, pe.v)
        b.op("vector", lambda v: v.tensor_copy(
            out=Achunks[:, :], in_=pk[0][:, 0:16]), sem=dve)
        b.wait("tensor", dve, dve.v)
        b.barrier()
        b.reset(m2)
        winb = b.tile([128, KC * 512], BF16)
        cwi = self.s("cwi")
        wq_cons = 0
        pq_free = [0, 0]
        for dcq in range(4):
            b.wait("sync", pe, wq_cons)
            lw = self.wload("W_in_bf", dcq * 512, (dcq + 1) * 512,
                            winb, cwi)
            b.wait("tensor", cwi, lw)
            for qcc in range(4):
                qc = dcq * 4 + qcc
                for tb in range(4):
                    bk = (qcc * 4 + tb) % 2
                    b.wait("tensor", dve, pq_free[bk])
                    for k in range(KC):
                        b.op("tensor", lambda t, k=k, qcc=qcc, tb=tb, bk=bk:
                             t.matmul(
                                 out=pk[bk][:, :],
                                 lhsT=winb[:, k * 512 + qcc * 128:
                                           k * 512 + qcc * 128 + 128],
                                 rhs=XnT[:, k * TOK + tb * 512:
                                         k * TOK + tb * 512 + 512],
                                 start=(k == 0), stop=(k == KC - 1)))
                    pvv = b.inc_last("tensor", pe)
                    b.wait("vector", pe, pvv)
                    dv = b.op("vector", lambda v, qc=qc, tb=tb, bk=bk:
                              v.tensor_scalar(
                                  out=QT[:, qc * TOK + tb * 512:
                                         qc * TOK + tb * 512 + 512],
                                  in0=pk[bk][:, :],
                                  scalar1=self.c["b_in"][:, qc:qc + 1],
                                  scalar2=Achunks[:, qc:qc + 1],
                                  op0=ALU.add, op1=ALU.mult), sem=dve)
                    pq_free[bk] = dv
            wq_cons = pe.v
        b.wait("tensor", dve, dve.v)
        self.wait_dma_all("gpsimd")
        b.barrier()
        b.reset(m1)
        wout = b.tile([128, KC * D], BF16)
        xr = [b.tile([128, 512], F32) for _ in range(2)]
        Xr = [b.tile([128, 512], F32) for _ in range(2)]
        yo = [b.tile([128, 512], F32) for _ in range(2)]
        pyy = [b.pb[0], b.pb[1]]
        cwo = self.s("cwo")
        cxs = [self.s("cx0"), self.s("cx1")]
        cys = [self.s("cy0"), self.s("cy1")]
        lw = self.wload("W_out1_bf", 0, D, wout, cwo)
        b.wait("tensor", cwo, lw)
        pyy_free = [0, 0]
        xr_cons = [0, 0]
        yo_st = [0, 0]
        for tc in range(16):
            for dc in range(4):
                yk = (tc * 4 + dc) % 2
                trow = tc * 128
                b.wait("tensor", dve, pyy_free[yk])
                for qc in range(KC):
                    b.op("tensor", lambda t, qc=qc, tc=tc, dc=dc, yk=yk:
                         t.matmul(
                             out=pyy[yk][:, :],
                             lhsT=QT[:, qc * TOK + tc * 128:
                                     qc * TOK + tc * 128 + 128],
                             rhs=wout[:, qc * D + dc * 512:
                                      qc * D + dc * 512 + 512],
                             start=(qc == 0), stop=False))
                b.op("tensor", lambda t, dc=dc, yk=yk: t.matmul(
                    out=pyy[yk][:, :], lhsT=self.c["ones_b"][:, :],
                    rhs=self.c["b_out1_bf"][0:1, dc * 512:(dc + 1) * 512],
                    start=False, stop=True))
                pvv = b.inc_last("tensor", pe)
                b.wait("gpsimd", dve, xr_cons[yk])
                b.dma("gpsimd", xr[yk][:, :],
                      xin[trow:trow + 128, dc * 512:(dc + 1) * 512],
                      cxs[yk])
                xld = b.dma("gpsimd", Xr[yk][:, :],
                            Xbuf[trow:trow + 128, dc * 512:(dc + 1) * 512],
                            cxs[yk])
                b.wait("vector", pe, pvv)
                b.wait("vector", cxs[yk], xld)
                b.wait("vector", cys[yk], yo_st[yk])
                b.op("vector", lambda v, yk=yk: v.tensor_tensor(
                    out=yo[yk][:, :], in0=pyy[yk][:, :], in1=Xr[yk][:, :],
                    op=ALU.add))
                dv = b.op("vector", lambda v, yk=yk: v.tensor_tensor(
                    out=yo[yk][:, :], in0=yo[yk][:, :], in1=xr[yk][:, :],
                    op=ALU.add), sem=dve)
                pyy_free[yk] = dv
                xr_cons[yk] = dv
                b.wait("gpsimd", dve, dv)
                yo_st[yk] = b.dma("gpsimd",
                                  xout[trow:trow + 128,
                                       dc * 512:(dc + 1) * 512],
                                  yo[yk][:, :], cys[yk])
        self.wait_dma_all("gpsimd")
        b.barrier()
        b.reset(mC)



QNUMEL = {"w1qT": FF * D, "w2qT": FF * D, "w3qT": FF * D}

WSPEC = {
    "A": dict(weights=[("W_hidden_sl", "W_hidden_bf", D, 8192),
                       ("W_qk_sl", "W_qk_bf", D, 128),
                       ("W_out0_sl", "W_out0_bf", 4096, D)], quant=[]),
    "B": dict(weights=[], quant=[("w1T_sl", "w1qT", D, FF),
                                 ("w2T_sl", "w2qT", D, FF),
                                 ("w3T_sl", "w3qT", 6144, D)]),
    "C": dict(weights=[("W_in_sl", "W_in_bf", D, 3 * D),
                       ("W_out1_sl", "W_out1_bf", D, D)], quant=[]),
    "D": dict(weights=[("gwT_sl", "gwT_bf", D, FF),
                       ("uwT_sl", "uwT_bf", D, FF),
                       ("dwT_sl", "dwT_bf", 6144, D)], quant=[]),
}

CONST_NAMES = ("ident_f", "ident_b", "ones_f", "ones_b")

PHASE_INPUTS = {
    "A": ["x_all", "W_hidden_sl", "W_qk_sl", "W_out0_sl",
          "b_qk", "gq", "bq", "gk", "bk", "b_hidden", "b_hidden_bf",
          "b_out0_bf"],
    "B": ["xs", "w1T_sl", "w2T_sl", "w3T_sl"],
    "C": ["xs", "W_in_sl", "W_out1_sl", "b_in", "b_in_bf", "b_out1_bf",
          "g1", "be1", "cosT", "sinT", "pairmask"],
    "D": ["xs", "gwT_sl", "uwT_sl", "dwT_sl", "gate_s", "up_s", "down_s"],
}

IN_SHAPES = {
    "x_all": ([TOKA, D], F32), "xs": ([TOK, D], F32),
    "W_hidden_sl": ([D // 8, 8192], F32), "W_qk_sl": ([D // 8, 128], F32),
    "W_out0_sl": ([4096 // 8, D], F32),
    "w1T_sl": ([D // 8, FF], F32), "w2T_sl": ([D // 8, FF], F32),
    "w3T_sl": ([768, D], F32),
    "W_in_sl": ([D // 8, 3 * D], F32), "W_out1_sl": ([D // 8, D], F32),
    "gwT_sl": ([D // 8, FF], F32), "uwT_sl": ([D // 8, FF], F32),
    "dwT_sl": ([768, D], F32),
    "b_qk": ([1, 128], F32), "gq": ([1, 128], F32), "bq": ([1, 128], F32),
    "gk": ([1, 128], F32), "bk": ([1, 128], F32),
    "b_hidden": ([1, 8192], F32), "b_hidden_bf": ([1, 8192], F32),
    "b_out0_bf": ([1, D], F32),
    "b_in": ([1, 3 * D], F32), "b_in_bf": ([1, 3 * D], F32),
    "b_out1_bf": ([1, D], F32),
    "g1": ([1, D], F32), "be1": ([1, D], F32),
    "gate_s": ([1, FF], F32), "up_s": ([1, FF], F32),
    "down_s": ([1, D], F32),
    "cosT": ([TOK, 1024], F32), "sinT": ([TOK, 1024], F32),
    "pairmask": ([8, 1], F32),
}

WDST_SHAPES = {
    "W_hidden_bf": [D, 8192], "W_qk_bf": [D, 128], "W_out0_bf": [4096, D],
    "w1qT": [D, FF], "w2qT": [D, FF], "w3qT": [6144, D],
    "W_in_bf": [D, 3 * D], "W_out1_bf": [D, D],
    "gwT_bf": [D, FF], "uwT_bf": [D, FF], "dwT_bf": [6144, D],
}


def build_program(phases, fused):
    nc = bass.Bass("TRN2", target_bir_lowering=False, debug=False)
    es = ExitStack()
    io = {}
    for nm in CONST_NAMES:
        dt = F32 if nm.endswith("_f") else BF16
        shape = [128, 128] if nm.startswith("ident") else [1, 128]
        io[nm] = nc.dram_tensor(nm, shape, dt, kind="ExternalInput").ap()
    names = set()
    for ph in phases:
        names.update(PHASE_INPUTS[ph])
    if fused and phases[0] == "A":
        names.discard("xs")
    for nm in sorted(names):
        shape, dt = IN_SHAPES[nm]
        io[nm] = nc.dram_tensor(nm, shape, dt, kind="ExternalInput").ap()

    if fused:
        out = nc.dram_tensor("xo", [TOK, D], F32, kind="ExternalOutput").ap()
        chain = {}
        prev = io.get("xs")
        for i, ph in enumerate(phases):
            if i == len(phases) - 1:
                nxt = out
            else:
                nxt = nc.dram_tensor(f"xbuf{i}", [TOK, D], F32,
                                     kind="Internal").ap()
            chain[ph] = (prev, nxt)
            prev = nxt
    else:
        out = nc.dram_tensor("xo", [TOK, D], F32, kind="ExternalOutput").ap()
        chain = {ph: (io.get("xs"), out) for ph in phases}
    b = B(nc, es)
    mk = MK(nc, b, io)
    mk.emit_startup()
    for ph in phases:
        spec = WSPEC[ph]
        mk.emit_W(spec["weights"], spec["quant"])
        xin, xout = chain[ph]
        if ph == "A":
            mk.emit_A(io["x_all"], xout)
        elif ph == "B":
            mk.emit_ffn(xin, xout, "w1qT", "w2qT", "w3qT", "L0")
        elif ph == "C":
            mk.emit_C(xin, xout)
        elif ph == "D":
            mk.emit_ffn(xin, xout, "gwT_bf", "uwT_bf", "dwT_bf", "L1")
    mk.wait_dma_all("gpsimd")
    b.finalize()
    return nc, es


_PROG_CACHE = {}


def _consts():
    import ml_dtypes
    return {
        "ident_f": np.eye(128, dtype=np.float32),
        "ident_b": np.eye(128, dtype=ml_dtypes.bfloat16),
        "ones_f": np.ones((1, 128), np.float32),
        "ones_b": np.ones((1, 128), ml_dtypes.bfloat16),
    }


def _rotary_tables(pos0, n, rot_dim=1024, theta=10000.0):
    inv = 1.0 / theta ** (np.arange(0, rot_dim, 2, dtype=np.float64) / rot_dim)
    f = (np.arange(pos0, pos0 + n, dtype=np.float64)[:, None] * inv[None, :])
    f = np.repeat(f, 2, axis=1)
    return (np.cos(f).astype(np.float32), np.sin(f).astype(np.float32))


def _prep_inputs(inputs):
    f = lambda a: np.ascontiguousarray(np.asarray(a), dtype=np.float32)
    x = f(inputs["L0_attn_x"])
    full = {
        "W_hidden": f(inputs["L0_attn_W_hidden"]),
        "W_qk": f(inputs["L0_attn_W_qk"]),
        "W_out0": f(inputs["L0_attn_W_out"]),
        "w1T": f(np.asarray(inputs["L0_ffn_w1"]).T),
        "w2T": f(np.asarray(inputs["L0_ffn_w2"]).T),
        "w3T": np.concatenate([f(np.asarray(inputs["L0_ffn_w3"]).T),
                               np.zeros((512, D), np.float32)], 0),
        "W_in": f(inputs["L1_attn_W_in"]),
        "W_out1": f(inputs["L1_attn_W_out"]),
        "gwT": f(np.asarray(inputs["L1_ffn_gate_w"]).T),
        "uwT": f(np.asarray(inputs["L1_ffn_up_w"]).T),
        "dwT": np.concatenate([f(np.asarray(inputs["L1_ffn_down_w"]).T),
                               np.zeros((512, D), np.float32)], 0),
    }
    row = lambda a: f(a).reshape(1, -1)
    vecs = {
        "b_qk": row(inputs["L0_attn_b_qk"]),
        "gq": row(inputs["L0_attn_gamma_q"]),
        "bq": row(inputs["L0_attn_beta_q"]),
        "gk": row(inputs["L0_attn_gamma_k"]),
        "bk": row(inputs["L0_attn_beta_k"]),
        "b_hidden": row(inputs["L0_attn_b_hidden"]),
        "b_hidden_bf": row(inputs["L0_attn_b_hidden"]),
        "b_out0_bf": row(inputs["L0_attn_b_out"]),
        "b_in": row(inputs["L1_attn_b_in"]),
        "b_in_bf": row(inputs["L1_attn_b_in"]),
        "b_out1_bf": row(inputs["L1_attn_b_out"]),
        "g1": row(inputs["L1_attn_gamma"]),
        "be1": row(inputs["L1_attn_beta"]),
        "gate_s": row(inputs["L1_ffn_gate_s"]),
        "up_s": row(inputs["L1_ffn_up_s"]),
        "down_s": row(inputs["L1_ffn_down_s"]),
    }
    consts = _consts()
    cores = []
    for c in range(8):
        bidx, h = c // 2, c % 2
        own = x[bidx, h * TOK:(h + 1) * TOK, :]
        other = x[bidx, (1 - h) * TOK:(2 - h) * TOK, :]
        cosT, sinT = _rotary_tables(h * TOK, TOK)
        m = dict(consts)
        m.update(vecs)
        m["x_all"] = np.ascontiguousarray(np.concatenate([own, other], 0))
        m["cosT"] = cosT
        m["sinT"] = sinT
        pm = np.zeros((8, 1), np.float32)
        pm[2 * bidx, 0] = 1.0
        pm[2 * bidx + 1, 0] = 1.0
        m["pairmask"] = pm
        for nm, w in full.items():
            r = w.shape[0] // 8
            m[nm + "_sl"] = np.ascontiguousarray(w[c * r:(c + 1) * r, :])
        cores.append(m)
    return cores


FUSED = True


def kernel(**inputs):
    cores = _prep_inputs(inputs)
    if FUSED:
        if "F" not in _PROG_CACHE:
            _PROG_CACHE["F"] = build_program("ABCD", fused=True)
        nc, _ = _PROG_CACHE["F"]
        res = run_bass_kernel_spmd(nc, cores, core_ids=list(range(8)))
        outs = [res.results[c]["xo"] for c in range(8)]
    else:
        cur = [m["x_all"][0:TOK, :] for m in cores]
        for ph in "ABCD":
            if ph not in _PROG_CACHE:
                _PROG_CACHE[ph] = build_program(ph, fused=False)
            nc, _ = _PROG_CACHE[ph]
            in_maps = []
            names = set(PHASE_INPUTS[ph]) | set(CONST_NAMES)
            for ci, m in enumerate(cores):
                mm = {k: m[k] for k in names if k in m}
                if "xs" in names:
                    mm["xs"] = np.ascontiguousarray(cur[ci])
                in_maps.append(mm)
            res = run_bass_kernel_spmd(nc, in_maps, core_ids=list(range(8)))
            cur = [res.results[c]["xo"] for c in range(8)]
        outs = cur
    y = np.empty((4, TOKA, D), np.float32)
    for c in range(8):
        bidx, h = c // 2, c % 2
        y[bidx, h * TOK:(h + 1) * TOK, :] = outs[c]
    return y


# version 05 (final)
# speedup vs baseline: 1.0471x; 1.0055x over previous
import numpy as np
from contextlib import ExitStack
import concourse.bass as bass
import concourse.mybir as mybir
from concourse.bass_utils import run_bass_kernel_spmd

F32 = mybir.dt.float32
BF16 = mybir.dt.bfloat16
AF = mybir.ActivationFunctionType
ALU = mybir.AluOpType
AXX = mybir.AxisListType.X

ENGS = ("sync", "tensor", "vector", "scalar", "gpsimd")
G8 = [[0, 1, 2, 3, 4, 5, 6, 7]]
TOK = 2048
TOKA = 4096
D = 2048
KC = 16
FF = 5632
FFC = 44
EPS_LN = 1e-5
EPS_Q = 1e-8
ARENA_KB = 205


class Sem:
    __slots__ = ("h", "v", "name")

    def __init__(self, h, name):
        self.h = h
        self.v = 0
        self.name = name


class B:

    def __init__(self, nc, es):
        self.nc = nc
        self.es = es
        self.q = {e: [] for e in ENGS}
        self.nbar = 0
        self.barsem = None
        self.arena_b = es.enter_context(
            nc.sbuf_tensor("arena", [128, ARENA_KB * 512], BF16))
        self.arena_f = self.arena_b.bitcast(F32)
        self.aoff = 0
        self.amax = ARENA_KB * 1024
        self.pb = [es.enter_context(nc.psum_tensor(f"pb{i}", [128, 512], F32))
                   for i in range(8)]
        self.pbb = [p.bitcast(BF16) for p in self.pb]

    def sem(self, name):
        return Sem(self.es.enter_context(self.nc.semaphore(name)), name)

    def mark(self):
        return self.aoff

    def reset(self, m):
        self.aoff = m

    def tile(self, shape, dt):
        p, f = shape
        esz = 2 if dt == BF16 else 4
        off = (self.aoff + 63) & ~63
        self.aoff = off + f * esz
        assert self.aoff <= self.amax, f"arena overflow {self.aoff}"
        base = self.arena_b if dt == BF16 else self.arena_f
        return base[0:p, off // esz:off // esz + f]

    def sb(self, name, shape, dt):
        return self.es.enter_context(self.nc.sbuf_tensor(name, shape, dt))

    def wait(self, eng, sem, val):
        if val <= 0:
            return
        assert val <= sem.v, f"wait {sem.name}>={val} but planned {sem.v} (hang)"
        self.q[eng].append(lambda e, s=sem, v=val: e.wait_ge(s.h, v))

    def dma(self, eng, out, in_, sem):
        sem.v += 16
        self.q[eng].append(
            lambda e, o=out, i=in_, s=sem:
            e.dma_start(out=o, in_=i).then_inc(s.h, 16))
        return sem.v

    def op(self, eng, fn, sem=None):
        if sem is None:
            self.q[eng].append(lambda e, f=fn: f(e))
            self._lastfn = getattr(self, "_lastfn", {})
            self._lastfn[eng] = fn
            return None
        sem.v += 1
        self.q[eng].append(lambda e, f=fn, s=sem: f(e).then_inc(s.h, 1))
        return sem.v

    def inc_last(self, eng, sem):
        fn = self._lastfn.get(eng)
        assert fn is not None, "inc_last without preceding op"
        self.q[eng].pop()
        sem.v += 1
        self.q[eng].append(lambda e, f=fn, s=sem: f(e).then_inc(s.h, 1))
        self._lastfn[eng] = None
        return sem.v

    def coll(self, kind, op, groups, ins, outs, sem):
        sem.v += 1
        self.q["gpsimd"].append(
            lambda e, s=sem: e.collective_compute(
                kind, op, replica_groups=groups, ins=ins,
                outs=outs).then_inc(s.h))
        return sem.v

    def barrier(self):
        if self.barsem is None:
            self.barsem = self.sem("barsem")
        self.nbar += 1
        tgt = 5 * self.nbar
        self.barsem.v = tgt
        for e in ENGS:
            self.q[e].append(lambda x, s=self.barsem: x.nop().then_inc(s.h, 1))
        for e in ENGS:
            self.q[e].append(lambda x, s=self.barsem, v=tgt: x.wait_ge(s.h, v))

    def finalize(self):
        blk = self.es.enter_context(self.nc.Block())
        for name in ENGS:
            qq = self.q[name]

            def make(qq):
                def _(eng):
                    for f in qq:
                        f(eng)
                return _
            getattr(blk, name)(make(qq))


class MK:
    def __init__(self, nc, b, io):
        self.nc = nc
        self.b = b
        self.io = io
        self.c = {}
        self.sm = {n: b.sem(n) for n in
                   ("ld", "ldg", "st", "stg", "pe", "dve", "act", "cc")}

    def s(self, name):
        if name not in self.sm:
            self.sm[name] = self.b.sem(name)
        return self.sm[name]

    def wload(self, dst, c0, c1, dest, stream):
        gf, off, srow, cols = self.wmeta[dst]
        q = srow // 128
        w = c1 - c0
        v = gf[:, off:off + srow * cols].rearrange(
            "r (q p f) -> p q r f", q=q, p=128, f=cols)
        dk = dest.rearrange("p (r q f) -> p q r f", r=8, q=q, f=w)
        lv = None
        for qq in range(q):
            lv = self.b.dma("sync", dk[:, qq, :, :],
                            v[:, qq, :, c0:c1], stream)
        return lv

    def wview(self, dst, c0, c1):
        gf, off, srow, cols = self.wmeta[dst]
        q = srow // 128
        v = gf[:, off:off + srow * cols]
        return v.rearrange("r (q p f) -> p r q f", q=q, p=128,
                           f=cols)[:, :, :, c0:c1]

    def wview_rows(self, dst, rb, c0, c1):
        gf, off, srow, cols = self.wmeta[dst]
        q = srow // 128
        v = gf[rb:rb + 1, off:off + srow * cols]
        return v.rearrange("one (j p f) -> p (one j) f", j=q, p=128,
                           f=cols)[:, :, c0:c1]

    def wait_dma_all(self, eng):
        for name, sem in list(self.sm.items()):
            if name in ("pe", "dve", "act", "cc"):
                continue
            self.b.wait(eng, sem, sem.v)

    def emit_startup(self):
        b = self.b
        ld, dve = self.sm["ld"], self.sm["dve"]
        for nm in ("ident_f", "ident_b", "ones_f", "ones_b"):
            dt = F32 if nm.endswith("_f") else BF16
            shape = [128, 128] if nm.startswith("ident") else [1, 128]
            t = b.sb("sb_" + nm, shape, dt)
            b.dma("sync", t[:, :], self.io[nm][:, :], ld)
            self.c[nm] = t
        eps = b.sb("epst", [128, 1], F32)
        onescol = b.sb("onescol", [128, 1], BF16)
        onescolf = b.sb("onescolf", [128, 1], F32)
        b.op("vector", lambda v: v.memset(eps[:, :], EPS_LN))
        b.op("vector", lambda v: v.memset(onescol[:, :], 1.0))
        b.op("vector", lambda v: v.memset(onescolf[:, :], 1.0), sem=dve)
        self.c["eps"] = eps
        self.c["ones_col_b"] = onescol
        self.c["ones_col_f"] = onescolf
        b.wait("sync", ld, ld.v)
        b.barrier()

    def prep_params(self, chunks=(), bf16rows=(), bcasts=()):
        b = self.b
        ld, dve = self.sm["ld"], self.sm["dve"]
        act, pe = self.sm["act"], self.sm["pe"]
        pv = b.pb[0]
        outs = {}
        for name, n in chunks:
            outs[name] = b.tile([128, max(n // 128, 1)], F32)
        for name, n in bf16rows:
            outs[name] = b.tile([1, n], BF16)
        for name, n in bcasts:
            outs[name] = b.tile([128, n], F32)
        mstage = b.mark()
        stage = b.tile([128, 8192], F32)
        for name, n in chunks:
            nch = max(n // 128, 1)
            out = outs[name]
            if n >= 128:
                src = self.io[name].rearrange("one (c p) -> (one c) p", p=128)
            else:
                src = self.io[name]
            b.wait("vector", pe, pe.v)
            zv = b.op("vector", lambda v: v.memset(
                stage[0:128, 0:128], 0.0), sem=dve)
            b.wait("sync", dve, zv)
            lv = b.dma("sync", stage[0:nch, 0:128], src, ld)
            b.wait("tensor", ld, lv)
            b.wait("tensor", dve, dve.v)
            b.op("tensor", lambda t: t.transpose(
                out=pv[:, 0:128], in_=stage[0:128, 0:128],
                identity=self.c["ident_f"][:, :]), sem=pe)
            b.wait("vector", pe, pe.v)
            b.op("vector", lambda v, o=out, nch=nch: v.tensor_copy(
                out=o[:, 0:nch], in_=pv[:, 0:nch]), sem=dve)
            b.wait("sync", dve, dve.v)
            self.c[name] = out
        for name, n in bf16rows:
            out = outs[name]
            lv = b.dma("sync", stage[0:1, 0:n], self.io[name][:, :], ld)
            b.wait("scalar", ld, lv)
            av = b.op("scalar", lambda s, o=out, n=n: s.activation(
                out=o[0:1, 0:n], in_=stage[0:1, 0:n], func=AF.Copy, bias=0.0),
                sem=act)
            b.wait("sync", act, av)
            self.c[name] = out
        for name, n in bcasts:
            out = outs[name]
            self._prep_ctr = getattr(self, "_prep_ctr", 0) + 1
            scr = self.nc.dram_tensor(f"prep_bc_{self._prep_ctr}", [1, n],
                                      F32, kind="Internal").ap()
            lv = b.dma("sync", stage[0:1, 0:n], self.io[name][:, :], ld)
            b.wait("sync", ld, lv)
            sv = b.dma("sync", scr[:, :], stage[0:1, 0:n], self.sm["st"])
            b.wait("sync", self.sm["st"], sv)
            b.dma("sync", out[:, 0:n], scr.broadcast_to([128, n]), ld)
            b.wait("sync", ld, ld.v)
            self.c[name] = out
        for eng in ("sync", "gpsimd", "tensor", "vector", "scalar"):
            b.wait(eng, dve, dve.v)
            b.wait(eng, act, act.v)
            b.wait(eng, pe, pe.v)
        b.reset(mstage)

    def emit_W(self, weights, quant):
        b, nc = self.b, self.nc
        ld, st, cc = self.sm["ld"], self.sm["st"], self.sm["cc"]
        dve, act, pe = self.sm["dve"], self.sm["act"], self.sm["pe"]
        pv = b.pb[0]
        m0 = b.mark()
        CW = 8192
        stage = [b.tile([128, CW], F32) for _ in range(2)]
        outb = [b.tile([128, CW], BF16) for _ in range(2)]
        sga = b.tile([128, CW], BF16)
        red = b.tile([128, 1], F32)
        redtmp = b.tile([128, 1], F32)
        sums = b.tile([1, 8], F32)
        thr = b.tile([1, 8], F32)
        thr_bc = b.tile([128, 8], F32)
        thr_nbc = b.tile([128, 8], F32)
        dbg = b.tile([128, 4], F32)
        self.dbg_tiles = dbg
        self._w_ctr = getattr(self, "_w_ctr", 0) + 1
        wc = self._w_ctr
        if quant:
            ar_in = nc.dram_tensor(f"w_ar_in{wc}", [1, 8], F32,
                                   kind="Internal").ap()
            ar_out = nc.dram_tensor(f"w_ar_out{wc}", [1, 8], F32,
                                    kind="Internal").ap()
            red_dram = nc.dram_tensor(f"w_red_dram{wc}", [768], F32,
                                      kind="Internal").ap()
            redrow = b.tile([1, 768], F32)
            red2 = b.tile([128, 6], F32)
            stage_cons = [0, 0]
            b.op("vector", lambda v: v.memset(sums[0:1, 0:8], 0.0), sem=dve)
            for qi, (src, dst, rows, cols) in enumerate(quant):
                srow = rows // 8
                ntile = (srow + 127) // 128
                b.op("vector", lambda v, n=ntile: v.memset(
                    red2[:, 0:n], 0.0), sem=dve)
                for ti in range(ntile):
                    r0, r1 = ti * 128, min((ti + 1) * 128, srow)
                    bu = stage[ti % 2]
                    wls = self.s(f"wld{ti % 2}")
                    b.wait("sync", dve, stage_cons[ti % 2])
                    lv = b.dma("sync", bu[0:r1 - r0, 0:cols],
                               self.io[src][r0:r1, :], wls)
                    b.wait("vector", wls, lv)
                    b.op("vector", lambda v, bu=bu, r=r1 - r0, ti=ti,
                         cols=cols: v.reduce_sum(
                             out=red2[0:r, ti:ti + 1], in_=bu[0:r, 0:cols],
                             axis=AXX, apply_absolute_value=True), sem=dve)
                    stage_cons[ti % 2] = dve.v
                b.wait("gpsimd", dve, dve.v)
                b.dma("gpsimd", red_dram[0:128 * ntile],
                      red2[:, 0:ntile], st)
                b.wait("gpsimd", st, st.v)
                rl = b.dma("gpsimd", redrow[0:1, 0:128 * ntile],
                           red_dram[0:128 * ntile], st)
                b.wait("vector", st, rl)
                b.op("vector", lambda v, qi=qi, n=ntile: v.reduce_sum(
                    out=sums[0:1, qi:qi + 1], in_=redrow[0:1, 0:128 * n],
                    axis=AXX), sem=dve)
            b.op("vector", lambda v: v.tensor_copy(
                out=dbg[32:33, 0:3], in_=sums[0:1, 0:3]), sem=dve)
            b.wait("gpsimd", dve, dve.v)
            b.dma("gpsimd", ar_in[:, :], sums[0:1, 0:8], st)
            b.wait("gpsimd", st, st.v)
            b.coll("AllReduce", ALU.add, G8, [ar_in.opt()], [ar_out.opt()], cc)
            b.wait("gpsimd", cc, cc.v)
            b.dma("gpsimd", sums[0:1, 0:8], ar_out[:, :], st)
            b.wait("vector", st, st.v)
            b.op("vector", lambda v: v.tensor_copy(
                out=dbg[64:65, 0:3], in_=sums[0:1, 0:3]), sem=dve)
            for qi, (src, dst, rows, cols) in enumerate(quant):
                b.op("vector",
                     lambda v, qi=qi, n=QNUMEL[quant[qi][1]]: v.tensor_scalar(
                         out=thr[0:1, qi:qi + 1], in0=sums[0:1, qi:qi + 1],
                         scalar1=0.5 / n, scalar2=0.5 * EPS_Q,
                         op0=ALU.mult, op1=ALU.add), sem=dve)
            b.wait("gpsimd", dve, dve.v)
            thr_dram = nc.dram_tensor(f"w_thr_dram{wc}", [1, 8], F32,
                                      kind="Internal").ap()
            b.dma("gpsimd", thr_dram[:, :], thr[0:1, 0:8], st)
            b.wait("gpsimd", st, st.v)
            b.dma("gpsimd", thr_bc[:, 0:8],
                  thr_dram.broadcast_to([128, 8]), st)
            b.wait("vector", st, st.v)
            b.op("vector", lambda v: v.tensor_scalar_mul(
                out=thr_nbc[:, 0:8], in0=thr_bc[:, 0:8], scalar1=-1.0),
                sem=dve)
            b.wait("scalar", dve, dve.v)
            b.wait("sync", dve, dve.v)
            b.wait("gpsimd", dve, dve.v)
        if not hasattr(self, "gins"):
            self.gins = {}
        allw = [(s, d, r, c, False) for (s, d, r, c) in weights] + \
               [(s, d, r, c, True) for (s, d, r, c) in quant]
        rbsize = sum(r // 8 * c for (_, _, r, c, _) in allw)
        gf_in = nc.dram_tensor(f"gfin{wc}", [1, rbsize], BF16,
                               kind="Internal").ap()
        gf_out = nc.dram_tensor(f"gfout{wc}", [8, rbsize], BF16,
                                kind="Internal").ap()
        goff = 0
        if not hasattr(self, "wmeta"):
            self.wmeta = {}
        stv = [0, 0]
        acons = [0, 0]
        sga_cons = 0
        tig = 0
        for src, dst, rows, cols, isq in allw:
            srow = rows // 8
            gin = gf_in[0:1, goff:goff + srow * cols].rearrange(
                "one (s f) -> (one s) f", f=cols)
            self.wmeta[dst] = (gf_out, goff, srow, cols)
            goff += srow * cols
            self.gins[dst] = gin
            if isq:
                qi = [q[0] for q in quant].index(src)
            ntile = (srow + 127) // 128
            for ti in range(ntile):
                r0, r1 = ti * 128, min((ti + 1) * 128, srow)
                rr = r1 - r0
                par = tig % 2
                tig += 1
                bu, ob = stage[par], outb[par]
                wls = self.s(f"wld{par}")
                wss = self.s(f"wst{par}")
                b.wait("sync", wss, stv[par])
                b.wait("sync", act, acons[par])
                lv = b.dma("sync", bu[0:rr, 0:cols], self.io[src][r0:r1, :],
                           wls)
                b.wait("scalar", wls, lv)
                b.wait("scalar", wss, stv[par])
                if isq:
                    b.wait("scalar", dve, sga_cons)
                    b.op("scalar", lambda s, bu=bu, rr=rr, cols=cols,
                         qi=qi: s.activation(
                             out=sga[0:rr, 0:cols], in_=bu[0:rr, 0:cols],
                             func=AF.Sign,
                             bias=thr_nbc[0:rr, qi:qi + 1]), sem=act)
                    av = b.op("scalar", lambda s, bu=bu, ob=ob, rr=rr,
                              cols=cols, qi=qi: s.activation(
                                  out=ob[0:rr, 0:cols], in_=bu[0:rr, 0:cols],
                                  func=AF.Sign,
                                  bias=thr_bc[0:rr, qi:qi + 1]), sem=act)
                    b.wait("vector", act, av)
                    dv = b.op("vector", lambda v, ob=ob, rr=rr, cols=cols:
                              v.tensor_tensor(out=ob[0:rr, 0:cols],
                                              in0=ob[0:rr, 0:cols],
                                              in1=sga[0:rr, 0:cols],
                                              op=ALU.add), sem=dve)
                    b.wait("gpsimd", dve, dv)
                    acons[par] = av
                    sga_cons = dv
                else:
                    av = b.op("scalar", lambda s, bu=bu, ob=ob, rr=rr,
                              cols=cols: s.activation(
                                  out=ob[0:rr, 0:cols], in_=bu[0:rr, 0:cols],
                                  func=AF.Copy, bias=0.0), sem=act)
                    b.wait("gpsimd", act, av)
                    acons[par] = av
                stv[par] = b.dma("gpsimd", gin[r0:r1, :],
                                 ob[0:rr, 0:cols], wss)
        self.wait_dma_all("gpsimd")
        if allw:
            b.coll("AllGather", ALU.bypass, G8, [gf_in.opt()],
                   [gf_out.opt()], cc)
        b.wait("gpsimd", cc, cc.v)
        b.barrier()
        b.reset(m0)

    def emit_ln(self, xin, lnT, ntile, n_add=0, x_add_out=None,
                keep_f32=None, inner=False, gamma_beta=None):
        b = self.b
        ld, st = self.sm["ldg"], self.sm["stg"]
        dve, act, pe = self.sm["dve"], self.sm["act"], self.sm["pe"]
        ncol = ntile * 128
        light = (n_add == 0 and keep_f32 is None and not inner)
        m0 = b.mark()
        xa = [b.tile([128, D], F32) for _ in range(2)]
        lnf = None if light else [b.tile([128, D], F32) for _ in range(2)]
        lnb = [b.tile([128, D], BF16) for _ in range(2)]
        xm = [b.tile([128, D], F32) for _ in range(2)] if n_add else None
        stats = b.tile([128, 24], F32)
        mv = b.tile([128, 2], F32)
        rstd = b.tile([128, 1], F32)
        pT = [b.pbb[6], b.pbb[7]]
        xa_free = [0, 0]
        lnb_free = [0, 0]
        xm_free = [0, 0]
        lnf_free = [0, 0]
        pT_free = [0, 0]
        act_mv_read = 0

        def stats_rstd(src, p):
            for i in range(3):
                b.op("vector", lambda v, p=p, i=i, s=src: v.bn_stats(
                    out=stats[:, i * 6:(i + 1) * 6],
                    in_=s[p][:, i * 512:(i + 1) * 512]))
            sv = b.op("vector", lambda v, p=p, s=src: v.bn_stats(
                out=stats[:, 18:24], in_=s[p][:, 1536:2048]), sem=dve)
            b.wait("vector", dve, sv)
            b.op("vector", lambda v: v.bn_aggr(out=mv[:, :], in_=stats[:, :]),
                 sem=dve)
            v_ = b.op("vector", lambda v: v.engine_nop(), sem=dve)
            b.wait("scalar", dve, v_)
            b.op("scalar", lambda s: s.activation(
                out=rstd[:, :], in_=mv[:, 1:2], func=AF.Ln,
                bias=self.c["eps"][:, :]))
            return b.op("scalar", lambda s: s.activation(
                out=rstd[:, :], in_=rstd[:, :], func=AF.Exp, scale=-0.5),
                sem=act)

        for t in range(ntile):
            p = t % 2
            lxs = self.s(f"lnx{p}")
            lms = self.s(f"lnm{p}")
            lfs = self.s(f"lnf{p}")
            b.wait("gpsimd", dve, xa_free[p])
            if n_add:
                b.wait("gpsimd", lms, xm_free[p])
            lv = b.dma("gpsimd", xa[p][:, :], xin[t * 128:(t + 1) * 128, :],
                       lxs)
            b.wait("vector", lxs, lv)
            b.wait("vector", act, act_mv_read)
            av = stats_rstd(xa, p)
            act_mv_read = av
            b.wait("vector", act, av)
            if keep_f32 is not None:
                b.wait("vector", lfs, lnf_free[p])
            if light:
                b.wait("vector", pe, lnb_free[p])
                b.op("vector", lambda v, p=p: v.tensor_scalar(
                    out=lnb[p][:, :], in0=xa[p][:, :], scalar1=mv[:, 0:1],
                    scalar2=rstd[:, 0:1], op0=ALU.subtract, op1=ALU.mult),
                    sem=dve)
            else:
                b.op("vector", lambda v, p=p: v.tensor_scalar(
                    out=lnf[p][:, :], in0=xa[p][:, :], scalar1=mv[:, 0:1],
                    scalar2=rstd[:, 0:1], op0=ALU.subtract, op1=ALU.mult),
                    sem=dve)
            if t < n_add:
                dv = b.op("vector", lambda v, p=p: v.tensor_tensor(
                    out=xm[p][:, :], in0=xa[p][:, :], in1=lnf[p][:, :],
                    op=ALU.add), sem=dve)
                b.wait("gpsimd", dve, dv)
                xm_free[p] = b.dma("gpsimd",
                                   x_add_out[t * 128:(t + 1) * 128, :],
                                   xm[p][:, :], lms)
            xa_free[p] = dve.v
            if keep_f32 is not None:
                b.wait("gpsimd", dve, dve.v)
                lnf_free[p] = b.dma("gpsimd",
                                    keep_f32[t * 128:(t + 1) * 128, :],
                                    lnf[p][:, :], lfs)
            if inner:
                b.wait("vector", act, act_mv_read)
                av = stats_rstd(lnf, p)
                act_mv_read = av
                b.wait("vector", act, av)
            if not light:
                b.wait("vector", pe, lnb_free[p])
                if inner:
                    b.op("vector", lambda v, p=p: v.tensor_scalar(
                        out=lnb[p][:, :], in0=lnf[p][:, :],
                        scalar1=mv[:, 0:1], scalar2=rstd[:, 0:1],
                        op0=ALU.subtract, op1=ALU.mult), sem=dve)
                else:
                    b.op("vector", lambda v, p=p: v.tensor_copy(
                        out=lnb[p][:, :], in_=lnf[p][:, :]), sem=dve)
            lnb_ready = dve.v
            for g in range(4):
                gp = g % 2
                b.wait("tensor", dve, max(lnb_ready, pT_free[gp]))
                for j in range(4):
                    k = g * 4 + j
                    b.op("tensor", lambda tt, p=p, gp=gp, j=j, k=k:
                         tt.transpose(
                             out=pT[gp][:, j * 128:(j + 1) * 128],
                             in_=lnb[p][:, k * 128:(k + 1) * 128],
                             identity=self.c["ident_b"][:, :]))
                pvv = b.inc_last("tensor", pe)
                b.wait("vector", pe, pvv)
                if gamma_beta is None:
                    b.op("vector", lambda v, gp=gp, g=g, t=t: v.tensor_copy(
                        out=lnT.rearrange("p (k c) -> p k c", k=KC)
                        [:, 4 * g:4 * g + 4, t * 128:(t + 1) * 128],
                        in_=pT[gp][:, 0:512].rearrange(
                            "p (j c) -> p j c", j=4)), sem=dve)
                else:
                    gch, bch = gamma_beta
                    for j in range(4):
                        k = g * 4 + j
                        b.op("vector", lambda v, gp=gp, j=j, k=k, t=t:
                             v.tensor_scalar(
                                 out=lnT[:, k * ncol + t * 128:
                                         k * ncol + (t + 1) * 128],
                                 in0=pT[gp][:, j * 128:(j + 1) * 128],
                                 scalar1=gch[:, k:k + 1],
                                 scalar2=bch[:, k:k + 1],
                                 op0=ALU.mult, op1=ALU.add), sem=dve)
                pT_free[gp] = dve.v
            lnb_free[p] = pe.v
        self.wait_dma_all("gpsimd")
        b.wait("tensor", dve, dve.v)
        b.reset(m0)

    def emit_A(self, x_all, xout):
        b, nc = self.b, self.nc
        ld = self.sm["ld"]
        ldg, stg = self.sm["ldg"], self.sm["stg"]
        dve, act, pe = self.sm["dve"], self.sm["act"], self.sm["pe"]
        xA = nc.dram_tensor("a_xA", [TOK, D], F32, kind="Internal").ap()
        vhbm = nc.dram_tensor("a_v", [16, 32, 128, 256], BF16,
                              kind="Internal").ap()
        gthbm = nc.dram_tensor("a_gt", [4, 32, 128, 512], BF16,
                               kind="Internal").ap()
        mA = b.mark()
        qT = b.tile([128, TOK], BF16)
        kT = b.tile([128, TOKA], BF16)
        self.prep_params(
            chunks=[("b_qk", 128), ("gq", 128), ("bq", 128), ("gk", 128),
                    ("bk", 128), ("b_hidden", 8192)],
            bf16rows=[("b_hidden_bf", 8192), ("b_out0_bf", 2048)])
        m1 = b.mark()
        for th in range(2):
            toff = th * TOK
            ln0T = b.tile([128, KC * TOK], BF16)
            self.emit_ln(x_all[toff:toff + TOK, :], ln0T, TOK // 128,
                         n_add=(TOK // 128 if th == 0 else 0), x_add_out=xA)
            b.barrier()
            qkT = b.tile([128, TOK], F32)
            wqk = b.tile([128, KC * 128], BF16)
            pq = [b.pb[0], b.pb[1]]
            awq = self.s("awq")
            lw = self.wload("W_qk_bf", 0, 128, wqk, awq)
            b.wait("tensor", awq, lw)
            pq_free = [0, 0]
            for tb in range(4):
                bk = tb % 2
                b.wait("tensor", act, pq_free[bk])
                for k in range(KC):
                    b.op("tensor", lambda t, k=k, tb=tb, bk=bk: t.matmul(
                        out=pq[bk][:, :], lhsT=wqk[:, k * 128:(k + 1) * 128],
                        rhs=ln0T[:, k * TOK + tb * 512:
                                 k * TOK + tb * 512 + 512],
                        start=(k == 0), stop=(k == KC - 1)))
                pvv = b.inc_last("tensor", pe)
                b.wait("scalar", pe, pvv)
                pq_free[bk] = b.op("scalar", lambda s, bk=bk, tb=tb:
                                   s.activation(
                                       out=qkT[:, tb * 512:(tb + 1) * 512],
                                       in_=pq[bk][:, :], func=AF.Silu,
                                       bias=self.c["b_qk"][:, 0:1]), sem=act)
            b.wait("vector", act, act.v)
            if th == 0:
                b.op("vector", lambda v: v.tensor_scalar(
                    out=qT[:, :], in0=qkT[:, :], scalar1=self.c["gq"][:, 0:1],
                    scalar2=self.c["bq"][:, 0:1], op0=ALU.mult, op1=ALU.add))
            b.op("vector", lambda v, toff=toff: v.tensor_scalar(
                out=kT[:, toff:toff + TOK], in0=qkT[:, :],
                scalar1=self.c["gk"][:, 0:1], scalar2=self.c["bk"][:, 0:1],
                op0=ALU.mult, op1=ALU.add), sem=dve)
            b.wait("tensor", dve, dve.v)
            whb = [b.tile([128, KC * 256], BF16) for _ in range(2)]
            vt = [b.tile([128, 512], BF16) for _ in range(2)]
            w_cons2 = [0, 0]
            pa_free = [0, 0]
            vst = [0, 0]
            awh = [self.s("awh0"), self.s("awh1")]
            avs = [self.s("avs0"), self.s("avs1")]
            for wb2 in range(32 if th == 0 else 16):
                wp2 = wb2 % 2
                b.wait("sync", pe, w_cons2[wp2])
                lw = self.wload("W_hidden_bf", wb2 * 256, (wb2 + 1) * 256,
                                whb[wp2], awh[wp2])
                b.wait("tensor", awh[wp2], lw)
                if wb2 < 16:
                    for t in range(16):
                        tg = th * 16 + t
                        bk = t % 2
                        b.wait("tensor", act, pa_free[bk])
                        for k in range(KC):
                            b.op("tensor", lambda tt, k=k, t=t, bk=bk,
                                 wp2=wp2: tt.matmul(
                                     out=pq[bk][:, 0:256],
                                     lhsT=ln0T[:, k * TOK + t * 128:
                                               k * TOK + t * 128 + 128],
                                     rhs=whb[wp2][:, k * 256:(k + 1) * 256],
                                     start=(k == 0), stop=False))
                        b.op("tensor", lambda tt, wb2=wb2, bk=bk: tt.matmul(
                            out=pq[bk][:, 0:256], lhsT=self.c["ones_b"][:, :],
                            rhs=self.c["b_hidden_bf"][0:1, wb2 * 256:
                                                      (wb2 + 1) * 256],
                            start=False, stop=True))
                        pvv = b.inc_last("tensor", pe)
                        b.wait("scalar", pe, pvv)
                        b.wait("scalar", avs[bk], vst[bk])
                        av = b.op("scalar", lambda s, bk=bk: s.activation(
                            out=vt[bk][:, 0:256], in_=pq[bk][:, 0:256],
                            func=AF.Silu), sem=act)
                        pa_free[bk] = av
                        b.wait("gpsimd", act, av)
                        vst[bk] = b.dma("gpsimd", vhbm[wb2, tg, :, :],
                                        vt[bk][:, 0:256], avs[bk])
                else:
                    for gfc in range(2):
                        for tb in range(4):
                            bk = (gfc * 4 + tb) % 2
                            b.wait("tensor", act, pa_free[bk])
                            for k in range(KC):
                                b.op("tensor", lambda tt, k=k, gfc=gfc,
                                     tb=tb, bk=bk, wp2=wp2: tt.matmul(
                                         out=pq[bk][:, :],
                                         lhsT=whb[wp2][:, k * 256 +
                                                       gfc * 128:
                                                       k * 256 +
                                                       gfc * 128 + 128],
                                         rhs=ln0T[:, k * TOK + tb * 512:
                                                  k * TOK + tb * 512 + 512],
                                         start=(k == 0),
                                         stop=(k == KC - 1)))
                            pvv = b.inc_last("tensor", pe)
                            b.wait("scalar", pe, pvv)
                            b.wait("scalar", avs[bk], vst[bk])
                            gch = 32 + (wb2 - 16) * 2 + gfc
                            av = b.op("scalar", lambda s, bk=bk, gch=gch:
                                      s.activation(
                                          out=vt[bk][:, :], in_=pq[bk][:, :],
                                          func=AF.Silu,
                                          bias=self.c["b_hidden"]
                                          [:, gch:gch + 1]), sem=act)
                            pa_free[bk] = av
                            b.wait("gpsimd", act, av)
                            vst[bk] = b.dma(
                                "gpsimd",
                                gthbm[tb, (wb2 - 16) * 2 + gfc, :, :],
                                vt[bk][:, :], avs[bk])
                w_cons2[wp2] = pe.v
            self.wait_dma_all("gpsimd")
            b.wait("tensor", act, act.v)
            b.barrier()
            b.reset(m1)
        attnT = b.tile([128, 32 * 512], BF16)
        ogT = b.tile([128, 32 * 512], BF16)
        vsec = [b.tile([128, 32 * 256], BF16) for _ in range(2)]
        wob_t = b.tile([128, 32 * 512], BF16)
        wob = [wob_t, wob_t]
        gt2 = [b.tile([128, 2 * 512], BF16) for _ in range(2)]
        simf = [b.tile([128, 512], F32) for _ in range(2)]
        xr2 = [b.tile([128, 512], F32) for _ in range(2)]
        yo2 = [b.tile([128, 512], F32) for _ in range(2)]
        psim = [b.pb[0], b.pb[1]]
        pout = [b.pb[2], b.pb[3], b.pb[4], b.pb[5]]
        pyy = [b.pb[6], b.pb[7]]
        attn_pe_reads = 0
        og_pe_reads = 0
        for g in range(4):
            psim_free = [0, 0]
            simf_free = [0, 0]
            b.wait("scalar", pe, attn_pe_reads)
            for tkc in range(32):
                sp = tkc % 2
                b.wait("tensor", dve, psim_free[sp])
                b.op("tensor", lambda t, tkc=tkc, g=g, sp=sp: t.matmul(
                    out=psim[sp][:, :],
                    lhsT=kT[:, tkc * 128:(tkc + 1) * 128],
                    rhs=qT[:, g * 512:(g + 1) * 512], start=True, stop=True))
                pvv = b.inc_last("tensor", pe)
                b.wait("vector", pe, pvv)
                b.wait("vector", act, simf_free[sp])
                dv = b.op("vector", lambda v, sp=sp: v.tensor_scalar_max(
                    out=simf[sp][:, :], in0=psim[sp][:, :], scalar1=0.0),
                    sem=dve)
                psim_free[sp] = dv
                b.wait("scalar", dve, dv)
                simf_free[sp] = b.op(
                    "scalar", lambda s, sp=sp, tkc=tkc: s.activation(
                        out=attnT[:, tkc * 512:(tkc + 1) * 512],
                        in_=simf[sp][:, :], func=AF.Square,
                        scale=1.0 / TOKA), sem=act)
            attn_ready = act.v
            vcons = [0, 0]
            gcons = [0, 0]
            vld = [0, 0]
            gld = [0, 0]
            po_free = [0, 0, 0, 0]
            b.wait("tensor", act, attn_ready)
            b.wait("vector", pe, og_pe_reads)
            avv = [self.s("av0"), self.s("av1")]
            agg = [self.s("ag0"), self.s("ag1")]
            for s in range(16):
                vp = s % 2
                b.wait("sync", pe, vcons[vp])
                vld[vp] = b.dma("sync",
                                vsec[vp].rearrange("p (t f) -> p t f", t=32),
                                vhbm[s, :, :, :].rearrange("t p f -> p t f"),
                                avv[vp])
                b.wait("gpsimd", dve, gcons[vp])
                b.dma("gpsimd", gt2[vp][:, 0:512], gthbm[g, 2 * s, :, :],
                      agg[vp])
                gld[vp] = b.dma("gpsimd", gt2[vp][:, 512:1024],
                                gthbm[g, 2 * s + 1, :, :], agg[vp])
                b.wait("tensor", avv[vp], vld[vp])
                for vfc in range(2):
                    ob = (2 * s + vfc) % 4
                    b.wait("tensor", dve, po_free[ob])
                    for tkc in range(32):
                        b.op("tensor", lambda t, vfc=vfc, tkc=tkc, ob=ob,
                             vp=vp: t.matmul(
                                 out=pout[ob][:, :],
                                 lhsT=vsec[vp][:, tkc * 256 + vfc * 128:
                                               tkc * 256 + vfc * 128 + 128],
                                 rhs=attnT[:, tkc * 512:(tkc + 1) * 512],
                                 start=(tkc == 0), stop=(tkc == 31)))
                    pvv = b.inc_last("tensor", pe)
                    b.wait("vector", pe, pvv)
                    b.wait("vector", agg[vp], gld[vp])
                    dv = b.op("vector", lambda v, s=s, vfc=vfc, ob=ob, vp=vp:
                              v.tensor_tensor(
                                  out=ogT[:, (2 * s + vfc) * 512:
                                          (2 * s + vfc) * 512 + 512],
                                  in0=pout[ob][:, :],
                                  in1=gt2[vp][:, vfc * 512:(vfc + 1) * 512],
                                  op=ALU.mult), sem=dve)
                    po_free[ob] = dv
                    gcons[vp] = dv
                vcons[vp] = pe.v
            og_ready = dve.v
            attn_pe_reads = pe.v
            wob_cons = [0, 0]
            pyy_free = [0, 0]
            xr_cons = [0, 0]
            yo_st = [0, 0]
            b.wait("tensor", dve, og_ready)
            awo = self.s("awo")
            axs = [self.s("ax0"), self.s("ax1")]
            ays = [self.s("ay0"), self.s("ay1")]
            for dc in range(4):
                wp = 0
                b.wait("sync", pe, wob_cons[wp])
                lwo = self.wload("W_out0_bf", dc * 512, (dc + 1) * 512,
                                 wob[wp], awo)
                b.wait("tensor", awo, lwo)
                for tc in range(4):
                    yk = tc % 2
                    trow = g * 512 + tc * 128
                    b.wait("tensor", dve, pyy_free[yk])
                    for vfc in range(32):
                        b.op("tensor", lambda t, vfc=vfc, tc=tc, yk=yk,
                             wp=wp: t.matmul(
                                 out=pyy[yk][:, :],
                                 lhsT=ogT[:, vfc * 512 + tc * 128:
                                          vfc * 512 + tc * 128 + 128],
                                 rhs=wob[wp][:, vfc * 512:(vfc + 1) * 512],
                                 start=(vfc == 0), stop=False))
                    b.op("tensor", lambda t, dc=dc, yk=yk: t.matmul(
                        out=pyy[yk][:, :], lhsT=self.c["ones_b"][:, :],
                        rhs=self.c["b_out0_bf"][0:1, dc * 512:(dc + 1) * 512],
                        start=False, stop=True))
                    pvv = b.inc_last("tensor", pe)
                    b.wait("gpsimd", dve, xr_cons[yk])
                    xld = b.dma("gpsimd", xr2[yk][:, :],
                                xA[trow:trow + 128, dc * 512:(dc + 1) * 512],
                                axs[yk])
                    b.wait("vector", pe, pvv)
                    b.wait("vector", axs[yk], xld)
                    b.wait("vector", ays[yk], yo_st[yk])
                    dv = b.op("vector", lambda v, yk=yk: v.tensor_tensor(
                        out=yo2[yk][:, :], in0=pyy[yk][:, :],
                        in1=xr2[yk][:, :], op=ALU.add), sem=dve)
                    pyy_free[yk] = dv
                    xr_cons[yk] = dv
                    b.wait("gpsimd", dve, dv)
                    yo_st[yk] = b.dma("gpsimd",
                                      xout[trow:trow + 128,
                                           dc * 512:(dc + 1) * 512],
                                      yo2[yk][:, :], ays[yk])
                wob_cons[wp] = pe.v
            og_pe_reads = pe.v
        self.wait_dma_all("gpsimd")
        b.barrier()
        b.reset(mA)

    def emit_ffn(self, xin, xout, w_up1, w_up2, w_dn, mode):
        b = self.b
        ld = self.sm["ld"]
        ldg, stg = self.sm["ldg"], self.sm["stg"]
        dve, act, pe = self.sm["dve"], self.sm["act"], self.sm["pe"]
        m0 = b.mark()
        if mode == "L1":
            self.prep_params(chunks=[("gate_s", FF), ("up_s", FF)],
                             bcasts=[("down_s", D)])
        HT = 1024
        lnT = b.tile([128, KC * HT], BF16)
        hdnT = b.tile([128, FFC * HT], BF16)
        sgt = [b.tile([128, 512], F32) for _ in range(2)]
        xr = [b.tile([128, 512], F32) for _ in range(2)]
        yo = [b.tile([128, 512], F32) for _ in range(2)]
        pu = [b.pb[0], b.pb[1], b.pb[2], b.pb[3]]
        py = b.pb
        for half in range(2):
            toff = half * HT
            mh = b.mark()
            self.emit_ln(xin[toff:toff + HT, :], lnT, HT // 128)
            b.barrier()
            w1b = [b.tile([128, KC * 256], BF16) for _ in range(2)]
            w2b = [b.tile([128, KC * 256], BF16) for _ in range(2)]
            w3b = [b.tile([128, 6 * 512], BF16) for _ in range(2)]
            w_free = 0
            pu_free = [0, 0, 0, 0]
            sgt_free = [0, 0]
            fwu = [self.s("fwu0"), self.s("fwu1")]
            w_free2 = [0, 0]
            for hb2 in range(22):
                wp2 = hb2 % 2
                b.wait("sync", pe, w_free2[wp2])
                self.wload(w_up1, hb2 * 256, (hb2 + 1) * 256, w1b[wp2],
                           fwu[wp2])
                l2 = self.wload(w_up2, hb2 * 256, (hb2 + 1) * 256, w2b[wp2],
                                fwu[wp2])
                b.wait("tensor", fwu[wp2], l2)
                for hc in range(2):
                    hcg = hb2 * 2 + hc
                    for tb in range(2):
                        bk = (hc * 2 + tb) % 2
                        b.wait("tensor", act, pu_free[bk])
                        b.wait("tensor", dve, pu_free[2 + bk])
                        for wb_, po in ((w1b[wp2], 0), (w2b[wp2], 2)):
                            for k in range(KC):
                                b.op("tensor", lambda t, k=k, hc=hc, tb=tb,
                                     bk=bk, wb_=wb_, po=po: t.matmul(
                                         out=pu[po + bk][:, :],
                                         lhsT=wb_[:, k * 256 + hc * 128:
                                                  k * 256 + hc * 128 + 128],
                                         rhs=lnT[:, k * HT + tb * 512:
                                                 k * HT + tb * 512 + 512],
                                         start=(k == 0),
                                         stop=(k == KC - 1)))
                        pvv = b.inc_last("tensor", pe)
                        b.wait("scalar", pe, pvv)
                        b.wait("scalar", dve, sgt_free[bk])
                        if mode == "L0":
                            av = b.op("scalar", lambda s, bk=bk: s.activation(
                                out=sgt[bk][:, :], in_=pu[bk][:, :],
                                func=AF.Sigmoid, scale=0.5), sem=act)
                        else:
                            av = b.op("scalar", lambda s, bk=bk, hcg=hcg:
                                      s.activation(
                                          out=sgt[bk][:, :],
                                          in_=pu[bk][:, :], func=AF.Silu,
                                          scale=self.c["gate_s"]
                                          [:, hcg:hcg + 1]), sem=act)
                        pu_free[bk] = av
                        b.wait("vector", act, av)
                        if mode == "L0":
                            dv = b.op("vector", lambda v, bk=bk, hcg=hcg,
                                      tb=tb: v.tensor_tensor(
                                          out=hdnT[:, hcg * HT + tb * 512:
                                                   hcg * HT + tb * 512 + 512],
                                          in0=sgt[bk][:, :],
                                          in1=pu[2 + bk][:, :],
                                          op=ALU.mult), sem=dve)
                        else:
                            dv = b.op("vector", lambda v, bk=bk, hcg=hcg,
                                      tb=tb: v.scalar_tensor_tensor(
                                          out=hdnT[:, hcg * HT + tb * 512:
                                                   hcg * HT + tb * 512 + 512],
                                          in0=pu[2 + bk][:, :],
                                          scalar=self.c["up_s"]
                                          [:, hcg:hcg + 1],
                                          in1=sgt[bk][:, :],
                                          op0=ALU.mult, op1=ALU.mult),
                                      sem=dve)
                        pu_free[2 + bk] = dv
                        sgt_free[bk] = dv
                w_free2[wp2] = pe.v
            b.wait("tensor", act, act.v)
            b.wait("tensor", dve, dve.v)
            xr_cons = [0, 0]
            yo_st = [0, 0]
            py_all_free = 0
            fw3 = [self.s("fw30"), self.s("fw31")]
            fxs = [self.s("fx0"), self.s("fx1")]
            fys = [self.s("fy0"), self.s("fy1")]
            w3_cons = [0, 0]
            for dc in range(4):
                for rb in range(8):
                    wp = rb % 2
                    nj = 6 if rb < 7 else 2
                    b.wait("sync", pe, w3_cons[wp])
                    l3 = b.dma("sync",
                               w3b[wp].rearrange("p (b f) -> p b f", b=6),
                               self.wview_rows(w_dn, rb, dc * 512,
                                               (dc + 1) * 512), fw3[wp])
                    b.wait("tensor", fw3[wp], l3)
                    if rb == 0:
                        b.wait("tensor", dve, py_all_free)
                    for tc in range(8):
                        for hc in range(nj):
                            hcg = rb * 6 + hc
                            b.op("tensor", lambda t, hcg=hcg, hc=hc, tc=tc,
                                 wp=wp: t.matmul(
                                     out=py[tc][:, :],
                                     lhsT=hdnT[:, hcg * HT + tc * 128:
                                               hcg * HT + tc * 128 + 128],
                                     rhs=w3b[wp][:, hc * 512:(hc + 1) * 512],
                                     start=(hcg == 0),
                                     stop=(hcg == FFC - 1)))
                    w3_cons[wp] = b.inc_last("tensor", pe)
                pvv = pe.v
                for tc in range(8):
                    bk = tc % 2
                    trow = toff + tc * 128
                    b.wait("gpsimd", dve, xr_cons[bk])
                    xld = b.dma("gpsimd", xr[bk][:, :],
                                xin[trow:trow + 128,
                                    dc * 512:(dc + 1) * 512], fxs[bk])
                    b.wait("vector", pe, pvv)
                    b.wait("vector", fxs[bk], xld)
                    b.wait("vector", fys[bk], yo_st[bk])
                    if mode == "L0":
                        dv = b.op("vector", lambda v, bk=bk, tc=tc:
                                  v.scalar_tensor_tensor(
                                      out=yo[bk][:, :], in0=py[tc][:, :],
                                      scalar=0.25, in1=xr[bk][:, :],
                                      op0=ALU.mult, op1=ALU.add), sem=dve)
                    else:
                        b.op("vector", lambda v, bk=bk, tc=tc, dc=dc:
                             v.tensor_tensor(
                                 out=yo[bk][:, :], in0=py[tc][:, :],
                                 in1=self.c["down_s"][:, dc * 512:
                                                      (dc + 1) * 512],
                                 op=ALU.mult))
                        dv = b.op("vector", lambda v, bk=bk:
                                  v.tensor_tensor(
                                      out=yo[bk][:, :], in0=yo[bk][:, :],
                                      in1=xr[bk][:, :], op=ALU.add), sem=dve)
                    xr_cons[bk] = dv
                    b.wait("gpsimd", dve, dv)
                    yo_st[bk] = b.dma("gpsimd",
                                      xout[trow:trow + 128,
                                           dc * 512:(dc + 1) * 512],
                                      yo[bk][:, :], fys[bk])
                py_all_free = dve.v
            b.wait("tensor", dve, dve.v)
            self.wait_dma_all("gpsimd")
            b.barrier()
            b.reset(mh)
        b.barrier()
        b.reset(m0)

    def emit_C(self, xin, xout):
        b, nc = self.b, self.nc
        ld = self.sm["ld"]
        ldg, stg = self.sm["ldg"], self.sm["stg"]
        dve, act, pe = self.sm["dve"], self.sm["act"], self.sm["pe"]
        cc = self.sm["cc"]
        Xbuf = nc.dram_tensor("c_X", [TOK, D], F32, kind="Internal").ap()
        Kbuf = nc.dram_tensor("c_K", [TOK, D], F32, kind="Internal").ap()
        Vbuf = nc.dram_tensor("c_V", [TOK, D], F32, kind="Internal").ap()
        ag_in = nc.dram_tensor("c_agin", [1, D], F32, kind="Internal").ap()
        ag_out = nc.dram_tensor("c_agout", [8, D], F32, kind="Internal").ap()
        asc = nc.dram_tensor("c_asc", [1, D], F32, kind="Internal").ap()
        mC = b.mark()
        QT = b.tile([128, KC * TOK], BF16)
        Achunks = b.tile([128, 16], F32)
        self.prep_params(chunks=[("g1", D), ("be1", D), ("b_in", 6144)],
                         bf16rows=[("b_in_bf", 6144), ("b_out1_bf", 2048)])
        m1 = b.mark()
        XnT = b.tile([128, KC * TOK], BF16)
        m2 = b.mark()
        self.emit_ln(xin, XnT, TOK // 128, keep_f32=Xbuf, inner=True,
                     gamma_beta=(self.c["g1"], self.c["be1"]))
        b.barrier()
        winb = b.tile([128, KC * 512], BF16)
        kv = [b.tile([128, 512], F32) for _ in range(2)]
        cst = [b.tile([128, 1024], F32) for _ in range(2)]
        rh = b.tile([128, 512], F32)
        tmp1 = b.tile([128, 512], F32)
        pk = [b.pb[0], b.pb[1]]
        pA = [b.pb[2], b.pb[3], b.pb[4], b.pb[5]]
        for which in range(2):
            coloff = D * (1 + which)
            dst = Kbuf if which == 0 else Vbuf
            w_cons = 0
            cwi = self.s("cwi")
            ccs = [self.s("ccs0"), self.s("ccs1")]
            cks = [self.s("cks0"), self.s("cks1")]
            for kcb in range(4):
                b.wait("sync", pe, w_cons)
                lw = self.wload("W_in_bf", coloff + kcb * 512,
                                coloff + kcb * 512 + 512, winb, cwi)
                b.wait("tensor", cwi, lw)
                pk_free = [0, 0]
                cs_ld = [0, 0]
                cs_cons = [0, 0]
                kv_st = [0, 0]
                for tc in range(16):
                    bk = tc % 2
                    if kcb < 2:
                        b.wait("gpsimd", dve, cs_cons[bk])
                        b.dma("gpsimd", cst[bk][:, 0:512],
                              self.io["cosT"][tc * 128:(tc + 1) * 128,
                                              kcb * 512:(kcb + 1) * 512],
                              ccs[bk])
                        cs_ld[bk] = b.dma(
                            "gpsimd", cst[bk][:, 512:1024],
                            self.io["sinT"][tc * 128:(tc + 1) * 128,
                                            kcb * 512:(kcb + 1) * 512],
                            ccs[bk])
                    b.wait("tensor", dve, pk_free[bk])
                    for k in range(KC):
                        b.op("tensor", lambda t, k=k, tc=tc, kcb=kcb, bk=bk:
                             t.matmul(
                                 out=pk[bk][:, :],
                                 lhsT=XnT[:, k * TOK + tc * 128:
                                          k * TOK + tc * 128 + 128],
                                 rhs=winb[:, k * 512:(k + 1) * 512],
                                 start=(k == 0), stop=False))
                    b.op("tensor", lambda t, kcb=kcb, coloff=coloff, bk=bk:
                         t.matmul(
                             out=pk[bk][:, :], lhsT=self.c["ones_b"][:, :],
                             rhs=self.c["b_in_bf"][0:1, coloff + kcb * 512:
                                                   coloff + kcb * 512 + 512],
                             start=False, stop=True))
                    pvv = b.inc_last("tensor", pe)
                    b.wait("vector", pe, pvv)
                    b.wait("vector", cks[bk], kv_st[bk])
                    if kcb < 2:
                        b.wait("vector", ccs[bk], cs_ld[bk])
                        b.op("vector", lambda v, bk=bk: v.tensor_scalar_mul(
                            out=rh.rearrange("p (c two) -> p c two", two=2)
                            [:, :, 0:1],
                            in0=pk[bk][:, 0:512].rearrange(
                                "p (c two) -> p c two", two=2)[:, :, 1:2],
                            scalar1=-1.0))
                        b.op("vector", lambda v, bk=bk: v.tensor_copy(
                            out=rh.rearrange("p (c two) -> p c two", two=2)
                            [:, :, 1:2],
                            in_=pk[bk][:, 0:512].rearrange(
                                "p (c two) -> p c two", two=2)[:, :, 0:1]))
                        b.op("vector", lambda v, bk=bk: v.tensor_tensor(
                            out=tmp1[:, :], in0=pk[bk][:, :],
                            in1=cst[bk][:, 0:512], op=ALU.mult))
                        b.op("vector", lambda v, bk=bk: v.tensor_tensor(
                            out=rh[:, :], in0=rh[:, :],
                            in1=cst[bk][:, 512:1024], op=ALU.mult))
                        dv = b.op("vector", lambda v, bk=bk: v.tensor_tensor(
                            out=kv[bk][:, :], in0=tmp1[:, :], in1=rh[:, :],
                            op=ALU.add), sem=dve)
                        cs_cons[bk] = dv
                    else:
                        dv = b.op("vector", lambda v, bk=bk: v.tensor_copy(
                            out=kv[bk][:, :], in_=pk[bk][:, :]), sem=dve)
                    pk_free[bk] = dv
                    b.wait("gpsimd", dve, dv)
                    kv_st[bk] = b.dma("gpsimd",
                                      dst[tc * 128:(tc + 1) * 128,
                                          kcb * 512:(kcb + 1) * 512],
                                      kv[bk][:, :], cks[bk])
                w_cons = pe.v
        self.wait_dma_all("gpsimd")
        b.barrier()
        b.reset(m2)
        Kf = b.tile([128, D], F32)
        Vf = b.tile([128, D], F32)
        ssq = b.tile([128, 2], F32)
        rn = b.tile([128, 2], F32)
        Kc2 = b.tile([128, D], BF16)
        Pc = b.tile([128, D], BF16)
        dump = Pc
        kv_cons = [0, 0]
        pe_P = 0
        ckv = self.s("ckv")
        for tc in range(16):
            b.wait("gpsimd", act, kv_cons[0])
            b.wait("gpsimd", dve, kv_cons[1])
            b.dma("gpsimd", Kf[:, :], Kbuf[tc * 128:(tc + 1) * 128, :], ckv)
            kld = b.dma("gpsimd", Vf[:, :],
                        Vbuf[tc * 128:(tc + 1) * 128, :], ckv)
            b.wait("scalar", ckv, kld)
            b.wait("scalar", pe, pe_P)
            b.op("scalar", lambda s: s.activation(
                out=dump[:, :], in_=Kf[:, :], func=AF.Square,
                accum_out=ssq[:, 0:1]))
            sv = b.op("scalar", lambda s: s.activation(
                out=dump[:, :], in_=Vf[:, :], func=AF.Square,
                accum_out=ssq[:, 1:2]), sem=act)
            b.wait("scalar", act, sv)
            b.op("scalar", lambda s: s.activation(
                out=rn[:, :], in_=ssq[:, :], func=AF.Ln))
            av = b.op("scalar", lambda s: s.activation(
                out=rn[:, :], in_=rn[:, :], func=AF.Exp, scale=-0.5), sem=act)
            kv_cons[0] = av
            b.wait("vector", act, av)
            b.wait("vector", pe, pe_P)
            b.op("vector", lambda v: v.tensor_scalar_mul(
                out=Kc2[:, :], in0=Kf[:, :], scalar1=rn[:, 0:1]))
            b.op("vector", lambda v: v.tensor_scalar_mul(
                out=Pc[:, :], in0=Vf[:, :], scalar1=rn[:, 1:2]))
            dv = b.op("vector", lambda v: v.tensor_tensor(
                out=Pc[:, :], in0=Pc[:, :], in1=Kc2[:, :], op=ALU.mult),
                sem=dve)
            kv_cons[1] = dv
            b.wait("tensor", dve, dv)
            for j in range(4):
                b.op("tensor", lambda t, j=j, tc=tc: t.matmul(
                    out=pA[j][0:1, :], lhsT=self.c["ones_col_b"][:, 0:1],
                    rhs=Pc[:, j * 512:(j + 1) * 512],
                    start=(tc == 0), stop=(tc == 15)))
            pe_P = b.inc_last("tensor", pe)
        Asb = b.tile([1, D], F32)
        agsb = b.tile([8, D], F32)
        shtmp = b.tile([4, D], F32)
        Astg = b.tile([128, 128], F32)
        masks = b.tile([8, 1], F32)
        lm = b.dma("sync", masks[0:8, 0:1], self.io["pairmask"][:, :], ld)
        b.wait("vector", pe, pe.v)
        for j in range(4):
            b.op("vector", lambda v, j=j: v.tensor_copy(
                out=Asb[0:1, j * 512:(j + 1) * 512], in_=pA[j][0:1, :]),
                sem=dve)
        b.wait("gpsimd", dve, dve.v)
        b.dma("gpsimd", ag_in[:, :], Asb[0:1, :], stg)
        b.wait("gpsimd", stg, stg.v)
        b.coll("AllGather", ALU.bypass, G8, [ag_in.opt()], [ag_out.opt()], cc)
        b.wait("gpsimd", cc, cc.v)
        gl = b.dma("gpsimd", agsb[0:8, :], ag_out[:, :], ldg)
        b.wait("vector", ldg, gl)
        b.wait("vector", ld, lm)
        b.op("vector", lambda v: v.tensor_scalar_mul(
            out=agsb[0:8, :], in0=agsb[0:8, :], scalar1=masks[0:8, 0:1]),
            sem=dve)
        for half in (4, 2, 1):
            b.wait("gpsimd", dve, dve.v)
            sh = b.dma("gpsimd", shtmp[0:half, :],
                       agsb[half:2 * half, :], ldg)
            b.wait("vector", ldg, sh)
            b.op("vector", lambda v, half=half: v.tensor_tensor(
                out=agsb[0:half, :], in0=agsb[0:half, :],
                in1=shtmp[0:half, :], op=ALU.add), sem=dve)
        b.op("vector", lambda v: v.tensor_scalar(
            out=Asb[0:1, :], in0=agsb[0:1, :],
            scalar1=0.5, scalar2=0.5, op0=ALU.mult, op1=ALU.add), sem=dve)
        b.wait("gpsimd", dve, dve.v)
        b.dma("gpsimd", asc[:, :], Asb[0:1, :], stg)
        b.wait("gpsimd", stg, stg.v)
        b.op("vector", lambda v: v.memset(Astg[0:128, 0:128], 0.0), sem=dve)
        b.wait("gpsimd", dve, dve.v)
        al = b.dma("gpsimd", Astg[0:16, 0:128],
                   asc.rearrange("one (c j) -> (one c) j", j=128), ldg)
        b.wait("tensor", ldg, al)
        b.wait("tensor", dve, dve.v)
        b.op("tensor", lambda t: t.transpose(
            out=pk[0][:, 0:128], in_=Astg[0:128, 0:128],
            identity=self.c["ident_f"][:, :]), sem=pe)
        b.wait("vector", pe, pe.v)
        b.op("vector", lambda v: v.tensor_copy(
            out=Achunks[:, :], in_=pk[0][:, 0:16]), sem=dve)
        b.wait("tensor", dve, dve.v)
        b.barrier()
        b.reset(m2)
        winb = b.tile([128, KC * 512], BF16)
        cwi = self.s("cwi")
        wq_cons = 0
        pq_free = [0, 0]
        for dcq in range(4):
            b.wait("sync", pe, wq_cons)
            lw = self.wload("W_in_bf", dcq * 512, (dcq + 1) * 512,
                            winb, cwi)
            b.wait("tensor", cwi, lw)
            for qcc in range(4):
                qc = dcq * 4 + qcc
                for tb in range(4):
                    bk = (qcc * 4 + tb) % 2
                    b.wait("tensor", dve, pq_free[bk])
                    for k in range(KC):
                        b.op("tensor", lambda t, k=k, qcc=qcc, tb=tb, bk=bk:
                             t.matmul(
                                 out=pk[bk][:, :],
                                 lhsT=winb[:, k * 512 + qcc * 128:
                                           k * 512 + qcc * 128 + 128],
                                 rhs=XnT[:, k * TOK + tb * 512:
                                         k * TOK + tb * 512 + 512],
                                 start=(k == 0), stop=(k == KC - 1)))
                    pvv = b.inc_last("tensor", pe)
                    b.wait("vector", pe, pvv)
                    dv = b.op("vector", lambda v, qc=qc, tb=tb, bk=bk:
                              v.tensor_scalar(
                                  out=QT[:, qc * TOK + tb * 512:
                                         qc * TOK + tb * 512 + 512],
                                  in0=pk[bk][:, :],
                                  scalar1=self.c["b_in"][:, qc:qc + 1],
                                  scalar2=Achunks[:, qc:qc + 1],
                                  op0=ALU.add, op1=ALU.mult), sem=dve)
                    pq_free[bk] = dv
            wq_cons = pe.v
        b.wait("tensor", dve, dve.v)
        self.wait_dma_all("gpsimd")
        b.barrier()
        b.reset(m1)
        wout = b.tile([128, KC * D], BF16)
        xr = [b.tile([128, 512], F32) for _ in range(2)]
        Xr = [b.tile([128, 512], F32) for _ in range(2)]
        yo = [b.tile([128, 512], F32) for _ in range(2)]
        pyy = [b.pb[0], b.pb[1]]
        cwo = self.s("cwo")
        cxs = [self.s("cx0"), self.s("cx1")]
        cys = [self.s("cy0"), self.s("cy1")]
        lw = self.wload("W_out1_bf", 0, D, wout, cwo)
        b.wait("tensor", cwo, lw)
        pyy_free = [0, 0]
        xr_cons = [0, 0]
        yo_st = [0, 0]
        for tc in range(16):
            for dc in range(4):
                yk = (tc * 4 + dc) % 2
                trow = tc * 128
                b.wait("tensor", dve, pyy_free[yk])
                for qc in range(KC):
                    b.op("tensor", lambda t, qc=qc, tc=tc, dc=dc, yk=yk:
                         t.matmul(
                             out=pyy[yk][:, :],
                             lhsT=QT[:, qc * TOK + tc * 128:
                                     qc * TOK + tc * 128 + 128],
                             rhs=wout[:, qc * D + dc * 512:
                                      qc * D + dc * 512 + 512],
                             start=(qc == 0), stop=False))
                b.op("tensor", lambda t, dc=dc, yk=yk: t.matmul(
                    out=pyy[yk][:, :], lhsT=self.c["ones_b"][:, :],
                    rhs=self.c["b_out1_bf"][0:1, dc * 512:(dc + 1) * 512],
                    start=False, stop=True))
                pvv = b.inc_last("tensor", pe)
                b.wait("gpsimd", dve, xr_cons[yk])
                b.dma("gpsimd", xr[yk][:, :],
                      xin[trow:trow + 128, dc * 512:(dc + 1) * 512],
                      cxs[yk])
                xld = b.dma("gpsimd", Xr[yk][:, :],
                            Xbuf[trow:trow + 128, dc * 512:(dc + 1) * 512],
                            cxs[yk])
                b.wait("vector", pe, pvv)
                b.wait("vector", cxs[yk], xld)
                b.wait("vector", cys[yk], yo_st[yk])
                b.op("vector", lambda v, yk=yk: v.tensor_tensor(
                    out=yo[yk][:, :], in0=pyy[yk][:, :], in1=Xr[yk][:, :],
                    op=ALU.add))
                dv = b.op("vector", lambda v, yk=yk: v.tensor_tensor(
                    out=yo[yk][:, :], in0=yo[yk][:, :], in1=xr[yk][:, :],
                    op=ALU.add), sem=dve)
                pyy_free[yk] = dv
                xr_cons[yk] = dv
                b.wait("gpsimd", dve, dv)
                yo_st[yk] = b.dma("gpsimd",
                                  xout[trow:trow + 128,
                                       dc * 512:(dc + 1) * 512],
                                  yo[yk][:, :], cys[yk])
        self.wait_dma_all("gpsimd")
        b.barrier()
        b.reset(mC)



QNUMEL = {"w1qT": FF * D, "w2qT": FF * D, "w3qT": FF * D}

WSPEC = {
    "A": dict(weights=[("W_hidden_sl", "W_hidden_bf", D, 8192),
                       ("W_qk_sl", "W_qk_bf", D, 128),
                       ("W_out0_sl", "W_out0_bf", 4096, D)], quant=[]),
    "B": dict(weights=[], quant=[("w1T_sl", "w1qT", D, FF),
                                 ("w2T_sl", "w2qT", D, FF),
                                 ("w3T_sl", "w3qT", 6144, D)]),
    "C": dict(weights=[("W_in_sl", "W_in_bf", D, 3 * D),
                       ("W_out1_sl", "W_out1_bf", D, D)], quant=[]),
    "D": dict(weights=[("gwT_sl", "gwT_bf", D, FF),
                       ("uwT_sl", "uwT_bf", D, FF),
                       ("dwT_sl", "dwT_bf", 6144, D)], quant=[]),
}

CONST_NAMES = ("ident_f", "ident_b", "ones_f", "ones_b")

PHASE_INPUTS = {
    "A": ["x_all", "W_hidden_sl", "W_qk_sl", "W_out0_sl",
          "b_qk", "gq", "bq", "gk", "bk", "b_hidden", "b_hidden_bf",
          "b_out0_bf"],
    "B": ["xs", "w1T_sl", "w2T_sl", "w3T_sl"],
    "C": ["xs", "W_in_sl", "W_out1_sl", "b_in", "b_in_bf", "b_out1_bf",
          "g1", "be1", "cosT", "sinT", "pairmask"],
    "D": ["xs", "gwT_sl", "uwT_sl", "dwT_sl", "gate_s", "up_s", "down_s"],
}

IN_SHAPES = {
    "x_all": ([TOKA, D], F32), "xs": ([TOK, D], F32),
    "W_hidden_sl": ([D // 8, 8192], F32), "W_qk_sl": ([D // 8, 128], F32),
    "W_out0_sl": ([4096 // 8, D], F32),
    "w1T_sl": ([D // 8, FF], F32), "w2T_sl": ([D // 8, FF], F32),
    "w3T_sl": ([768, D], F32),
    "W_in_sl": ([D // 8, 3 * D], F32), "W_out1_sl": ([D // 8, D], F32),
    "gwT_sl": ([D // 8, FF], F32), "uwT_sl": ([D // 8, FF], F32),
    "dwT_sl": ([768, D], F32),
    "b_qk": ([1, 128], F32), "gq": ([1, 128], F32), "bq": ([1, 128], F32),
    "gk": ([1, 128], F32), "bk": ([1, 128], F32),
    "b_hidden": ([1, 8192], F32), "b_hidden_bf": ([1, 8192], F32),
    "b_out0_bf": ([1, D], F32),
    "b_in": ([1, 3 * D], F32), "b_in_bf": ([1, 3 * D], F32),
    "b_out1_bf": ([1, D], F32),
    "g1": ([1, D], F32), "be1": ([1, D], F32),
    "gate_s": ([1, FF], F32), "up_s": ([1, FF], F32),
    "down_s": ([1, D], F32),
    "cosT": ([TOK, 1024], F32), "sinT": ([TOK, 1024], F32),
    "pairmask": ([8, 1], F32),
}

WDST_SHAPES = {
    "W_hidden_bf": [D, 8192], "W_qk_bf": [D, 128], "W_out0_bf": [4096, D],
    "w1qT": [D, FF], "w2qT": [D, FF], "w3qT": [6144, D],
    "W_in_bf": [D, 3 * D], "W_out1_bf": [D, D],
    "gwT_bf": [D, FF], "uwT_bf": [D, FF], "dwT_bf": [6144, D],
}


def build_program(phases, fused):
    nc = bass.Bass("TRN2", target_bir_lowering=False, debug=False)
    es = ExitStack()
    io = {}
    for nm in CONST_NAMES:
        dt = F32 if nm.endswith("_f") else BF16
        shape = [128, 128] if nm.startswith("ident") else [1, 128]
        io[nm] = nc.dram_tensor(nm, shape, dt, kind="ExternalInput").ap()
    names = set()
    for ph in phases:
        names.update(PHASE_INPUTS[ph])
    if fused and phases[0] == "A":
        names.discard("xs")
    for nm in sorted(names):
        shape, dt = IN_SHAPES[nm]
        io[nm] = nc.dram_tensor(nm, shape, dt, kind="ExternalInput").ap()

    if fused:
        out = nc.dram_tensor("xo", [TOK, D], F32, kind="ExternalOutput").ap()
        chain = {}
        prev = io.get("xs")
        for i, ph in enumerate(phases):
            if i == len(phases) - 1:
                nxt = out
            else:
                nxt = nc.dram_tensor(f"xbuf{i}", [TOK, D], F32,
                                     kind="Internal").ap()
            chain[ph] = (prev, nxt)
            prev = nxt
    else:
        out = nc.dram_tensor("xo", [TOK, D], F32, kind="ExternalOutput").ap()
        chain = {ph: (io.get("xs"), out) for ph in phases}
    b = B(nc, es)
    mk = MK(nc, b, io)
    mk.emit_startup()
    for ph in phases:
        spec = WSPEC[ph]
        mk.emit_W(spec["weights"], spec["quant"])
        xin, xout = chain[ph]
        if ph == "A":
            mk.emit_A(io["x_all"], xout)
        elif ph == "B":
            mk.emit_ffn(xin, xout, "w1qT", "w2qT", "w3qT", "L0")
        elif ph == "C":
            mk.emit_C(xin, xout)
        elif ph == "D":
            mk.emit_ffn(xin, xout, "gwT_bf", "uwT_bf", "dwT_bf", "L1")
    mk.wait_dma_all("gpsimd")
    b.finalize()
    return nc, es


_PROG_CACHE = {}


def _consts():
    import ml_dtypes
    return {
        "ident_f": np.eye(128, dtype=np.float32),
        "ident_b": np.eye(128, dtype=ml_dtypes.bfloat16),
        "ones_f": np.ones((1, 128), np.float32),
        "ones_b": np.ones((1, 128), ml_dtypes.bfloat16),
    }


def _rotary_tables(pos0, n, rot_dim=1024, theta=10000.0):
    inv = 1.0 / theta ** (np.arange(0, rot_dim, 2, dtype=np.float64) / rot_dim)
    f = (np.arange(pos0, pos0 + n, dtype=np.float64)[:, None] * inv[None, :])
    f = np.repeat(f, 2, axis=1)
    return (np.cos(f).astype(np.float32), np.sin(f).astype(np.float32))


def _prep_inputs(inputs):
    f = lambda a: np.ascontiguousarray(np.asarray(a), dtype=np.float32)
    x = f(inputs["L0_attn_x"])
    full = {
        "W_hidden": f(inputs["L0_attn_W_hidden"]),
        "W_qk": f(inputs["L0_attn_W_qk"]),
        "W_out0": f(inputs["L0_attn_W_out"]),
        "w1T": f(np.asarray(inputs["L0_ffn_w1"]).T),
        "w2T": f(np.asarray(inputs["L0_ffn_w2"]).T),
        "w3T": np.concatenate([f(np.asarray(inputs["L0_ffn_w3"]).T),
                               np.zeros((512, D), np.float32)], 0),
        "W_in": f(inputs["L1_attn_W_in"]),
        "W_out1": f(inputs["L1_attn_W_out"]),
        "gwT": f(np.asarray(inputs["L1_ffn_gate_w"]).T),
        "uwT": f(np.asarray(inputs["L1_ffn_up_w"]).T),
        "dwT": np.concatenate([f(np.asarray(inputs["L1_ffn_down_w"]).T),
                               np.zeros((512, D), np.float32)], 0),
    }
    row = lambda a: f(a).reshape(1, -1)
    vecs = {
        "b_qk": row(inputs["L0_attn_b_qk"]),
        "gq": row(inputs["L0_attn_gamma_q"]),
        "bq": row(inputs["L0_attn_beta_q"]),
        "gk": row(inputs["L0_attn_gamma_k"]),
        "bk": row(inputs["L0_attn_beta_k"]),
        "b_hidden": row(inputs["L0_attn_b_hidden"]),
        "b_hidden_bf": row(inputs["L0_attn_b_hidden"]),
        "b_out0_bf": row(inputs["L0_attn_b_out"]),
        "b_in": row(inputs["L1_attn_b_in"]),
        "b_in_bf": row(inputs["L1_attn_b_in"]),
        "b_out1_bf": row(inputs["L1_attn_b_out"]),
        "g1": row(inputs["L1_attn_gamma"]),
        "be1": row(inputs["L1_attn_beta"]),
        "gate_s": row(inputs["L1_ffn_gate_s"]),
        "up_s": row(inputs["L1_ffn_up_s"]),
        "down_s": row(inputs["L1_ffn_down_s"]),
    }
    consts = _consts()
    cores = []
    for c in range(8):
        bidx, h = c // 2, c % 2
        own = x[bidx, h * TOK:(h + 1) * TOK, :]
        other = x[bidx, (1 - h) * TOK:(2 - h) * TOK, :]
        cosT, sinT = _rotary_tables(h * TOK, TOK)
        m = dict(consts)
        m.update(vecs)
        m["x_all"] = np.ascontiguousarray(np.concatenate([own, other], 0))
        m["cosT"] = cosT
        m["sinT"] = sinT
        pm = np.zeros((8, 1), np.float32)
        pm[2 * bidx, 0] = 1.0
        pm[2 * bidx + 1, 0] = 1.0
        m["pairmask"] = pm
        for nm, w in full.items():
            r = w.shape[0] // 8
            m[nm + "_sl"] = np.ascontiguousarray(w[c * r:(c + 1) * r, :])
        cores.append(m)
    return cores


FUSED = True


def kernel(**inputs):
    cores = _prep_inputs(inputs)
    if FUSED:
        if "F" not in _PROG_CACHE:
            _PROG_CACHE["F"] = build_program("ABCD", fused=True)
        nc, _ = _PROG_CACHE["F"]
        res = run_bass_kernel_spmd(nc, cores, core_ids=list(range(8)))
        outs = [res.results[c]["xo"] for c in range(8)]
    else:
        cur = [m["x_all"][0:TOK, :] for m in cores]
        for ph in "ABCD":
            if ph not in _PROG_CACHE:
                _PROG_CACHE[ph] = build_program(ph, fused=False)
            nc, _ = _PROG_CACHE[ph]
            in_maps = []
            names = set(PHASE_INPUTS[ph]) | set(CONST_NAMES)
            for ci, m in enumerate(cores):
                mm = {k: m[k] for k in names if k in m}
                if "xs" in names:
                    mm["xs"] = np.ascontiguousarray(cur[ci])
                in_maps.append(mm)
            res = run_bass_kernel_spmd(nc, in_maps, core_ids=list(range(8)))
            cur = [res.results[c]["xo"] for c in range(8)]
        outs = cur
    y = np.empty((4, TOKA, D), np.float32)
    for c in range(8):
        bidx, h = c // 2, c % 2
        y[bidx, h * TOK:(h + 1) * TOK, :] = outs[c]
    return y
